# version 10
# baseline (speedup 1.0000x reference)
import os
os.environ.setdefault("BASS_INTERP_NUM_WORKERS", "1")
import contextlib
import numpy as np

import concourse.bass as bass
import concourse.mybir as mybir
from concourse.bass_utils import run_bass_kernel_spmd

F32 = mybir.dt.float32
F32R = mybir.dt.float32r
BF16 = mybir.dt.bfloat16
AF = mybir.ActivationFunctionType
OP = mybir.AluOpType

B = 8; T = 2048; D = 1024; DI = 2048; H = 32; DH = 64
NG = 8; L = 256; NCH = T // L; BL = 16; NBL = L // BL
NIC = DI // 128
ROWS_F = 256
DT_MIN, DT_MAX, K_MAX, EPS = 1e-4, 0.1, 0.5, 1e-8
LN_EPS = 1e-5
EPAD = 8; EW = EPAD + NBL


def _host_arrays(inputs, core):
    W_in = inputs["L0_attn_W_in"]
    perm = np.arange(4160)
    perm[2 * DI:2 * DI + H] = 2 * DI + 2 * np.arange(H)
    perm[2 * DI + H:2 * DI + 2 * H] = 2 * DI + 2 * np.arange(H) + 1
    W_inT = np.ascontiguousarray(W_in[perm].T)

    W_bc = inputs["L0_attn_W_bc"]
    o = np.arange(4096)
    hh, qq, nn = o // 128, (o % 128) // 32, o % 32
    pos = (qq * NG + hh // 4) * 128 + (hh % 4) * 32 + nn
    new = np.zeros(4096, dtype=np.int64); new[pos] = o
    W_bcT = np.ascontiguousarray(W_bc[new].T)

    g = {
        "x_ssm": np.ascontiguousarray(inputs["L0_attn_x"][core]),
        "x_ffn": np.ascontiguousarray(
            inputs["L0_attn_x"][:, core * ROWS_F:(core + 1) * ROWS_F, :].reshape(T, D)),
        "W_inT": W_inT,
        "W_bcT": W_bcT,
        "W_outT": np.ascontiguousarray(inputs["L0_attn_W_out"].T),
        "dwW": np.ascontiguousarray(
            inputs["L0_attn_dw_weight"].reshape(NIC, 128, 4).transpose(1, 0, 2).reshape(128, NIC * 4)),
        "dwB": np.ascontiguousarray(inputs["L0_attn_dw_bias"].reshape(NIC, 128).T),
        "skipc": np.ascontiguousarray(inputs["L0_attn_skip"].reshape(NIC, 128).T),
        "A_log": np.ascontiguousarray(inputs["L0_attn_A_log"].reshape(NG, 128).T),
        "dt_bias": np.ascontiguousarray(inputs["L0_attn_dt_bias"].reshape(H, 1)),
        "bo": inputs["L0_ffn_bo"].reshape(1, D),
        "in_c": np.ascontiguousarray(inputs["L0_ffn_in_c"].reshape(8, 128).T),
    }
    for nm in ("wf", "wi", "wc", "uf", "ui", "uc", "wo", "uo"):
        g[nm + "T"] = np.ascontiguousarray(inputs["L0_ffn_" + nm].T)
    for nm in ("bf", "bi", "bc"):
        g["b" + nm[1]] = np.ascontiguousarray(inputs["L0_ffn_" + nm].reshape(8, 128).T)
    g["ident"] = np.eye(128, dtype=np.float32)
    g["trilm"] = np.triu(np.ones((128, 128), dtype=np.float32))
    sel4 = np.zeros((128, 4), dtype=np.float32)
    for hs in range(4):
        sel4[hs * 32:(hs + 1) * 32, hs] = 1.0
    g["sel4"] = sel4
    g["sel4T"] = np.ascontiguousarray(sel4.T)
    g["ones1"] = np.ones((1, 128), dtype=np.float32)
    g["onescol"] = np.ones((128, 1), dtype=np.float32)
    return g


class _Q:
    def __init__(self, step):
        self.ops = []; self.count = 0; self.step = step; self.sem = None


class KB:
    def __init__(self):
        self.q = {n: _Q(16 if n in ("sync", "gpd") else 1)
                  for n in ("sync", "gpd", "pe", "dve", "act", "gp")}

    def on(self, qn, fn):
        eq = self.q[qn]
        if eq.step == 16 and eq.count > 0:
            eq.ops.append(lambda e, s=eq, v=eq.count: e.wait_ge(s.sem, v))
        eq.ops.append(lambda e, f=fn, eq=eq: f(e).then_inc(eq.sem, eq.step))
        eq.count += eq.step
        return eq.count

    def drain(self, qn):
        self.q[qn].ops.append(lambda e: e.drain())

    def wait(self, qn, prod, val):
        if val > 0:
            self.q[qn].ops.append(lambda e, s=self.q[prod], v=val: e.wait_ge(s.sem, v))

    def wait_sem(self, qn, sem, val):
        self.q[qn].ops.append(lambda e, s=sem, v=val: e.wait_ge(s, v))


def build_program():
    PH = int(os.environ.get("KPHASE", "9"))
    nc = bass.Bass("TRN2", target_bir_lowering=False)

    def din(name, shape, dtype=F32):
        return nc.dram_tensor(name, list(shape), dtype, kind="ExternalInput").ap()

    x_ssm = din("x_ssm", (T, D)); x_ffn = din("x_ffn", (T, D))
    W_inT = din("W_inT", (D, 4160)); W_bcT = din("W_bcT", (DI, 4096))
    W_outT = din("W_outT", (DI, D))
    wT = {nm: din(nm + "T", (D, D)) for nm in ("wf", "wi", "wc", "uf", "ui", "uc", "wo", "uo")}
    dwW = din("dwW", (128, NIC * 4)); dwB = din("dwB", (128, NIC))
    skipc = din("skipc", (128, NIC))
    A_log = din("A_log", (128, NG)); dt_bias = din("dt_bias", (H, 1))
    b_in = {nm: din("b" + nm, (128, 8)) for nm in ("f", "i", "c")}
    bo = din("bo", (1, D)); in_c = din("in_c", (128, 8))
    ident_in = din("ident", (128, 128)); tril_in = din("trilm", (128, 128))
    sel4_in = din("sel4", (128, 4)); sel4T_in = din("sel4T", (4, 128))
    ones1_in = din("ones1", (1, 128))
    onescol_in = din("onescol", (128, 1))

    out = nc.dram_tensor("out", [T, D], F32, kind="ExternalOutput").ap()
    cc_in = nc.dram_tensor("cc_in", [ROWS_F, D], F32).ap()
    cc_out = nc.dram_tensor("cc_out", [T, D], F32).ap()
    d_xn = nc.dram_tensor("d_xn", [128, 8 * T], BF16).ap()
    d_xnf = nc.dram_tensor("d_xnf", [128, 8 * T], BF16).ap()
    d_gate = nc.dram_tensor("d_gate", [128, NIC * T], F32).ap()
    d_vr = nc.dram_tensor("d_vr", [128, NIC * T], F32).ap()
    d_val = nc.dram_tensor("d_val", [128, NIC * T], F32).ap()
    d_par = nc.dram_tensor("d_par", [64, T], F32).ap()
    d_bc = nc.dram_tensor("d_bc", [4, 128, NG * T], F32).ap()
    d_yg = nc.dram_tensor("d_yg", [128, NIC * T], BF16).ap()
    xw_d = nc.dram_tensor("xw_d", [3, 128, 8 * T], BF16).ap()

    es = contextlib.ExitStack()
    with es:
        def sb(name, p, fr, dtype=F32):
            return es.enter_context(nc.sbuf_tensor(name, [p, fr], dtype))

        identf = sb("identf", 128, 128)
        identr = sb("identr", 128, 128, F32R)
        identb = sb("identb", 128, 128, BF16)
        trils = sb("trils", 128, 128)
        sel4s = sb("sel4s", 128, 4); sel4Ts = sb("sel4Ts", 4, 128)
        ones1s = sb("ones1s", 1, 128)
        onescol = sb("onescol_s", 128, 1)
        zc = sb("zc", 128, 1)
        dwWs = sb("dwWs", 128, NIC * 4); dwBs = sb("dwBs", 128, NIC)
        skips = sb("skips", 128, NIC)
        A_s = sb("A_s", 128, NG); Aneg = sb("Aneg", 128, NG)
        dtb_s = sb("dtb_s", H, 1)
        bias_s = {nm: sb("bias" + nm, 128, 8) for nm in ("f", "i", "c")}
        bo_r = sb("bo_r", 1, D); bo_b = sb("bo_b", 128, D)
        inc_s = sb("inc_s", 128, 8)
        dtsl = sb("dtsl", 32, L); kdsl = sb("kdsl", 32, L)
        dt2sl = sb("dt2sl", 32, L); dk1sl = sb("dk1sl", 32, L)
        parsl = sb("parsl", 64, L)
        rbsl = sb("rbsl", 128, NG * NBL)
        PL4 = sb("PL4", 32, 4 * H)
        PLt = {c2: sb("PLt" + c2, 128, NG) for c2 in ("11", "12", "21", "22")}
        E1 = {c2: sb("E1" + c2, 128, NG * EW) for c2 in ("11", "12", "21", "22")}
        E2 = {c2: sb("E2" + c2, 128, NG * EW) for c2 in ("11", "12", "21", "22")}
        BE = {c2: sb("BE" + c2, 128, NG * NBL) for c2 in ("11", "12", "21", "22")}
        idE = sb("idE", 128, NG * NBL)
        xln = sb("xln", 128, D)
        xln2 = sb("xln2", 128, D)
        xnbf = sb("xnbf", 128, D, BF16)
        ygbf = sb("ygbf", 128, L, BF16)
        stat = sb("stat", 128, 4)
        sqt = sb("sqt", 128, 512)
        fss = sb("fss", 4, 512); fsb = sb("fsb", 128, 512)
        Gsb = sb("Gsb", 128, L)
        KTt = sb("KTt", 128, DH)
        ysb = sb("ysb", 128, L)
        winB = sb("winB", 128, 8 * 128, BF16)
        wrowB = sb("wrowB", 128, D, BF16)
        wrow2 = sb("wrow2", 128, D, BF16)
        xwr = sb("xwr", 128, 8 * ROWS_F, BF16)
        vring = sb("vring", 128, L)
        gring = sb("gring", 128, L)

        AW = 36864
        BIG_OFF = 81856
        bigh = nc.alloc_sbuf_tensor_at("big", [128, AW], F32, offset=BIG_OFF)
        big = bigh.ap()

        def BG(off, w):
            assert off + w <= AW, (off, w)
            return big[:, off:off + w]

        def BGb(off, wbf):
            assert off + wbf // 2 <= AW
            return big[:, off:off + wbf // 2].bitcast(BF16)

        def BGr(off, w):
            return big[:, off:off + w].bitcast(F32R)

        SC_bz, SC_bx, SC_cz, SC_cx = 0, 2048, 4096, 6144
        SC_dtc, SC_S, SC_Sdt, SC_m12, SC_m22 = 8192, 10240, 12288, 14336, 16384
        SC_p11, SC_p12, SC_p21, SC_p22 = 18432, 20480, 22528, 24576
        SC_cumI, SC_idPw = 26624, 28672
        SC_qt1, SC_qt2 = 26624, 28672
        SC_tq1, SC_tq2 = 14336, 16384
        SC_U = 8192
        SC_Xz, SC_Xs, SC_Xf = 30720, 32768, 34816

        Xz = big[0:32, SC_Xz:SC_Xz + 2048]
        Xs_ = big[0:32, SC_Xs:SC_Xs + 2048]
        Xfull = nc.alloc_sbuf_tensor_at(
            "Xfull", [64, 2048], F32, offset=BIG_OFF + SC_Xf * 4).ap()
        ktuz = big[0:32, SC_bz:SC_bz + 2048]
        ktus = big[0:32, SC_bx:SC_bx + 2048]
        tmpX = big[0:32, SC_tq1:SC_tq1 + 2048]
        Qst = nc.alloc_sbuf_tensor_at(
            "Qst", [64, 4096], F32, offset=BIG_OFF + SC_p11 * 4).ap()
        Kst = nc.alloc_sbuf_tensor_at(
            "Kst", [64, 4096], F32, offset=BIG_OFF + SC_p21 * 4).ap()
        Ust = nc.alloc_sbuf_tensor_at(
            "Ust", [128, 4096], F32, offset=BIG_OFF + SC_U * 4).ap()
        xwr3 = BGb(20480, 6144)
        cst = BG(23552, 2048)
        cbf = BGb(25600, 4096)
        cbf2 = BGb(27648, 4096)
        gateT = BG(29696, 768)

        ps_a = es.enter_context(nc.psum_tensor("ps_a", [128, 512], F32))
        ps_b = es.enter_context(nc.psum_tensor("ps_b", [128, 512], F32))
        ps_c = es.enter_context(nc.psum_tensor("ps_c", [128, 512], F32))
        ps_d = es.enter_context(nc.psum_tensor("ps_d", [128, 512], F32))
        ps_G = es.enter_context(nc.psum_tensor("ps_G", [128, 512], F32))
        ps_y = es.enter_context(nc.psum_tensor("ps_y", [64, 512], F32))
        ps_k0 = es.enter_context(nc.psum_tensor("ps_k0", [32, 512], F32))
        ps_k1 = es.enter_context(nc.psum_tensor("ps_k1", [32, 512], F32))

        kb = KB()
        sems = {}
        for nm in ("sync", "gpd", "pe", "dve", "act", "gp", "cc"):
            sems[nm] = es.enter_context(nc.semaphore(nm))
        for nm in ("sync", "gpd", "pe", "dve", "act", "gp"):
            kb.q[nm].sem = sems[nm]

        blkctx = es.enter_context(nc.Block())

        def DMA(qn, dst, src):
            return kb.on(qn, lambda e: e.dma_start(out=dst, in_=src))

        def MM(dst, lhsT, rhs, start=True, stop=True):
            return kb.on("pe", lambda e: e.matmul(dst, lhsT, rhs, start=start, stop=stop))

        def TRN(dst, src, ident):
            return kb.on("pe", lambda e: e.transpose(dst, src, ident))

        def TT(dst, a, b, op=OP.mult, q="dve"):
            return kb.on(q, lambda e: e.tensor_tensor(dst, a, b, op))

        def TS1(dst, a, s1, op=OP.mult, q="dve"):
            return kb.on(q, lambda e: e.tensor_single_scalar(dst, a, s1, op))

        def TS2(dst, a, s1, s2, op0, op1, q="dve"):
            return kb.on(q, lambda e: e.tensor_scalar(dst, a, s1, s2, op0, op1))

        def STT(dst, in0, scalar, in1, op0=OP.mult, op1=OP.add, q="dve"):
            return kb.on(q, lambda e: e.scalar_tensor_tensor(dst, in0, scalar, in1, op0, op1))

        def CPY(dst, src, q="dve"):
            return kb.on(q, lambda e: e.tensor_copy(dst, src))

        def RCP(dst, src):
            return kb.on("dve", lambda e: e.reciprocal(dst, src))

        def ACTV(dst, src, func, bias=0.0, scale=1.0):
            return kb.on("act", lambda e: e.activation(dst, src, func, bias=bias, scale=scale))

        def SCANOP(dst, d0, d1, init):
            return kb.on("dve", lambda e: e.tensor_tensor_scan(dst, d0, d1, init, OP.mult, OP.add))

        def MS(qn, ap, v):
            return kb.on(qn, lambda e: e.memset(ap, v))

        def FILL(ap, v):
            src_col = onescol if v == 1.0 else zc
            shp = list(ap.shape)
            bc_ap = src_col[0:shp[0], 0:1]
            for d in shp[2:]:
                bc_ap = bc_ap.unsqueeze(len(bc_ap.shape))
            return kb.on("dve", lambda e: e.tensor_copy(ap, bc_ap.broadcast_to(shp)))

        def RS(dst, src):
            return kb.on("dve", lambda e: e.reduce_sum(dst, src, axis=mybir.AxisListType.X))

        Wt = kb.wait; WS = kb.wait_sem; DR = kb.drain

        def bar(qn):
            for n in kb.q:
                if n != qn and kb.q[n].count > 0:
                    Wt(qn, n, kb.q[n].count)

        for dst, src in ((trils[:], tril_in), (sel4s[:], sel4_in), (sel4Ts[:], sel4T_in),
                         (dwWs[:], dwW), (dwBs[:], dwB), (skips[:], skipc),
                         (A_s[:], A_log), (dtb_s[:], dt_bias), (bo_r[:], bo),
                         (inc_s[:], in_c), (identf[:], ident_in), (ones1s[:], ones1_in),
                         (onescol[:], onescol_in)):
            DMA("sync", dst, src)
        for nm in ("f", "i", "c"):
            ld_c = DMA("sync", bias_s[nm][:], b_in[nm])

        Wt("act", "sync", ld_c)
        aA = ACTV(A_s[:], A_s[:], AF.Exp)
        Wt("dve", "act", aA)
        CPY(identb[:], identf[:])
        CPY(identr[:], identf[:])
        DR("dve")
        TS1(Aneg[:], A_s[:], -1.0)
        if os.environ.get("SKIPBO", "") != "1":
            Wt("pe", "sync", ld_c)
            MM(ps_a[:, 0:512], ones1s[:], bo_r[:, 0:512])
            m2 = MM(ps_b[:, 0:512], ones1s[:], bo_r[:, 512:1024])
            Wt("dve", "pe", m2)
            CPY(bo_b[:, 0:512], ps_a[:, 0:512])
            CPY(bo_b[:, 512:1024], ps_b[:, 0:512])
        SKIPX = os.environ.get("SKIPX", "") == "1"
        for Ex in (() if SKIPX else (E1, E2)):
            MS("dve", Ex["11"][:], 0.0); MS("dve", Ex["12"][:], 0.0)
            MS("dve", Ex["21"][:], 0.0); MS("dve", Ex["22"][:], 0.0)
        DR("dve")
        for Ex in (() if SKIPX else (E1, E2)):
            for gi in range(NG):
                MS("dve", Ex["11"][:, gi * EW: gi * EW + EPAD], 1.0)
                MS("dve", Ex["22"][:, gi * EW: gi * EW + EPAD], 1.0)
        if not SKIPX:
            MS("dve", Xz, 0.0)
            MS("dve", Xs_, 0.0)
            MS("dve", zc[:], 0.0)
            DR("dve")
            CPY(Xfull[0:32, :], Xz)
            CPY(Xfull[32:64, :], Xs_)
            DR("dve")

        def _finish():
            kb.q["sync"].ops.append(
                lambda e, s=kb.q["sync"]: e.wait_ge(s.sem, s.count))

            @blkctx.sync
            def _(e):
                for op in kb.q["sync"].ops:
                    op(e)

            @blkctx.tensor
            def _(e):
                for op in kb.q["pe"].ops:
                    op(e)

            @blkctx.vector
            def _(e):
                for op in kb.q["dve"].ops:
                    op(e)

            @blkctx.scalar
            def _(e):
                for op in kb.q["act"].ops:
                    op(e)

            @blkctx.gpsimd
            def _(e):
                for op in kb.q["gpd"].ops:
                    op(e)
            return nc


        LNCUT = int(os.environ.get("LNCUT", "9"))

        def ln_tile(src_hbm, tt, dst_dram):
            lx = DMA("sync", xln[:], src_hbm[tt * 128:(tt + 1) * 128, :])
            Wt("dve", "sync", lx)
            bar("dve") if False else None
            DR("dve")
            RS(stat[:, 0:1].unsqueeze(2), xln[:].unsqueeze(1))
            if LNCUT <= 1:
                return lx
            Wt("act", "sync", lx)
            Wt("act", "dve", kb.q["dve"].count)
            a1 = ACTV(sqt[:, 0:512], xln[:, 0:512], AF.Square)
            Wt("dve", "act", a1)
            DR("dve")
            RS(stat[:, 1:2].unsqueeze(2), sqt[:, 0:512].unsqueeze(1))
            d0 = kb.q["dve"].count
            Wt("act", "dve", d0)
            a2 = ACTV(sqt[:, 0:512], xln[:, 512:1024], AF.Square)
            Wt("dve", "act", a2)
            DR("dve")
            RS(stat[:, 2:3].unsqueeze(2), sqt[:, 0:512].unsqueeze(1))
            DR("dve")
            TT(stat[:, 1:2], stat[:, 1:2], stat[:, 2:3], OP.add)
            DR("dve")
            TS1(stat[:, 0:1], stat[:, 0:1], 1.0 / D)
            TS1(stat[:, 1:2], stat[:, 1:2], 1.0 / D)
            DR("dve")
            STT(stat[:, 2:3], stat[:, 0:1], stat[:, 0:1], stat[:, 1:2],
                OP.mult, OP.subtract)
            DR("dve")
            d1 = TS2(stat[:, 2:3], stat[:, 2:3], -1.0, LN_EPS, OP.mult, OP.add)
            Wt("act", "dve", d1)
            ar = ACTV(stat[:, 3:4], stat[:, 2:3], AF.Sqrt)
            Wt("dve", "act", ar)
            RCP(stat[:, 3:4], stat[:, 3:4])
            DR("dve")
            TT(stat[:, 2:3], stat[:, 0:1], stat[:, 3:4], OP.mult)
            DR("dve")
            d2 = TS1(stat[:, 2:3], stat[:, 2:3], -1.0)
            if LNCUT <= 2:
                return lx
            Wt("act", "dve", d2)
            a3 = ACTV(xln2[:], xln[:], AF.Identity, bias=stat[:, 2:3], scale=stat[:, 3:4])
            if LNCUT <= 3:
                return lx
            for dc in range(8):
                Wt("pe", "act", kb.q["act"].count)
                ptr = TRN(ps_d[:, 0:128], xln2[:, dc * 128:(dc + 1) * 128], identf[:])
                if LNCUT <= 4:
                    continue
                Wt("act", "pe", ptr)
                Wt("act", "sync", kb.q["sync"].count)
                dcc2 = ACTV(ygbf[:, 0:128], ps_d[:, 0:128], AF.Identity)
                if LNCUT <= 5:
                    continue
                Wt("sync", "act", dcc2)
                dl = DMA("sync", dst_dram[:, dc * T + tt * 128: dc * T + tt * 128 + 128],
                         ygbf[:, 0:128])
            if LNCUT <= 5:
                return lx
            return dl

        if PH <= -2:
            return _finish()
        for tt in range(16):
            ln_tile(x_ffn, tt, d_xnf)
            if PH <= -1:
                return _finish()
        for tt in range(16):
            ld_xn = ln_tile(x_ssm, tt, d_xn)

        if PH <= 0:
            return _finish()

        bcH_in = din("bcH", (32, NG * 128))
        bcHs = sb("bcHs", 32, NG * 128)
        idEn = sb("idEn", 128, NG * NBL)
        ldh = DMA("sync", bcHs[:], bcH_in)

        xnfb = BGb(0, 16384)
        WS("gpd", sems["sync"], kb.q["sync"].count)
        DMA("gpd", xnfb[:, 0:8192], d_xnf[:, 0:8192])
        ldf = DMA("gpd", xnfb[:, 8192:16384], d_xnf[:, 8192:16384])
        FFu = {"uf": 8192, "ui": 12288, "uc": 16384}
        for nm in ("uf", "ui", "uc"):
            for kc in range(8):
                ldf = DMA("gpd", BGb(FFu[nm] + kc * 512, 2048)[:, 0:1024],
                          wT[nm][kc * 128:(kc + 1) * 128, :])

        PSL = [ps_a, ps_b, ps_c, ps_d]
        for gidx, wnm in enumerate(("wf", "wi", "wc")):
            for rs in range(4):
                for mh in range(2):
                    for kc in range(8):
                        WS("gpd", sems["pe"], kb.q["pe"].count)
                        lw = DMA("gpd", wrowB[:, 0:512],
                                 wT[wnm][kc * 128:(kc + 1) * 128, mh * 512:(mh + 1) * 512])
                        WS("pe", sems["gpd"], lw)
                        if kc == 0:
                            Wt("pe", "act", kb.q["act"].count)
                        for mq in range(4):
                            mmx = MM(PSL[mq][:, 0:512],
                                     wrowB[:, mq * 128:(mq + 1) * 128],
                                     xnfb[:, kc * 2048 + rs * 512: kc * 2048 + (rs + 1) * 512],
                                     start=(kc == 0), stop=(kc == 7))
                    Wt("act", "pe", mmx)
                    for mq in range(4):
                        mc = mh * 4 + mq
                        Wt("act", "sync", kb.q["sync"].count)
                        av = ACTV(xnbf[:, 0:512], PSL[mq][:, 0:512], AF.Identity)
                        Wt("sync", "act", av)
                        DMA("sync", xw_d[gidx][:, mc * T + rs * 512: mc * T + (rs + 1) * 512],
                            xnbf[:, 0:512])

        Wt("dve", "sync", ld_c)
        for mc in range(8):
            CPY(cst[:, mc * 256:(mc + 1) * 256],
                inc_s[:, mc:mc + 1].broadcast_to([128, 256]))
        DR("dve")
        for mc in range(8):
            d_cbf = CPY(cbf[:, mc * 256:(mc + 1) * 256], cst[:, mc * 256:(mc + 1) * 256])
        WS("pe", sems["gpd"], ldf)

        for step in range(8):
            csrc = cbf if step % 2 == 0 else cbf2
            cdst = cbf2 if step % 2 == 0 else cbf
            Wt("sync", "dve", kb.q["dve"].count)
            for gidx in range(3):
                lxw = DMA("sync", xwr3[:, gidx * 2048:(gidx + 1) * 2048].rearrange(
                              "p (mc t) -> p mc t", mc=8),
                          xw_d[gidx].rearrange("p (mc t) -> p mc t", mc=8)[:, :, step * 256:(step + 1) * 256])
            WS("pe", sems["sync"], lxw)
            Wt("pe", "dve", d_cbf)
            for mc in range(8):
                Wt("pe", "dve", kb.q["dve"].count)
                for gidx in range(3):
                    ps = (ps_a, ps_b, ps_c)[gidx]
                    unm = ("uf", "ui", "uc")[gidx]
                    for kc in range(8):
                        mmu = MM(ps[:, 0:256],
                                 BGb(FFu[unm] + kc * 512, 2048)[:, mc * 128:(mc + 1) * 128],
                                 csrc[:, kc * 256:(kc + 1) * 256],
                                 start=(kc == 0), stop=(kc == 7))
                Wt("dve", "pe", mmu)
                for gidx in range(3):
                    ps = (ps_a, ps_b, ps_c)[gidx]
                    TT(sqt[:, 0:256], ps[:, 0:256],
                       xwr3[:, gidx * 2048 + mc * 256: gidx * 2048 + (mc + 1) * 256], OP.add)
                    DR("dve")
                    dtx = kb.q["dve"].count
                    Wt("act", "dve", dtx)
                    fn = (AF.Sigmoid, AF.Sigmoid, AF.Tanh)[gidx]
                    ag = ACTV(gateT[:, gidx * 256:(gidx + 1) * 256], sqt[:, 0:256],
                              fn, bias=bias_s[("f", "i", "c")[gidx]][:, mc:mc + 1])
                    Wt("dve", "act", ag)
                TT(cst[:, mc * 256:(mc + 1) * 256], gateT[:, 0:256],
                   cst[:, mc * 256:(mc + 1) * 256])
                TT(gateT[:, 256:512], gateT[:, 256:512], gateT[:, 512:768])
                DR("dve")
                TT(cst[:, mc * 256:(mc + 1) * 256], cst[:, mc * 256:(mc + 1) * 256],
                   gateT[:, 256:512], OP.add)
                DR("dve")
                d_cbf = CPY(cdst[:, mc * 256:(mc + 1) * 256], cst[:, mc * 256:(mc + 1) * 256])

        cfin = cbf2 if 8 % 2 == 1 else cbf
        for rc in range(2):
            for oc in range(2):
                ps = (ps_a, ps_b)[oc]
                for kc in range(8):
                    WS("gpd", sems["pe"], kb.q["pe"].count)
                    lw = DMA("gpd", wrowB[:], wT["uo"][kc * 128:(kc + 1) * 128, :])
                    WS("pe", sems["gpd"], lw)
                    if kc == 0:
                        Wt("pe", "dve", kb.q["dve"].count)
                    mmu = MM(ps[:, 0:512],
                             cfin[:, kc * 256 + rc * 128: kc * 256 + rc * 128 + 128],
                             wrowB[:, oc * 512:(oc + 1) * 512],
                             start=(kc == 0), stop=(kc == 7))
                Wt("dve", "pe", mmu)
                Wt("dve", "sync", kb.q["sync"].count)
                duo = CPY(sqt[:, 0:512], ps[:, 0:512])
                Wt("sync", "dve", duo)
                dcc = DMA("sync", cc_in[rc * 128:(rc + 1) * 128, oc * 512:(oc + 1) * 512],
                          sqt[:, 0:512])
        WS("gpd", sems["sync"], dcc)
        kb.q["gpd"].ops.append(
            lambda e: e.collective_compute(
                "AllGather", OP.bypass, replica_groups=[list(range(8))],
                ins=[cc_in], outs=[cc_out]).then_inc(sems["cc"], 1))

        if PH <= 1:
            return _finish()

        xnb = BGb(0, 16384)
        WS("gpd", sems["pe"], kb.q["pe"].count)
        DMA("gpd", xnb[:, 0:8192], d_xn[:, 0:8192])
        ldx = DMA("gpd", xnb[:, 8192:16384], d_xn[:, 8192:16384])
        WS("pe", sems["gpd"], ldx)
        for oc in range(33):
            owid = 128 if oc < 32 else 64
            WS("gpd", sems["pe"], kb.q["pe"].count)
            for kc in range(8):
                lw = DMA("gpd", winB[:, kc * 128: kc * 128 + owid],
                         W_inT[kc * 128:(kc + 1) * 128, oc * 128: oc * 128 + owid])
            for tch in range(4):
                WS("pe", sems["gpd"], lw)
                Wt("pe", "act", kb.q["act"].count)
                Wt("pe", "dve", kb.q["dve"].count)
                for kc in range(8):
                    mmp = MM(ps_d[0:owid, 0:512], winB[:, kc * 128: kc * 128 + owid],
                             xnb[:, kc * 2048 + tch * 512: kc * 2048 + (tch + 1) * 512],
                             start=(kc == 0), stop=(kc == 7))
                if oc < 16:
                    Wt("act", "pe", mmp)
                    Wt("act", "sync", kb.q["sync"].count)
                    av = ACTV(sqt[:, 0:512], ps_d[:, 0:512], AF.Silu)
                    Wt("sync", "act", av)
                    DMA("sync", d_gate[:, oc * T + tch * 512: oc * T + (tch + 1) * 512],
                        sqt[:, 0:512])
                elif oc < 32:
                    Wt("act", "pe", mmp)
                    Wt("act", "sync", kb.q["sync"].count)
                    av = ACTV(sqt[:, 0:512], ps_d[:, 0:512], AF.Identity)
                    Wt("sync", "act", av)
                    DMA("sync", d_vr[:, (oc - 16) * T + tch * 512: (oc - 16) * T + (tch + 1) * 512],
                        sqt[:, 0:512])
                else:
                    Wt("act", "pe", mmp)
                    Wt("act", "sync", kb.q["sync"].count)
                    av = ACTV(sqt[0:64, 0:512], ps_d[0:64, 0:512], AF.Identity)
                    Wt("sync", "act", av)
                    DMA("sync", d_par[:, tch * 512:(tch + 1) * 512], sqt[0:64, 0:512])

        if PH <= 2:
            return _finish()

        vrb = BG(0, 2048)
        acc = BG(2048, 2048)
        vout = BGr(4096, 2048)
        WS("gpd", sems["pe"], kb.q["pe"].count)
        for ic in range(NIC):
            WS("gpd", sems["sync"], kb.q["sync"].count)
            WS("gpd", sems["dve"], kb.q["dve"].count)
            WS("gpd", sems["act"], kb.q["act"].count)
            lv = DMA("gpd", vrb, d_vr[:, ic * T:(ic + 1) * T])
            WS("dve", sems["gpd"], lv)
            TS2(acc, vrb, dwWs[:, ic * 4 + 3:ic * 4 + 4], dwBs[:, ic:ic + 1],
                OP.mult, OP.add)
            DR("dve")
            for kk in (2, 1, 0):
                sh = 3 - kk
                STT(acc[:, sh:T], vrb[:, 0:T - sh],
                    dwWs[:, ic * 4 + kk:ic * 4 + kk + 1], acc[:, sh:T],
                    OP.mult, OP.add)
                DR("dve")
            dcv = kb.q["dve"].count
            Wt("act", "dve", dcv)
            av = ACTV(vout, acc, AF.Silu)
            Wt("sync", "act", av)
            dv = DMA("sync", d_val[:, ic * T:(ic + 1) * T], vout.bitcast(F32))

        if PH <= 3:
            return _finish()

        valb = BGb(0, 32768)
        bstg = BG(16384, 512)
        WS("gpd", sems["sync"], dv)
        WS("gpd", sems["dve"], kb.q["dve"].count)
        ldv = DMA("gpd", valb[:, 0:8192], d_val[:, 0:8192])
        ldv = DMA("gpd", valb[:, 8192:16384], d_val[:, 8192:16384])
        ldv = DMA("gpd", valb[:, 16384:24576], d_val[:, 16384:24576])
        ldv = DMA("gpd", valb[:, 24576:32768], d_val[:, 24576:32768])
        WS("pe", sems["gpd"], ldv)
        for q4 in range(4):
            for gi in range(NG):
                ocol = q4 * NG + gi
                WS("gpd", sems["pe"], kb.q["pe"].count)
                for kc in range(16):
                    buf, slot = (wrowB, kc) if kc < 8 else (wrow2, kc - 8)
                    lw = DMA("gpd", buf[:, slot * 128:(slot + 1) * 128],
                             W_bcT[kc * 128:(kc + 1) * 128, ocol * 128:(ocol + 1) * 128])
                for tch in range(4):
                    WS("pe", sems["gpd"], lw)
                    Wt("pe", "dve", kb.q["dve"].count)
                    Wt("pe", "act", kb.q["act"].count)
                    for kc in range(16):
                        buf, slot = (wrowB, kc) if kc < 8 else (wrow2, kc - 8)
                        mmb = MM(ps_d[:, 0:512], buf[:, slot * 128:(slot + 1) * 128],
                                 valb[:, kc * 2048 + tch * 512: kc * 2048 + (tch + 1) * 512],
                                 start=(kc == 0), stop=(kc == 15))
                    Wt("act", "pe", mmb)
                    asq = ACTV(sqt[:, 0:512], ps_d[:, 0:512], AF.Square)
                    Wt("pe", "act", asq)
                    mss = MM(ps_G[0:4, 0:512], sel4s[:], sqt[:, 0:512])
                    Wt("act", "pe", mss)
                    asr = ACTV(fss[:, 0:512], ps_G[0:4, 0:512], AF.Sqrt)
                    Wt("dve", "act", asr)
                    TS1(fss[:, 0:512], fss[:, 0:512], EPS, OP.add)
                    DR("dve")
                    dfr = RCP(fss[:, 0:512], fss[:, 0:512])
                    Wt("pe", "dve", dfr)
                    mbb = MM(ps_G[:, 0:512], sel4Ts[:], fss[:, 0:512])
                    Wt("dve", "pe", mbb)
                    dcp = CPY(fsb[:, 0:512], ps_G[:, 0:512])
                    DR("dve")
                    Wt("dve", "sync", kb.q["sync"].count)
                    dtt = TT(bstg, ps_d[:, 0:512], fsb[:, 0:512])
                    Wt("sync", "dve", dtt)
                    dbc = DMA("sync", d_bc[q4][:, gi * T + tch * 512: gi * T + (tch + 1) * 512],
                              bstg)

        if PH <= 4:
            return _finish()

        def cgAP(off):
            return BG(off, 2048)

        def cgrAP(off):
            return BGr(off, 2048)

        def Ab(col):
            return col.unsqueeze(2).broadcast_to([128, NG, 256])

        def cg3(off):
            return BG(off, 2048).rearrange("p (g t) -> p g t", g=NG)

        def Edat(Ex, c2, sh=0):
            ap = Ex[c2][:].rearrange("p (g s) -> p g s", g=NG)
            return ap[:, :, EPAD - sh: EPAD - sh + NBL]

        def Ebcast(Ex, c2):
            ap = Ex[c2][:].rearrange("p (g s) -> p g s", g=NG)
            return ap[:, :, EPAD:EPAD + NBL].unsqueeze(3).broadcast_to([128, NG, NBL, BL])

        idE3 = idE[:].rearrange("p (g b) -> p g b", g=NG)

        CHCUT = int(os.environ.get("CHCUT", "99"))
        for ch in range(NCH):
            t0 = ch * L
            WS("gpd", sems["dve"], kb.q["dve"].count)
            WS("gpd", sems["pe"], kb.q["pe"].count)
            for q4, off in enumerate((SC_bz, SC_bx, SC_cz, SC_cx)):
                lbc = DMA("gpd", cg3(off),
                          d_bc[q4].rearrange("p (g t) -> p g t", g=NG)[:, :, t0:t0 + L])
            if CHCUT < 1:
                continue
            Wt("sync", "act", kb.q["act"].count)
            Wt("sync", "dve", kb.q["dve"].count)
            lpar = DMA("sync", parsl[:], d_par[:, t0:t0 + L])
            WS("act", sems["sync"], lpar)
            Wt("act", "dve", kb.q["dve"].count)
            Wt("act", "pe", kb.q["pe"].count)
            ax = ACTV(kdsl[:], parsl[0:32, :], AF.Identity, bias=dtb_s[:])
            aab = ACTV(dtsl[:], parsl[0:32, :], AF.Abs, bias=dtb_s[:])
            DR("act")
            ae = ACTV(dtsl[:], dtsl[:], AF.Exp, scale=-1.0)
            DR("act")
            al = ACTV(dtsl[:], dtsl[:], AF.Ln, bias=1.0)
            Wt("dve", "act", al)
            STT(dtsl[:], kdsl[:], 0.0, dtsl[:], OP.max, OP.add)
            DR("dve")
            ddt = TS2(dtsl[:], dtsl[:], DT_MAX, DT_MIN, OP.min, OP.max)
            Wt("act", "dve", ddt)
            akd = ACTV(kdsl[:], parsl[32:64, :], AF.Sigmoid)
            Wt("dve", "act", akd)
            TS1(kdsl[:], kdsl[:], K_MAX)
            DR("dve")
            TT(dt2sl[:], dtsl[:], dtsl[:])
            TT(dk1sl[:], dtsl[:], kdsl[:])
            DR("dve")
            dgr = TS1(dk1sl[:], dk1sl[:], 1.0, OP.add)
            if CHCUT < 2:
                continue
            Wt("pe", "dve", dgr)
            for gi in range(NG):
                mm1 = MM(ps_a[:, 0:256], bcHs[:, gi * 128:(gi + 1) * 128], dtsl[:])
                mm2 = MM(ps_b[:, 0:256], bcHs[:, gi * 128:(gi + 1) * 128], dk1sl[:])
                mm3 = MM(ps_c[:, 0:256], bcHs[:, gi * 128:(gi + 1) * 128], dt2sl[:])
                Wt("dve", "pe", mm3)
                CPY(cgAP(SC_dtc)[:, gi * 256:(gi + 1) * 256], ps_a[:, 0:256])
                CPY(cgAP(SC_S)[:, gi * 256:(gi + 1) * 256], ps_b[:, 0:256])
                DR("dve")
                dbb = STT(cgAP(SC_S)[:, gi * 256:(gi + 1) * 256], ps_c[:, 0:256],
                          A_s[:, gi:gi + 1], cgAP(SC_S)[:, gi * 256:(gi + 1) * 256],
                          OP.mult, OP.add)
                Wt("pe", "dve", dbb)
            DR("dve")
            if CHCUT < 3:
                continue
            for gi in range(NG):
                SCANOP(cgAP(SC_cumI)[:, gi * 256:(gi + 1) * 256],
                       cgAP(SC_S)[:, gi * 256:(gi + 1) * 256],
                       zc[:].broadcast_to([128, 256]), 1.0)
            DR("dve")
            MS("dve", rbsl[:], 1.0)
            MS("dve", idE[:], 1.0)
            DR("dve")
            cum3 = cg3(SC_cumI)
            RCP(rbsl[:].rearrange("p (g b) -> p g b", g=NG)[:, :, 1:NBL],
                cum3[:, :, BL - 1::BL][:, :, 0:NBL - 1])
            CPY(idE3[:, :, 1:NBL], cum3[:, :, BL - 1::BL][:, :, 0:NBL - 1])
            DR("dve")
            TS1(idEn[:], idE[:], -1.0)
            TT(BG(SC_idPw, 2048).rearrange("p (g b i) -> p g b i", g=NG, i=BL),
               BG(SC_cumI, 2048).rearrange("p (g b i) -> p g b i", g=NG, i=BL),
               rbsl[:].rearrange("p (g b) -> p g b", g=NG).unsqueeze(3).broadcast_to([128, NG, NBL, BL]))
            DR("dve")
            if CHCUT < 4:
                continue
            RCP(cgAP(SC_S), cgAP(SC_S))
            DR("dve")
            TT(cgAP(SC_Sdt), cgAP(SC_S), cgAP(SC_dtc))
            DR("dve")
            WS("dve", sems["gpd"], lbc)
            TT(cgAP(SC_bz), cgAP(SC_Sdt), cgAP(SC_bz))
            DR("dve")
            TT(cgAP(SC_bx), cgAP(SC_bz), cgAP(SC_bx), OP.add)
            DR("dve")
            TT(cgAP(SC_bx), cgAP(SC_dtc), cgAP(SC_bx))
            DR("dve")
            TT(cgAP(SC_bx), cgAP(SC_bx), cgAP(SC_idPw))
            TT(cgAP(SC_bz), cgAP(SC_bz), cgAP(SC_idPw))
            DR("dve")
            TT(cg3(SC_m12), cg3(SC_Sdt), Ab(Aneg[:]))
            TT(cgAP(SC_m22), cgAP(SC_Sdt), cgAP(SC_dtc))
            DR("dve")
            TT(cg3(SC_m22), cg3(SC_m22), Ab(Aneg[:]))
            DR("dve")
            dmb = TS1(cgAP(SC_m22), cgAP(SC_m22), 1.0, OP.add)
            if CHCUT < 5:
                continue
            def st_i(off, i):
                return BG(off, 2048).rearrange("p (g b i) -> p (g b) i", g=NG, i=BL)[:, :, i]
            DR("dve")
            CPY(st_i(SC_p11, 0), st_i(SC_S, 0))
            CPY(st_i(SC_p12, 0), st_i(SC_m12, 0))
            DR("dve")
            CPY(st_i(SC_p21, 0), st_i(SC_Sdt, 0))
            CPY(st_i(SC_p22, 0), st_i(SC_m22, 0))
            DR("dve")
            for i in range(1, BL):
                TT(sqt[:, 0:128], st_i(SC_m12, i), st_i(SC_p21, i - 1))
                TT(st_i(SC_p11, i), st_i(SC_S, i), st_i(SC_p11, i - 1))
                TT(sqt[:, 128:256], st_i(SC_m12, i), st_i(SC_p22, i - 1))
                TT(st_i(SC_p12, i), st_i(SC_S, i), st_i(SC_p12, i - 1))
                DR("dve")
                TT(st_i(SC_p11, i), st_i(SC_p11, i), sqt[:, 0:128], OP.add)
                TT(st_i(SC_p12, i), st_i(SC_p12, i), sqt[:, 128:256], OP.add)
                TT(sqt[:, 256:384], st_i(SC_m22, i), st_i(SC_p21, i - 1))
                TT(st_i(SC_p21, i), st_i(SC_Sdt, i), st_i(SC_p11, i - 1))
                TT(sqt[:, 384:512], st_i(SC_m22, i), st_i(SC_p22, i - 1))
                TT(st_i(SC_p22, i), st_i(SC_Sdt, i), st_i(SC_p12, i - 1))
                DR("dve")
                TT(st_i(SC_p21, i), st_i(SC_p21, i), sqt[:, 256:384], OP.add)
                TT(st_i(SC_p22, i), st_i(SC_p22, i), sqt[:, 384:512], OP.add)
                DR("dve")
            if CHCUT < 6:
                continue
            def bend(off):
                return BG(off, 2048).rearrange("p (g b i) -> p g b i", g=NG, i=BL)[:, :, :, BL - 1]
            for c2, off in (("11", SC_p11), ("12", SC_p12), ("21", SC_p21), ("22", SC_p22)):
                CPY(Edat(E1, c2)[:, :, 1:NBL], bend(off)[:, :, 0:NBL - 1])
            DR("dve")
            MS("dve", Edat(E1, "11")[:, :, 0:1], 1.0)
            MS("dve", Edat(E1, "12")[:, :, 0:1], 0.0)
            MS("dve", Edat(E1, "21")[:, :, 0:1], 0.0)
            MS("dve", Edat(E1, "22")[:, :, 0:1], 1.0)
            DR("dve")
            src_, dst_ = E1, E2
            for sh in (1, 2, 4, 8):
                TT(Edat(dst_, "11"), Edat(src_, "11"), Edat(src_, "11", sh))
                TT(sqt[:, 0:128], Edat(src_, "12"), Edat(src_, "21", sh))
                TT(Edat(dst_, "12"), Edat(src_, "11"), Edat(src_, "12", sh))
                TT(sqt[:, 128:256], Edat(src_, "12"), Edat(src_, "22", sh))
                DR("dve")
                TT(Edat(dst_, "11"), Edat(dst_, "11"), sqt[:, 0:128], OP.add)
                TT(Edat(dst_, "12"), Edat(dst_, "12"), sqt[:, 128:256], OP.add)
                TT(Edat(dst_, "21"), Edat(src_, "21"), Edat(src_, "11", sh))
                TT(sqt[:, 256:384], Edat(src_, "22"), Edat(src_, "21", sh))
                DR("dve")
                TT(Edat(dst_, "21"), Edat(dst_, "21"), sqt[:, 256:384], OP.add)
                TT(Edat(dst_, "22"), Edat(src_, "21"), Edat(src_, "12", sh))
                TT(sqt[:, 384:512], Edat(src_, "22"), Edat(src_, "22", sh))
                DR("dve")
                dE = TT(Edat(dst_, "22"), Edat(dst_, "22"), sqt[:, 384:512], OP.add)
                DR("dve")
                src_, dst_ = dst_, src_
            Ecum = src_
            def Eslot(Ex, c2, b):
                return Ex[c2][:].rearrange("p (g s) -> p g s", g=NG)[:, :, EPAD + b]
            TT(PLt["11"][:], bend(SC_p11)[:, :, NBL - 1], Eslot(Ecum, "11", NBL - 1))
            TT(sqt[:, 0:8], bend(SC_p12)[:, :, NBL - 1], Eslot(Ecum, "21", NBL - 1))
            TT(PLt["12"][:], bend(SC_p11)[:, :, NBL - 1], Eslot(Ecum, "12", NBL - 1))
            TT(sqt[:, 8:16], bend(SC_p12)[:, :, NBL - 1], Eslot(Ecum, "22", NBL - 1))
            TT(PLt["21"][:], bend(SC_p21)[:, :, NBL - 1], Eslot(Ecum, "11", NBL - 1))
            TT(sqt[:, 16:24], bend(SC_p22)[:, :, NBL - 1], Eslot(Ecum, "21", NBL - 1))
            TT(PLt["22"][:], bend(SC_p21)[:, :, NBL - 1], Eslot(Ecum, "12", NBL - 1))
            TT(sqt[:, 24:32], bend(SC_p22)[:, :, NBL - 1], Eslot(Ecum, "22", NBL - 1))
            DR("dve")
            TT(PLt["11"][:], PLt["11"][:], sqt[:, 0:8], OP.add)
            TT(PLt["12"][:], PLt["12"][:], sqt[:, 8:16], OP.add)
            TT(PLt["21"][:], PLt["21"][:], sqt[:, 16:24], OP.add)
            TT(PLt["22"][:], PLt["22"][:], sqt[:, 24:32], OP.add)
            DR("dve")
            for ci, c2 in enumerate(("11", "12", "21", "22")):
                for hs in range(4):
                    CPY(PL4[:, [[1, 1], [4, NG]]].copy() if False else
                        bass.AP(PL4.tensor if False else PL4[:].tensor, 0, [[0, 0]]) if False else
                        PL4[0:32, ci * H + hs: ci * H + hs + 29:4] if False else
                        PL4[0:32, :].rearrange("p (c h) -> p c h", c=4)[:, ci, hs::4],
                        PLt[c2][hs * 32:(hs + 1) * 32, :])
            DR("dve")
            TT(idE3 if False else BE["11"][:].rearrange("p (g b) -> p g b", g=NG),
               Edat(Ecum, "11"), idE3)
            TT(BE["12"][:].rearrange("p (g b) -> p g b", g=NG), Edat(Ecum, "12"),
               idEn[:].rearrange("p (g b) -> p g b", g=NG))
            TT(BE["21"][:].rearrange("p (g b) -> p g b", g=NG), Edat(Ecum, "21"),
               idEn[:].rearrange("p (g b) -> p g b", g=NG))
            dbe = TT(BE["22"][:].rearrange("p (g b) -> p g b", g=NG), Edat(Ecum, "22"), idE3)
            DR("dve")
            if CHCUT < 7:
                continue
            TT(cgAP(SC_qt1), cgAP(SC_p11), cgAP(SC_cz))
            TT(sqt[:, 0:0] if False else cgAP(SC_tq1), cgAP(SC_p21), cgAP(SC_cx))
            DR("dve")
            TT(cgAP(SC_qt1), cgAP(SC_qt1), cgAP(SC_tq1), OP.add)
            TT(cgAP(SC_qt2), cgAP(SC_p12), cgAP(SC_cz))
            TT(cgAP(SC_tq2), cgAP(SC_p22), cgAP(SC_cx))
            DR("dve")
            dqt = TT(cgAP(SC_qt2), cgAP(SC_qt2), cgAP(SC_tq2), OP.add)
            DR("dve")
            TT(cgAP(SC_cz), cgAP(SC_p22), cgAP(SC_bz))
            TT(cgAP(SC_tq1), cgAP(SC_p12), cgAP(SC_bx))
            TT(cgAP(SC_cx), cgAP(SC_p11), cgAP(SC_bx))
            TT(cgAP(SC_tq2), cgAP(SC_p21), cgAP(SC_bz))
            DR("dve")
            TT(cgAP(SC_cz), cgAP(SC_cz), cgAP(SC_tq1), OP.subtract)
            dkt = TT(cgAP(SC_cx), cgAP(SC_cx), cgAP(SC_tq2), OP.subtract)
            DR("dve")
            if CHCUT < 8:
                continue
            WS("gpd", sems["dve"], kb.q["dve"].count)
            for ic in range(NIC):
                WS("gpd", sems["pe"], kb.q["pe"].count)
                lvv = DMA("gpd", vring[:],
                          d_val[:, ic * T + t0: ic * T + t0 + L])
                for tb in range(2):
                    WS("pe", sems["gpd"], lvv)
                    Wt("pe", "dve", kb.q["dve"].count)
                    ptr = TRN(ps_d[:, 0:128], vring[:, tb * 128:(tb + 1) * 128],
                              identf[:])
                    Wt("dve", "pe", ptr)
                    dU = CPY(Ust[:, (ic * 2 + tb) * 128:(ic * 2 + tb) * 128 + 128],
                             ps_d[:, 0:128])
                    DR("dve")
                if ic < NIC - 1:
                    WS("gpd", sems["dve"], dU)
            if CHCUT < 9:
                continue
            for hh in range(2):
                Wt("dve", "pe", kb.q["pe"].count)
                TT(cg3(SC_tq1).rearrange("p g t -> p (g t)").rearrange("p (g b i) -> p g b i", g=NG, i=BL) if False else
                   BG(SC_tq1, 2048).rearrange("p (g b i) -> p g b i", g=NG, i=BL),
                   BG(SC_qt1, 2048).rearrange("p (g b i) -> p g b i", g=NG, i=BL),
                   Ebcast(Ecum, "11"))
                TT(BG(SC_tq2, 2048).rearrange("p (g b i) -> p g b i", g=NG, i=BL),
                   BG(SC_qt2, 2048).rearrange("p (g b i) -> p g b i", g=NG, i=BL),
                   Ebcast(Ecum, "21"))
                DR("dve")
                for hs in range(4):
                    TT(Qst[0:32, :].rearrange("p (hl t) -> p hl t", t=L)[:, hs::4, :],
                       BG(SC_tq1, 2048)[hs * 32:(hs + 1) * 32, :].rearrange(
                           "p (g t) -> p g t", t=L)[:, 4 * hh:4 * hh + 4, :],
                       BG(SC_tq2, 2048)[hs * 32:(hs + 1) * 32, :].rearrange(
                           "p (g t) -> p g t", t=L)[:, 4 * hh:4 * hh + 4, :], OP.add)
                DR("dve")
                TT(BG(SC_tq1, 2048).rearrange("p (g b i) -> p g b i", g=NG, i=BL),
                   BG(SC_qt1, 2048).rearrange("p (g b i) -> p g b i", g=NG, i=BL),
                   Ebcast(Ecum, "12"))
                TT(BG(SC_tq2, 2048).rearrange("p (g b i) -> p g b i", g=NG, i=BL),
                   BG(SC_qt2, 2048).rearrange("p (g b i) -> p g b i", g=NG, i=BL),
                   Ebcast(Ecum, "22"))
                DR("dve")
                for hs in range(4):
                    TT(Qst[32:64, :].rearrange("p (hl t) -> p hl t", t=L)[:, hs::4, :],
                       BG(SC_tq1, 2048)[hs * 32:(hs + 1) * 32, :].rearrange(
                           "p (g t) -> p g t", t=L)[:, 4 * hh:4 * hh + 4, :],
                       BG(SC_tq2, 2048)[hs * 32:(hs + 1) * 32, :].rearrange(
                           "p (g t) -> p g t", t=L)[:, 4 * hh:4 * hh + 4, :], OP.add)
                DR("dve")
                def BEb(c2):
                    return BE[c2][:].rearrange("p (g b) -> p g b", g=NG).unsqueeze(3).broadcast_to([128, NG, NBL, BL])
                TT(BG(SC_tq1, 2048).rearrange("p (g b i) -> p g b i", g=NG, i=BL),
                   BG(SC_cz, 2048).rearrange("p (g b i) -> p g b i", g=NG, i=BL), BEb("22"))
                TT(BG(SC_tq2, 2048).rearrange("p (g b i) -> p g b i", g=NG, i=BL),
                   BG(SC_cx, 2048).rearrange("p (g b i) -> p g b i", g=NG, i=BL), BEb("12"))
                DR("dve")
                for hs in range(4):
                    TT(Kst[0:32, :].rearrange("p (hl t) -> p hl t", t=L)[:, hs::4, :],
                       BG(SC_tq1, 2048)[hs * 32:(hs + 1) * 32, :].rearrange(
                           "p (g t) -> p g t", t=L)[:, 4 * hh:4 * hh + 4, :],
                       BG(SC_tq2, 2048)[hs * 32:(hs + 1) * 32, :].rearrange(
                           "p (g t) -> p g t", t=L)[:, 4 * hh:4 * hh + 4, :], OP.add)
                DR("dve")
                TT(BG(SC_tq1, 2048).rearrange("p (g b i) -> p g b i", g=NG, i=BL),
                   BG(SC_cz, 2048).rearrange("p (g b i) -> p g b i", g=NG, i=BL), BEb("21"))
                TT(BG(SC_tq2, 2048).rearrange("p (g b i) -> p g b i", g=NG, i=BL),
                   BG(SC_cx, 2048).rearrange("p (g b i) -> p g b i", g=NG, i=BL), BEb("11"))
                DR("dve")
                for hs in range(4):
                    dasm = TT(Kst[32:64, :].rearrange("p (hl t) -> p hl t", t=L)[:, hs::4, :],
                              BG(SC_tq1, 2048)[hs * 32:(hs + 1) * 32, :].rearrange(
                                  "p (g t) -> p g t", t=L)[:, 4 * hh:4 * hh + 4, :],
                              BG(SC_tq2, 2048)[hs * 32:(hs + 1) * 32, :].rearrange(
                                  "p (g t) -> p g t", t=L)[:, 4 * hh:4 * hh + 4, :], OP.add)
                DR("dve")
                HH = int(os.environ.get("HHCUT", "99"))
                for hl in range(16):
                    hgl = hh * 16 + hl
                    ic = hgl // 2
                    if HH < 2:
                        continue
                    Wt("pe", "dve", dasm)
                    Wt("pe", "act", kb.q["act"].count)
                    mmy = MM(ps_y[0:64, 0:L], Xfull[0:64, hgl * 64:(hgl + 1) * 64],
                             Qst[0:64, hl * L:(hl + 1) * L], start=True, stop=False)
                    if HH < 3:
                        continue
                    for j in (1, 0):
                        Wt("pe", "dve", kb.q["dve"].count)
                        Wt("pe", "act", kb.q["act"].count)
                        mmg = MM(ps_G[:, 0:L], Kst[0:64, hl * L + j * 128: hl * L + j * 128 + 128],
                                 Qst[0:64, hl * L:(hl + 1) * L], start=True, stop=True)
                        if HH < 4:
                            continue
                        Wt("act", "pe", mmg)
                        Wt("act", "gpd", kb.q["gpd"].count)
                        if j == 0:
                            aG = ACTV(Gsb[:, 0:256], ps_G[:, 0:256], AF.Identity)
                        else:
                            aG = ACTV(Gsb[:, 128:256], ps_G[:, 128:256], AF.Identity)
                        WS("gpd", sems["act"], aG)
                        dG = kb.on("gpd", lambda e, j=j: e.affine_select(
                            Gsb[:, j * 128:(j + 1) * 128], Gsb[:, j * 128:(j + 1) * 128],
                            [[1, 128]], OP.is_ge, 0.0, base=0, channel_multiplier=-1))
                        if HH < 5:
                            continue
                        WS("pe", sems["gpd"], dG)
                        mmy = MM(ps_y[0:64, j * 128:L],
                                 Ust[:, (ic * 2 + j) * 128 + (hgl % 2) * 64:
                                                 (ic * 2 + j) * 128 + (hgl % 2) * 64 + 64],
                                 Gsb[:, j * 128:L], start=False, stop=(j == 0))
                    if HH < 6:
                        continue
                    Wt("act", "pe", mmy)
                    Wt("act", "dve", kb.q["dve"].count)
                    ay = ACTV(ysb[(hgl % 2) * 64:(hgl % 2) * 64 + 64, 0:L], ps_y[0:64, 0:L],
                              AF.Identity)
                    if hgl % 2 == 1:
                        WS("gpd", sems["dve"], kb.q["dve"].count)
                        lvv = DMA("gpd", vring[:], d_val[:, ic * T + t0: ic * T + t0 + L])
                        lgg = DMA("gpd", gring[:], d_gate[:, ic * T + t0: ic * T + t0 + L])
                        Wt("dve", "act", ay)
                        WS("dve", sems["gpd"], lgg)
                        STT(sqt[:, 0:L], vring[:], skips[:, ic:ic + 1],
                            ysb[:, 0:L], OP.mult, OP.add)
                        DR("dve")
                        Wt("dve", "sync", kb.q["sync"].count)
                        dyg = TT(ygbf[:], sqt[:, 0:L], gring[:])
                        Wt("sync", "dve", dyg)
                        DMA("sync", d_yg[:, ic * T + t0: ic * T + t0 + L], ygbf[:])
                for hl in (range(16) if HH >= 7 else ()):
                    hgl = hh * 16 + hl
                    for j in range(2):
                        Wt("pe", "dve", kb.q["dve"].count)
                        ptr = TRN(ps_d[:, 0:64],
                                  Kst[0:64, hl * L + j * 128: hl * L + j * 128 + 128],
                                  identf[0:64, 0:64])
                        Wt("dve", "pe", ptr)
                        dKT = CPY(KTt[:], ps_d[:, 0:64])
                        Wt("pe", "dve", dKT)
                        MM(ps_k0[0:32, (hgl % 8) * 64:(hgl % 8) * 64 + 64],
                           KTt[:, 0:32],
                           Ust[:, (ic * 2 + j) * 128 + (hgl % 2) * 64:
                                           (ic * 2 + j) * 128 + (hgl % 2) * 64 + 64]
                           if False else
                           Ust[:, ((hgl // 2) * 2 + j) * 128 + (hgl % 2) * 64:
                                           ((hgl // 2) * 2 + j) * 128 + (hgl % 2) * 64 + 64],
                           start=(j == 0), stop=(j == 1))
                        mku = MM(ps_k1[0:32, (hgl % 8) * 64:(hgl % 8) * 64 + 64],
                                 KTt[:, 32:64],
                                 Ust[:, ((hgl // 2) * 2 + j) * 128 + (hgl % 2) * 64:
                                                 ((hgl // 2) * 2 + j) * 128 + (hgl % 2) * 64 + 64],
                                 start=(j == 0), stop=(j == 1))
                    if hgl % 8 == 7:
                        grp = hgl // 8
                        Wt("dve", "pe", mku)
                        TT(ktuz[:, grp * 512:(grp + 1) * 512], ps_k0[0:32, 0:512],
                           Xz[:, grp * 512:(grp + 1) * 512], OP.add)
                        dku = TT(ktus[:, grp * 512:(grp + 1) * 512], ps_k1[0:32, 0:512],
                                 Xs_[:, grp * 512:(grp + 1) * 512], OP.add)
                        DR("dve")
                        Wt("pe", "dve", dku)
            if CHCUT < 10:
                continue
            def PLb(ci):
                return PL4[0:32, :].rearrange("p (c h) -> p c h", c=4)[:, ci, :].unsqueeze(2).broadcast_to([32, H, DH])
            def hd3(ap):
                return ap.rearrange("p (h d) -> p h d", d=DH)
            TT(hd3(Xz), hd3(ktuz), PLb(0))
            TT(hd3(tmpX), hd3(ktus), PLb(1))
            DR("dve")
            TT(Xz, Xz, tmpX, OP.add)
            TT(hd3(Xs_), hd3(ktuz), PLb(2))
            DR("dve")
            TT(hd3(tmpX), hd3(ktus), PLb(3))
            DR("dve")
            TT(Xs_, Xs_, tmpX, OP.add)
            DR("dve")
            CPY(Xfull[0:32, :], Xz)
            dXf = CPY(Xfull[32:64, :], Xs_)
            DR("dve")

        if PH <= 5:
            return _finish()

        wob = BGb(0, 16384)
        woTb = BGb(8192, 8192)
        ygl = BGb(12288, 4096)
        xnl = BGb(14336, 2048)
        xt_t = BG(16384, 1024)
        uo_t = BG(17408, 1024)
        ht_t = BG(18432, 1024)
        g_t = BG(19456, 1024)
        o_t = BG(20480, 1024)
        WS("gpd", sems["pe"], kb.q["pe"].count)
        WS("gpd", sems["dve"], kb.q["dve"].count)
        for kc in range(16):
            lwo = DMA("gpd", wob[:, kc * 1024:(kc + 1) * 1024],
                      W_outT[kc * 128:(kc + 1) * 128, :])
        for kc in range(8):
            lwo = DMA("gpd", woTb[:, kc * 1024:(kc + 1) * 1024],
                      wT["wo"][kc * 128:(kc + 1) * 128, :])
        WS("dve", sems["cc"], 1)
        WS("sync", sems["cc"], 1)
        DR("dve")
        for tt in range(16):
            WS("gpd", sems["pe"], kb.q["pe"].count)
            for kc in range(16):
                lyg = DMA("gpd", ygl[:, kc * 128:(kc + 1) * 128],
                          d_yg[:, kc * T + tt * 128: kc * T + (tt + 1) * 128])
            for kc in range(8):
                lxn = DMA("gpd", xnl[:, kc * 128:(kc + 1) * 128],
                          d_xn[:, kc * T + tt * 128: kc * T + (tt + 1) * 128])
            Wt("sync", "dve", kb.q["dve"].count)
            lx = DMA("sync", xt_t, x_ssm[tt * 128:(tt + 1) * 128, :])
            lu = DMA("sync", uo_t, cc_out[tt * 128:(tt + 1) * 128, :])
            WS("pe", sems["gpd"], lyg if False else kb.q["gpd"].count)
            Wt("pe", "dve", kb.q["dve"].count)
            for oc in range(2):
                ps = (ps_a, ps_b)[oc]
                for kc in range(16):
                    mms = MM(ps[:, 0:512], ygl[:, kc * 128:(kc + 1) * 128],
                             wob[:, kc * 1024 + oc * 512: kc * 1024 + (oc + 1) * 512],
                             start=(kc == 0), stop=(kc == 15))
            for oc in range(2):
                ps = (ps_c, ps_d)[oc]
                for kc in range(8):
                    mmh = MM(ps[:, 0:512], xnl[:, kc * 128:(kc + 1) * 128],
                             woTb[:, kc * 1024 + oc * 512: kc * 1024 + (oc + 1) * 512],
                             start=(kc == 0), stop=(kc == 7))
            Wt("dve", "pe", mmh)
            WS("dve", sems["sync"], lu)
            TT(ht_t, uo_t, bo_b[:], OP.add)
            DR("dve")
            TT(ht_t[:, 0:512], ht_t[:, 0:512], ps_c[:, 0:512], OP.add)
            TT(ht_t[:, 512:1024], ht_t[:, 512:1024], ps_d[:, 0:512], OP.add)
            DR("dve")
            dht = kb.q["dve"].count
            Wt("act", "dve", dht)
            ah = ACTV(g_t, ht_t, AF.Tanh)
            DR("act")
            ah = ACTV(g_t, g_t, AF.Silu)
            Wt("dve", "act", ah)
            TT(ht_t[:, 0:512], ps_a[:, 0:512], g_t[:, 0:512])
            TT(ht_t[:, 512:1024], ps_b[:, 0:512], g_t[:, 512:1024])
            DR("dve")
            Wt("dve", "sync", kb.q["sync"].count)
            dout_ = TT(o_t, xt_t, ht_t, OP.add)
            Wt("sync", "dve", dout_)
            dfin = DMA("sync", out[tt * 128:(tt + 1) * 128, :], o_t)
        return _finish()

    return nc


_NC_CACHE = [None]


def kernel(**inputs):
    if _NC_CACHE[0] is None:
        _NC_CACHE[0] = build_program()
    nc = _NC_CACHE[0]
    in_maps = []
    for core in range(B):
        g = _host_arrays(inputs, core)
        bcH = np.zeros((32, NG * 128), dtype=np.float32)
        for gi in range(NG):
            for hs in range(4):
                bcH[4 * gi + hs, gi * 128 + hs * 32: gi * 128 + (hs + 1) * 32] = 1.0
        g["bcH"] = bcH
        in_maps.append(g)
    res = run_bass_kernel_spmd(nc, in_maps, core_ids=list(range(B)))
    return np.stack([res.results[c]["out"] for c in range(B)], axis=0)


# version 11
# speedup vs baseline: 1.0185x; 1.0185x over previous
import os
os.environ.setdefault("BASS_INTERP_NUM_WORKERS", "1")
import contextlib
import numpy as np

import concourse.bass as bass
import concourse.mybir as mybir
from concourse.bass_utils import run_bass_kernel_spmd

F32 = mybir.dt.float32
F32R = mybir.dt.float32r
BF16 = mybir.dt.bfloat16
AF = mybir.ActivationFunctionType
OP = mybir.AluOpType

B = 8; T = 2048; D = 1024; DI = 2048; H = 32; DH = 64
NG = 8; L = 256; NCH = T // L; BL = 16; NBL = L // BL
NIC = DI // 128
ROWS_F = 256
DT_MIN, DT_MAX, K_MAX, EPS = 1e-4, 0.1, 0.5, 1e-8
LN_EPS = 1e-5
EPAD = 8; EW = EPAD + NBL


def _host_arrays(inputs, core):
    W_in = inputs["L0_attn_W_in"]
    perm = np.arange(4160)
    perm[2 * DI:2 * DI + H] = 2 * DI + 2 * np.arange(H)
    perm[2 * DI + H:2 * DI + 2 * H] = 2 * DI + 2 * np.arange(H) + 1
    W_inT = np.ascontiguousarray(W_in[perm].T)

    W_bc = inputs["L0_attn_W_bc"]
    o = np.arange(4096)
    hh, qq, nn = o // 128, (o % 128) // 32, o % 32
    pos = (qq * NG + hh // 4) * 128 + (hh % 4) * 32 + nn
    new = np.zeros(4096, dtype=np.int64); new[pos] = o
    W_bcT = np.ascontiguousarray(W_bc[new].T)

    g = {
        "x_ssm": np.ascontiguousarray(inputs["L0_attn_x"][core]),
        "x_ffn": np.ascontiguousarray(
            inputs["L0_attn_x"][:, core * ROWS_F:(core + 1) * ROWS_F, :].reshape(T, D)),
        "W_inT": W_inT,
        "W_bcT": W_bcT,
        "W_outT": np.ascontiguousarray(inputs["L0_attn_W_out"].T),
        "dwW": np.ascontiguousarray(
            inputs["L0_attn_dw_weight"].reshape(NIC, 128, 4).transpose(1, 0, 2).reshape(128, NIC * 4)),
        "dwB": np.ascontiguousarray(inputs["L0_attn_dw_bias"].reshape(NIC, 128).T),
        "skipc": np.ascontiguousarray(inputs["L0_attn_skip"].reshape(NIC, 128).T),
        "A_log": np.ascontiguousarray(inputs["L0_attn_A_log"].reshape(NG, 128).T),
        "dt_bias": np.ascontiguousarray(inputs["L0_attn_dt_bias"].reshape(H, 1)),
        "bo": inputs["L0_ffn_bo"].reshape(1, D),
        "in_c": np.ascontiguousarray(inputs["L0_ffn_in_c"].reshape(8, 128).T),
    }
    for nm in ("wf", "wi", "wc", "uf", "ui", "uc", "wo", "uo"):
        g[nm + "T"] = np.ascontiguousarray(inputs["L0_ffn_" + nm].T)
    for nm in ("bf", "bi", "bc"):
        g["b" + nm[1]] = np.ascontiguousarray(inputs["L0_ffn_" + nm].reshape(8, 128).T)
    g["ident"] = np.eye(128, dtype=np.float32)
    g["trilm"] = np.triu(np.ones((128, 128), dtype=np.float32))
    sel4 = np.zeros((128, 4), dtype=np.float32)
    for hs in range(4):
        sel4[hs * 32:(hs + 1) * 32, hs] = 1.0
    g["sel4"] = sel4
    g["sel4T"] = np.ascontiguousarray(sel4.T)
    g["ones1"] = np.ones((1, 128), dtype=np.float32)
    g["onescol"] = np.ones((128, 1), dtype=np.float32)
    return g


class _Q:
    def __init__(self, step):
        self.ops = []; self.count = 0; self.step = step; self.sem = None


class KB:
    def __init__(self):
        self.q = {n: _Q(16 if n in ("sync", "gpd") else 1)
                  for n in ("sync", "gpd", "pe", "dve", "act", "gp")}

    def on(self, qn, fn):
        eq = self.q[qn]
        if eq.step == 16 and eq.count > 0:
            eq.ops.append(lambda e, s=eq, v=eq.count: e.wait_ge(s.sem, v))
        eq.ops.append(lambda e, f=fn, eq=eq: f(e).then_inc(eq.sem, eq.step))
        eq.count += eq.step
        return eq.count

    def drain(self, qn):
        self.q[qn].ops.append(lambda e: e.drain())

    def wait(self, qn, prod, val):
        if val > 0:
            self.q[qn].ops.append(lambda e, s=self.q[prod], v=val: e.wait_ge(s.sem, v))

    def wait_sem(self, qn, sem, val):
        self.q[qn].ops.append(lambda e, s=sem, v=val: e.wait_ge(s, v))


def build_program():
    PH = int(os.environ.get("KPHASE", "9"))
    nc = bass.Bass("TRN2", target_bir_lowering=False)

    def din(name, shape, dtype=F32):
        return nc.dram_tensor(name, list(shape), dtype, kind="ExternalInput").ap()

    x_ssm = din("x_ssm", (T, D)); x_ffn = din("x_ffn", (T, D))
    W_inT = din("W_inT", (D, 4160)); W_bcT = din("W_bcT", (DI, 4096))
    W_outT = din("W_outT", (DI, D))
    wT = {nm: din(nm + "T", (D, D)) for nm in ("wf", "wi", "wc", "uf", "ui", "uc", "wo", "uo")}
    dwW = din("dwW", (128, NIC * 4)); dwB = din("dwB", (128, NIC))
    skipc = din("skipc", (128, NIC))
    A_log = din("A_log", (128, NG)); dt_bias = din("dt_bias", (H, 1))
    b_in = {nm: din("b" + nm, (128, 8)) for nm in ("f", "i", "c")}
    bo = din("bo", (1, D)); in_c = din("in_c", (128, 8))
    ident_in = din("ident", (128, 128)); tril_in = din("trilm", (128, 128))
    sel4_in = din("sel4", (128, 4)); sel4T_in = din("sel4T", (4, 128))
    ones1_in = din("ones1", (1, 128))
    onescol_in = din("onescol", (128, 1))

    out = nc.dram_tensor("out", [T, D], F32, kind="ExternalOutput").ap()
    cc_in = nc.dram_tensor("cc_in", [ROWS_F, D], F32).ap()
    cc_out = nc.dram_tensor("cc_out", [T, D], F32).ap()
    d_xn = nc.dram_tensor("d_xn", [128, 8 * T], BF16).ap()
    d_xnf = nc.dram_tensor("d_xnf", [128, 8 * T], BF16).ap()
    d_gate = nc.dram_tensor("d_gate", [128, NIC * T], F32).ap()
    d_vr = nc.dram_tensor("d_vr", [128, NIC * T], F32).ap()
    d_val = nc.dram_tensor("d_val", [128, NIC * T], F32).ap()
    d_par = nc.dram_tensor("d_par", [64, T], F32).ap()
    d_bc = nc.dram_tensor("d_bc", [4, 128, NG * T], F32).ap()
    d_yg = nc.dram_tensor("d_yg", [128, NIC * T], BF16).ap()
    xw_d = nc.dram_tensor("xw_d", [3, 128, 8 * T], BF16).ap()

    es = contextlib.ExitStack()
    with es:
        def sb(name, p, fr, dtype=F32):
            return es.enter_context(nc.sbuf_tensor(name, [p, fr], dtype))

        identf = sb("identf", 128, 128)
        identr = sb("identr", 128, 128, F32R)
        identb = sb("identb", 128, 128, BF16)
        trils = sb("trils", 128, 128)
        sel4s = sb("sel4s", 128, 4); sel4Ts = sb("sel4Ts", 4, 128)
        ones1s = sb("ones1s", 1, 128)
        onescol = sb("onescol_s", 128, 1)
        zc = sb("zc", 128, 1)
        dwWs = sb("dwWs", 128, NIC * 4); dwBs = sb("dwBs", 128, NIC)
        skips = sb("skips", 128, NIC)
        A_s = sb("A_s", 128, NG); Aneg = sb("Aneg", 128, NG)
        dtb_s = sb("dtb_s", H, 1)
        bias_s = {nm: sb("bias" + nm, 128, 8) for nm in ("f", "i", "c")}
        bo_r = sb("bo_r", 1, D); bo_b = sb("bo_b", 128, D)
        inc_s = sb("inc_s", 128, 8)
        dtsl = sb("dtsl", 32, L); kdsl = sb("kdsl", 32, L)
        dt2sl = sb("dt2sl", 32, L); dk1sl = sb("dk1sl", 32, L)
        parsl = sb("parsl", 64, L)
        rbsl = sb("rbsl", 128, NG * NBL)
        PL4 = sb("PL4", 32, 4 * H)
        PLt = {c2: sb("PLt" + c2, 128, NG) for c2 in ("11", "12", "21", "22")}
        E1 = {c2: sb("E1" + c2, 128, NG * EW) for c2 in ("11", "12", "21", "22")}
        E2 = {c2: sb("E2" + c2, 128, NG * EW) for c2 in ("11", "12", "21", "22")}
        BE = {c2: sb("BE" + c2, 128, NG * NBL) for c2 in ("11", "12", "21", "22")}
        idE = sb("idE", 128, NG * NBL)
        xln = sb("xln", 128, D)
        xln2 = sb("xln2", 128, D)
        xnbf = sb("xnbf", 128, D, BF16)
        ygbf = sb("ygbf", 128, L, BF16)
        stat = sb("stat", 128, 4)
        sqt = sb("sqt", 128, 512)
        fss = sb("fss", 4, 512); fsb = sb("fsb", 128, 512)
        Gsb = sb("Gsb", 128, L)
        KTt = sb("KTt", 128, DH)
        ysb = sb("ysb", 128, L)
        winB = sb("winB", 128, 8 * 128, BF16)
        wrowB = sb("wrowB", 128, D, BF16)
        wrow2 = sb("wrow2", 128, D, BF16)
        xwr = sb("xwr", 128, 8 * ROWS_F, BF16)
        vring = sb("vring", 128, L)
        gring = sb("gring", 128, L)

        AW = 36864
        BIG_OFF = 81856
        bigh = nc.alloc_sbuf_tensor_at("big", [128, AW], F32, offset=BIG_OFF)
        big = bigh.ap()

        def BG(off, w):
            assert off + w <= AW, (off, w)
            return big[:, off:off + w]

        def BGb(off, wbf):
            assert off + wbf // 2 <= AW
            return big[:, off:off + wbf // 2].bitcast(BF16)

        def BGr(off, w):
            return big[:, off:off + w].bitcast(F32R)

        SC_bz, SC_bx, SC_cz, SC_cx = 0, 2048, 4096, 6144
        SC_dtc, SC_S, SC_Sdt, SC_m12, SC_m22 = 8192, 10240, 12288, 14336, 16384
        SC_p11, SC_p12, SC_p21, SC_p22 = 18432, 20480, 22528, 24576
        SC_cumI, SC_idPw = 26624, 28672
        SC_qt1, SC_qt2 = 26624, 28672
        SC_tq1, SC_tq2 = 14336, 16384
        SC_U = 8192
        SC_Xz, SC_Xs, SC_Xf = 30720, 32768, 34816

        Xz = big[0:32, SC_Xz:SC_Xz + 2048]
        Xs_ = big[0:32, SC_Xs:SC_Xs + 2048]
        Xfull = nc.alloc_sbuf_tensor_at(
            "Xfull", [64, 2048], F32, offset=BIG_OFF + SC_Xf * 4).ap()
        ktuz = big[0:32, SC_bz:SC_bz + 2048]
        ktus = big[0:32, SC_bx:SC_bx + 2048]
        tmpX = big[0:32, SC_tq1:SC_tq1 + 2048]
        Qst = nc.alloc_sbuf_tensor_at(
            "Qst", [64, 4096], F32, offset=BIG_OFF + SC_p11 * 4).ap()
        Kst = nc.alloc_sbuf_tensor_at(
            "Kst", [64, 4096], F32, offset=BIG_OFF + SC_p21 * 4).ap()
        Ust = nc.alloc_sbuf_tensor_at(
            "Ust", [128, 4096], F32, offset=BIG_OFF + SC_U * 4).ap()
        xwr3 = BGb(20480, 6144)
        cst = BG(23552, 2048)
        cbf = BGb(25600, 4096)
        cbf2 = BGb(27648, 4096)
        gateT = BG(29696, 768)

        ps_a = es.enter_context(nc.psum_tensor("ps_a", [128, 512], F32))
        ps_b = es.enter_context(nc.psum_tensor("ps_b", [128, 512], F32))
        ps_c = es.enter_context(nc.psum_tensor("ps_c", [128, 512], F32))
        ps_d = es.enter_context(nc.psum_tensor("ps_d", [128, 512], F32))
        ps_G = es.enter_context(nc.psum_tensor("ps_G", [128, 512], F32))
        ps_y = es.enter_context(nc.psum_tensor("ps_y", [64, 512], F32))
        ps_k0 = es.enter_context(nc.psum_tensor("ps_k0", [32, 512], F32))
        ps_k1 = es.enter_context(nc.psum_tensor("ps_k1", [32, 512], F32))

        kb = KB()
        sems = {}
        for nm in ("sync", "gpd", "pe", "dve", "act", "gp", "cc"):
            sems[nm] = es.enter_context(nc.semaphore(nm))
        for nm in ("sync", "gpd", "pe", "dve", "act", "gp"):
            kb.q[nm].sem = sems[nm]

        blkctx = es.enter_context(nc.Block())

        def DMA(qn, dst, src):
            return kb.on(qn, lambda e: e.dma_start(out=dst, in_=src))

        def MM(dst, lhsT, rhs, start=True, stop=True):
            return kb.on("pe", lambda e: e.matmul(dst, lhsT, rhs, start=start, stop=stop))

        def TRN(dst, src, ident):
            return kb.on("pe", lambda e: e.transpose(dst, src, ident))

        def TT(dst, a, b, op=OP.mult, q="dve"):
            return kb.on(q, lambda e: e.tensor_tensor(dst, a, b, op))

        def TS1(dst, a, s1, op=OP.mult, q="dve"):
            return kb.on(q, lambda e: e.tensor_single_scalar(dst, a, s1, op))

        def TS2(dst, a, s1, s2, op0, op1, q="dve"):
            return kb.on(q, lambda e: e.tensor_scalar(dst, a, s1, s2, op0, op1))

        def STT(dst, in0, scalar, in1, op0=OP.mult, op1=OP.add, q="dve"):
            return kb.on(q, lambda e: e.scalar_tensor_tensor(dst, in0, scalar, in1, op0, op1))

        def CPY(dst, src, q="dve"):
            return kb.on(q, lambda e: e.tensor_copy(dst, src))

        def RCP(dst, src):
            return kb.on("dve", lambda e: e.reciprocal(dst, src))

        def ACTV(dst, src, func, bias=0.0, scale=1.0):
            return kb.on("act", lambda e: e.activation(dst, src, func, bias=bias, scale=scale))

        def SCANOP(dst, d0, d1, init):
            return kb.on("dve", lambda e: e.tensor_tensor_scan(dst, d0, d1, init, OP.mult, OP.add))

        def MS(qn, ap, v):
            return kb.on(qn, lambda e: e.memset(ap, v))

        def FILL(ap, v):
            src_col = onescol if v == 1.0 else zc
            shp = list(ap.shape)
            bc_ap = src_col[0:shp[0], 0:1]
            for d in shp[2:]:
                bc_ap = bc_ap.unsqueeze(len(bc_ap.shape))
            return kb.on("dve", lambda e: e.tensor_copy(ap, bc_ap.broadcast_to(shp)))

        def RS(dst, src):
            return kb.on("dve", lambda e: e.reduce_sum(dst, src, axis=mybir.AxisListType.X))

        Wt = kb.wait; WS = kb.wait_sem; DR = kb.drain

        def bar(qn):
            for n in kb.q:
                if n != qn and kb.q[n].count > 0:
                    Wt(qn, n, kb.q[n].count)

        for dst, src in ((trils[:], tril_in), (sel4s[:], sel4_in), (sel4Ts[:], sel4T_in),
                         (dwWs[:], dwW), (dwBs[:], dwB), (skips[:], skipc),
                         (A_s[:], A_log), (dtb_s[:], dt_bias), (bo_r[:], bo),
                         (inc_s[:], in_c), (identf[:], ident_in), (ones1s[:], ones1_in),
                         (onescol[:], onescol_in)):
            DMA("sync", dst, src)
        for nm in ("f", "i", "c"):
            ld_c = DMA("sync", bias_s[nm][:], b_in[nm])

        Wt("act", "sync", ld_c)
        aA = ACTV(A_s[:], A_s[:], AF.Exp)
        Wt("dve", "act", aA)
        CPY(identb[:], identf[:])
        CPY(identr[:], identf[:])
        DR("dve")
        TS1(Aneg[:], A_s[:], -1.0)
        if os.environ.get("SKIPBO", "") != "1":
            Wt("pe", "sync", ld_c)
            MM(ps_a[:, 0:512], ones1s[:], bo_r[:, 0:512])
            m2 = MM(ps_b[:, 0:512], ones1s[:], bo_r[:, 512:1024])
            Wt("dve", "pe", m2)
            CPY(bo_b[:, 0:512], ps_a[:, 0:512])
            CPY(bo_b[:, 512:1024], ps_b[:, 0:512])
        SKIPX = os.environ.get("SKIPX", "") == "1"
        for Ex in (() if SKIPX else (E1, E2)):
            MS("dve", Ex["11"][:], 0.0); MS("dve", Ex["12"][:], 0.0)
            MS("dve", Ex["21"][:], 0.0); MS("dve", Ex["22"][:], 0.0)
        DR("dve")
        for Ex in (() if SKIPX else (E1, E2)):
            for gi in range(NG):
                MS("dve", Ex["11"][:, gi * EW: gi * EW + EPAD], 1.0)
                MS("dve", Ex["22"][:, gi * EW: gi * EW + EPAD], 1.0)
        if not SKIPX:
            MS("dve", Xz, 0.0)
            MS("dve", Xs_, 0.0)
            MS("dve", zc[:], 0.0)
            DR("dve")
            CPY(Xfull[0:32, :], Xz)
            CPY(Xfull[32:64, :], Xs_)
            DR("dve")

        def _finish():
            kb.q["sync"].ops.append(
                lambda e, s=kb.q["sync"]: e.wait_ge(s.sem, s.count))

            @blkctx.sync
            def _(e):
                for op in kb.q["sync"].ops:
                    op(e)

            @blkctx.tensor
            def _(e):
                for op in kb.q["pe"].ops:
                    op(e)

            @blkctx.vector
            def _(e):
                for op in kb.q["dve"].ops:
                    op(e)

            @blkctx.scalar
            def _(e):
                for op in kb.q["act"].ops:
                    op(e)

            @blkctx.gpsimd
            def _(e):
                for op in kb.q["gpd"].ops:
                    op(e)
            return nc


        LNCUT = int(os.environ.get("LNCUT", "9"))

        def ln_tile(src_hbm, tt, dst_dram):
            lx = DMA("sync", xln[:], src_hbm[tt * 128:(tt + 1) * 128, :])
            Wt("dve", "sync", lx)
            bar("dve") if False else None
            DR("dve")
            RS(stat[:, 0:1].unsqueeze(2), xln[:].unsqueeze(1))
            if LNCUT <= 1:
                return lx
            Wt("act", "sync", lx)
            Wt("act", "dve", kb.q["dve"].count)
            a1 = ACTV(sqt[:, 0:512], xln[:, 0:512], AF.Square)
            Wt("dve", "act", a1)
            DR("dve")
            RS(stat[:, 1:2].unsqueeze(2), sqt[:, 0:512].unsqueeze(1))
            d0 = kb.q["dve"].count
            Wt("act", "dve", d0)
            a2 = ACTV(sqt[:, 0:512], xln[:, 512:1024], AF.Square)
            Wt("dve", "act", a2)
            DR("dve")
            RS(stat[:, 2:3].unsqueeze(2), sqt[:, 0:512].unsqueeze(1))
            DR("dve")
            TT(stat[:, 1:2], stat[:, 1:2], stat[:, 2:3], OP.add)
            DR("dve")
            TS1(stat[:, 0:1], stat[:, 0:1], 1.0 / D)
            TS1(stat[:, 1:2], stat[:, 1:2], 1.0 / D)
            DR("dve")
            STT(stat[:, 2:3], stat[:, 0:1], stat[:, 0:1], stat[:, 1:2],
                OP.mult, OP.subtract)
            DR("dve")
            d1 = TS2(stat[:, 2:3], stat[:, 2:3], -1.0, LN_EPS, OP.mult, OP.add)
            Wt("act", "dve", d1)
            ar = ACTV(stat[:, 3:4], stat[:, 2:3], AF.Sqrt)
            Wt("dve", "act", ar)
            RCP(stat[:, 3:4], stat[:, 3:4])
            DR("dve")
            TT(stat[:, 2:3], stat[:, 0:1], stat[:, 3:4], OP.mult)
            DR("dve")
            d2 = TS1(stat[:, 2:3], stat[:, 2:3], -1.0)
            if LNCUT <= 2:
                return lx
            Wt("act", "dve", d2)
            a3 = ACTV(xln2[:], xln[:], AF.Identity, bias=stat[:, 2:3], scale=stat[:, 3:4])
            if LNCUT <= 3:
                return lx
            for dc in range(8):
                Wt("pe", "act", kb.q["act"].count)
                ptr = TRN(ps_d[:, 0:128], xln2[:, dc * 128:(dc + 1) * 128], identf[:])
                if LNCUT <= 4:
                    continue
                Wt("act", "pe", ptr)
                Wt("act", "sync", kb.q["sync"].count)
                dcc2 = ACTV(ygbf[:, 0:128], ps_d[:, 0:128], AF.Identity)
                if LNCUT <= 5:
                    continue
                Wt("sync", "act", dcc2)
                dl = DMA("sync", dst_dram[:, dc * T + tt * 128: dc * T + tt * 128 + 128],
                         ygbf[:, 0:128])
            if LNCUT <= 5:
                return lx
            return dl

        if PH <= -2:
            return _finish()
        for tt in range(16):
            ln_tile(x_ffn, tt, d_xnf)
            if PH <= -1:
                return _finish()
        for tt in range(16):
            ld_xn = ln_tile(x_ssm, tt, d_xn)

        if PH <= 0:
            return _finish()

        bcH_in = din("bcH", (32, NG * 128))
        bcHs = sb("bcHs", 32, NG * 128)
        idEn = sb("idEn", 128, NG * NBL)
        ldh = DMA("sync", bcHs[:], bcH_in)

        xnfb = BGb(0, 16384)
        WS("gpd", sems["sync"], kb.q["sync"].count)
        DMA("gpd", xnfb[:, 0:8192], d_xnf[:, 0:8192])
        ldf = DMA("gpd", xnfb[:, 8192:16384], d_xnf[:, 8192:16384])
        FFu = {"uf": 8192, "ui": 12288, "uc": 16384}
        for nm in ("uf", "ui", "uc"):
            for kc in range(8):
                ldf = DMA("gpd", BGb(FFu[nm] + kc * 512, 2048)[:, 0:1024],
                          wT[nm][kc * 128:(kc + 1) * 128, :])

        PSL = [ps_a, ps_b, ps_c, ps_d]
        for gidx, wnm in enumerate(("wf", "wi", "wc")):
            for rs in range(4):
                for mh in range(2):
                    for kc in range(8):
                        WS("gpd", sems["pe"], kb.q["pe"].count)
                        lw = DMA("gpd", wrowB[:, 0:512],
                                 wT[wnm][kc * 128:(kc + 1) * 128, mh * 512:(mh + 1) * 512])
                        WS("pe", sems["gpd"], lw)
                        if kc == 0:
                            Wt("pe", "act", kb.q["act"].count)
                        for mq in range(4):
                            mmx = MM(PSL[mq][:, 0:512],
                                     wrowB[:, mq * 128:(mq + 1) * 128],
                                     xnfb[:, kc * 2048 + rs * 512: kc * 2048 + (rs + 1) * 512],
                                     start=(kc == 0), stop=(kc == 7))
                    Wt("act", "pe", mmx)
                    for mq in range(4):
                        mc = mh * 4 + mq
                        Wt("act", "sync", kb.q["sync"].count)
                        av = ACTV(xnbf[:, 0:512], PSL[mq][:, 0:512], AF.Identity)
                        Wt("sync", "act", av)
                        DMA("sync", xw_d[gidx][:, mc * T + rs * 512: mc * T + (rs + 1) * 512],
                            xnbf[:, 0:512])

        Wt("dve", "sync", ld_c)
        for mc in range(8):
            CPY(cst[:, mc * 256:(mc + 1) * 256],
                inc_s[:, mc:mc + 1].broadcast_to([128, 256]))
        DR("dve")
        for mc in range(8):
            d_cbf = CPY(cbf[:, mc * 256:(mc + 1) * 256], cst[:, mc * 256:(mc + 1) * 256])
        WS("pe", sems["gpd"], ldf)

        for step in range(8):
            csrc = cbf if step % 2 == 0 else cbf2
            cdst = cbf2 if step % 2 == 0 else cbf
            Wt("sync", "dve", kb.q["dve"].count)
            for gidx in range(3):
                lxw = DMA("sync", xwr3[:, gidx * 2048:(gidx + 1) * 2048].rearrange(
                              "p (mc t) -> p mc t", mc=8),
                          xw_d[gidx].rearrange("p (mc t) -> p mc t", mc=8)[:, :, step * 256:(step + 1) * 256])
            WS("pe", sems["sync"], lxw)
            Wt("pe", "dve", d_cbf)
            for mc in range(8):
                Wt("pe", "dve", kb.q["dve"].count)
                for gidx in range(3):
                    ps = (ps_a, ps_b, ps_c)[gidx]
                    unm = ("uf", "ui", "uc")[gidx]
                    for kc in range(8):
                        mmu = MM(ps[:, 0:256],
                                 BGb(FFu[unm] + kc * 512, 2048)[:, mc * 128:(mc + 1) * 128],
                                 csrc[:, kc * 256:(kc + 1) * 256],
                                 start=(kc == 0), stop=(kc == 7))
                Wt("dve", "pe", mmu)
                for gidx in range(3):
                    ps = (ps_a, ps_b, ps_c)[gidx]
                    TT(sqt[:, 0:256], ps[:, 0:256],
                       xwr3[:, gidx * 2048 + mc * 256: gidx * 2048 + (mc + 1) * 256], OP.add)
                    DR("dve")
                    dtx = kb.q["dve"].count
                    Wt("act", "dve", dtx)
                    fn = (AF.Sigmoid, AF.Sigmoid, AF.Tanh)[gidx]
                    ag = ACTV(gateT[:, gidx * 256:(gidx + 1) * 256], sqt[:, 0:256],
                              fn, bias=bias_s[("f", "i", "c")[gidx]][:, mc:mc + 1])
                    Wt("dve", "act", ag)
                TT(cst[:, mc * 256:(mc + 1) * 256], gateT[:, 0:256],
                   cst[:, mc * 256:(mc + 1) * 256])
                TT(gateT[:, 256:512], gateT[:, 256:512], gateT[:, 512:768])
                DR("dve")
                TT(cst[:, mc * 256:(mc + 1) * 256], cst[:, mc * 256:(mc + 1) * 256],
                   gateT[:, 256:512], OP.add)
                DR("dve")
                d_cbf = CPY(cdst[:, mc * 256:(mc + 1) * 256], cst[:, mc * 256:(mc + 1) * 256])

        cfin = cbf2 if 8 % 2 == 1 else cbf
        for rc in range(2):
            for oc in range(2):
                ps = (ps_a, ps_b)[oc]
                for kc in range(8):
                    WS("gpd", sems["pe"], kb.q["pe"].count)
                    lw = DMA("gpd", wrowB[:], wT["uo"][kc * 128:(kc + 1) * 128, :])
                    WS("pe", sems["gpd"], lw)
                    if kc == 0:
                        Wt("pe", "dve", kb.q["dve"].count)
                    mmu = MM(ps[:, 0:512],
                             cfin[:, kc * 256 + rc * 128: kc * 256 + rc * 128 + 128],
                             wrowB[:, oc * 512:(oc + 1) * 512],
                             start=(kc == 0), stop=(kc == 7))
                Wt("dve", "pe", mmu)
                Wt("dve", "sync", kb.q["sync"].count)
                duo = CPY(sqt[:, 0:512], ps[:, 0:512])
                Wt("sync", "dve", duo)
                dcc = DMA("sync", cc_in[rc * 128:(rc + 1) * 128, oc * 512:(oc + 1) * 512],
                          sqt[:, 0:512])
        WS("gpd", sems["sync"], dcc)
        kb.q["gpd"].ops.append(
            lambda e: e.collective_compute(
                "AllGather", OP.bypass, replica_groups=[list(range(8))],
                ins=[cc_in], outs=[cc_out]).then_inc(sems["cc"], 1))

        if PH <= 1:
            return _finish()

        xnb = BGb(0, 16384)
        WS("gpd", sems["pe"], kb.q["pe"].count)
        DMA("gpd", xnb[:, 0:8192], d_xn[:, 0:8192])
        ldx = DMA("gpd", xnb[:, 8192:16384], d_xn[:, 8192:16384])
        WS("pe", sems["gpd"], ldx)
        winB2 = BGb(8192, 1024)
        pe_done = [0, 0]
        lw_par = [0, 0]
        for oc in range(33):
            owid = 128 if oc < 32 else 64
            wbuf = (winB, winB2)[oc % 2]
            WS("gpd", sems["pe"], pe_done[oc % 2])
            for kc in range(8):
                lw = DMA("gpd", wbuf[:, kc * 128: kc * 128 + owid],
                         W_inT[kc * 128:(kc + 1) * 128, oc * 128: oc * 128 + owid])
            lw_par[oc % 2] = lw
            for tch in range(4):
                WS("pe", sems["gpd"], lw_par[oc % 2])
                Wt("pe", "act", kb.q["act"].count)
                Wt("pe", "dve", kb.q["dve"].count)
                for kc in range(8):
                    mmp = MM(ps_d[0:owid, 0:512], wbuf[:, kc * 128: kc * 128 + owid],
                             xnb[:, kc * 2048 + tch * 512: kc * 2048 + (tch + 1) * 512],
                             start=(kc == 0), stop=(kc == 7))
                if oc < 16:
                    Wt("act", "pe", mmp)
                    Wt("act", "sync", kb.q["sync"].count)
                    av = ACTV(sqt[:, 0:512], ps_d[:, 0:512], AF.Silu)
                    Wt("sync", "act", av)
                    DMA("sync", d_gate[:, oc * T + tch * 512: oc * T + (tch + 1) * 512],
                        sqt[:, 0:512])
                elif oc < 32:
                    Wt("act", "pe", mmp)
                    Wt("act", "sync", kb.q["sync"].count)
                    av = ACTV(sqt[:, 0:512], ps_d[:, 0:512], AF.Identity)
                    Wt("sync", "act", av)
                    DMA("sync", d_vr[:, (oc - 16) * T + tch * 512: (oc - 16) * T + (tch + 1) * 512],
                        sqt[:, 0:512])
                else:
                    Wt("act", "pe", mmp)
                    Wt("act", "sync", kb.q["sync"].count)
                    av = ACTV(sqt[0:64, 0:512], ps_d[0:64, 0:512], AF.Identity)
                    Wt("sync", "act", av)
                    DMA("sync", d_par[:, tch * 512:(tch + 1) * 512], sqt[0:64, 0:512])
            pe_done[oc % 2] = kb.q["pe"].count

        if PH <= 2:
            return _finish()

        vrb = BG(0, 2048)
        acc = BG(2048, 2048)
        vout = BGr(4096, 2048)
        WS("gpd", sems["pe"], kb.q["pe"].count)
        for ic in range(NIC):
            WS("gpd", sems["sync"], kb.q["sync"].count)
            WS("gpd", sems["dve"], kb.q["dve"].count)
            WS("gpd", sems["act"], kb.q["act"].count)
            lv = DMA("gpd", vrb, d_vr[:, ic * T:(ic + 1) * T])
            WS("dve", sems["gpd"], lv)
            TS2(acc, vrb, dwWs[:, ic * 4 + 3:ic * 4 + 4], dwBs[:, ic:ic + 1],
                OP.mult, OP.add)
            DR("dve")
            for kk in (2, 1, 0):
                sh = 3 - kk
                STT(acc[:, sh:T], vrb[:, 0:T - sh],
                    dwWs[:, ic * 4 + kk:ic * 4 + kk + 1], acc[:, sh:T],
                    OP.mult, OP.add)
                DR("dve")
            dcv = kb.q["dve"].count
            Wt("act", "dve", dcv)
            av = ACTV(vout, acc, AF.Silu)
            Wt("sync", "act", av)
            dv = DMA("sync", d_val[:, ic * T:(ic + 1) * T], vout.bitcast(F32))

        if PH <= 3:
            return _finish()

        valb = BGb(0, 32768)
        bstg = BG(16384, 512)
        WS("gpd", sems["sync"], dv)
        WS("gpd", sems["dve"], kb.q["dve"].count)
        ldv = DMA("gpd", valb[:, 0:8192], d_val[:, 0:8192])
        ldv = DMA("gpd", valb[:, 8192:16384], d_val[:, 8192:16384])
        ldv = DMA("gpd", valb[:, 16384:24576], d_val[:, 16384:24576])
        ldv = DMA("gpd", valb[:, 24576:32768], d_val[:, 24576:32768])
        WS("pe", sems["gpd"], ldv)
        for q4 in range(4):
            for gi in range(NG):
                ocol = q4 * NG + gi
                WS("gpd", sems["pe"], kb.q["pe"].count)
                for kc in range(16):
                    buf, slot = (wrowB, kc) if kc < 8 else (wrow2, kc - 8)
                    lw = DMA("gpd", buf[:, slot * 128:(slot + 1) * 128],
                             W_bcT[kc * 128:(kc + 1) * 128, ocol * 128:(ocol + 1) * 128])
                for tch in range(4):
                    WS("pe", sems["gpd"], lw)
                    Wt("pe", "dve", kb.q["dve"].count)
                    Wt("pe", "act", kb.q["act"].count)
                    for kc in range(16):
                        buf, slot = (wrowB, kc) if kc < 8 else (wrow2, kc - 8)
                        mmb = MM(ps_d[:, 0:512], buf[:, slot * 128:(slot + 1) * 128],
                                 valb[:, kc * 2048 + tch * 512: kc * 2048 + (tch + 1) * 512],
                                 start=(kc == 0), stop=(kc == 15))
                    Wt("act", "pe", mmb)
                    asq = ACTV(sqt[:, 0:512], ps_d[:, 0:512], AF.Square)
                    Wt("pe", "act", asq)
                    mss = MM(ps_G[0:4, 0:512], sel4s[:], sqt[:, 0:512])
                    Wt("act", "pe", mss)
                    asr = ACTV(fss[:, 0:512], ps_G[0:4, 0:512], AF.Sqrt)
                    Wt("dve", "act", asr)
                    TS1(fss[:, 0:512], fss[:, 0:512], EPS, OP.add)
                    DR("dve")
                    dfr = RCP(fss[:, 0:512], fss[:, 0:512])
                    Wt("pe", "dve", dfr)
                    mbb = MM(ps_G[:, 0:512], sel4Ts[:], fss[:, 0:512])
                    Wt("dve", "pe", mbb)
                    dcp = CPY(fsb[:, 0:512], ps_G[:, 0:512])
                    DR("dve")
                    Wt("dve", "sync", kb.q["sync"].count)
                    dtt = TT(bstg, ps_d[:, 0:512], fsb[:, 0:512])
                    Wt("sync", "dve", dtt)
                    dbc = DMA("sync", d_bc[q4][:, gi * T + tch * 512: gi * T + (tch + 1) * 512],
                              bstg)

        if PH <= 4:
            return _finish()

        def cgAP(off):
            return BG(off, 2048)

        def cgrAP(off):
            return BGr(off, 2048)

        def Ab(col):
            return col.unsqueeze(2).broadcast_to([128, NG, 256])

        def cg3(off):
            return BG(off, 2048).rearrange("p (g t) -> p g t", g=NG)

        def Edat(Ex, c2, sh=0):
            ap = Ex[c2][:].rearrange("p (g s) -> p g s", g=NG)
            return ap[:, :, EPAD - sh: EPAD - sh + NBL]

        def Ebcast(Ex, c2):
            ap = Ex[c2][:].rearrange("p (g s) -> p g s", g=NG)
            return ap[:, :, EPAD:EPAD + NBL].unsqueeze(3).broadcast_to([128, NG, NBL, BL])

        idE3 = idE[:].rearrange("p (g b) -> p g b", g=NG)

        CHCUT = int(os.environ.get("CHCUT", "99"))
        for ch in range(NCH):
            t0 = ch * L
            WS("gpd", sems["dve"], kb.q["dve"].count)
            WS("gpd", sems["pe"], kb.q["pe"].count)
            for q4, off in enumerate((SC_bz, SC_bx, SC_cz, SC_cx)):
                lbc = DMA("gpd", cg3(off),
                          d_bc[q4].rearrange("p (g t) -> p g t", g=NG)[:, :, t0:t0 + L])
            if CHCUT < 1:
                continue
            Wt("sync", "act", kb.q["act"].count)
            Wt("sync", "dve", kb.q["dve"].count)
            lpar = DMA("sync", parsl[:], d_par[:, t0:t0 + L])
            WS("act", sems["sync"], lpar)
            Wt("act", "dve", kb.q["dve"].count)
            Wt("act", "pe", kb.q["pe"].count)
            ax = ACTV(kdsl[:], parsl[0:32, :], AF.Identity, bias=dtb_s[:])
            aab = ACTV(dtsl[:], parsl[0:32, :], AF.Abs, bias=dtb_s[:])
            DR("act")
            ae = ACTV(dtsl[:], dtsl[:], AF.Exp, scale=-1.0)
            DR("act")
            al = ACTV(dtsl[:], dtsl[:], AF.Ln, bias=1.0)
            Wt("dve", "act", al)
            STT(dtsl[:], kdsl[:], 0.0, dtsl[:], OP.max, OP.add)
            DR("dve")
            ddt = TS2(dtsl[:], dtsl[:], DT_MAX, DT_MIN, OP.min, OP.max)
            Wt("act", "dve", ddt)
            akd = ACTV(kdsl[:], parsl[32:64, :], AF.Sigmoid)
            Wt("dve", "act", akd)
            TS1(kdsl[:], kdsl[:], K_MAX)
            DR("dve")
            TT(dt2sl[:], dtsl[:], dtsl[:])
            TT(dk1sl[:], dtsl[:], kdsl[:])
            DR("dve")
            dgr = TS1(dk1sl[:], dk1sl[:], 1.0, OP.add)
            if CHCUT < 2:
                continue
            Wt("pe", "dve", dgr)
            for gi in range(NG):
                mm1 = MM(ps_a[:, 0:256], bcHs[:, gi * 128:(gi + 1) * 128], dtsl[:])
                mm2 = MM(ps_b[:, 0:256], bcHs[:, gi * 128:(gi + 1) * 128], dk1sl[:])
                mm3 = MM(ps_c[:, 0:256], bcHs[:, gi * 128:(gi + 1) * 128], dt2sl[:])
                Wt("dve", "pe", mm3)
                CPY(cgAP(SC_dtc)[:, gi * 256:(gi + 1) * 256], ps_a[:, 0:256])
                CPY(cgAP(SC_S)[:, gi * 256:(gi + 1) * 256], ps_b[:, 0:256])
                DR("dve")
                dbb = STT(cgAP(SC_S)[:, gi * 256:(gi + 1) * 256], ps_c[:, 0:256],
                          A_s[:, gi:gi + 1], cgAP(SC_S)[:, gi * 256:(gi + 1) * 256],
                          OP.mult, OP.add)
                Wt("pe", "dve", dbb)
            DR("dve")
            if CHCUT < 3:
                continue
            for gi in range(NG):
                SCANOP(cgAP(SC_cumI)[:, gi * 256:(gi + 1) * 256],
                       cgAP(SC_S)[:, gi * 256:(gi + 1) * 256],
                       zc[:].broadcast_to([128, 256]), 1.0)
            DR("dve")
            MS("dve", rbsl[:], 1.0)
            MS("dve", idE[:], 1.0)
            DR("dve")
            cum3 = cg3(SC_cumI)
            RCP(rbsl[:].rearrange("p (g b) -> p g b", g=NG)[:, :, 1:NBL],
                cum3[:, :, BL - 1::BL][:, :, 0:NBL - 1])
            CPY(idE3[:, :, 1:NBL], cum3[:, :, BL - 1::BL][:, :, 0:NBL - 1])
            DR("dve")
            TS1(idEn[:], idE[:], -1.0)
            TT(BG(SC_idPw, 2048).rearrange("p (g b i) -> p g b i", g=NG, i=BL),
               BG(SC_cumI, 2048).rearrange("p (g b i) -> p g b i", g=NG, i=BL),
               rbsl[:].rearrange("p (g b) -> p g b", g=NG).unsqueeze(3).broadcast_to([128, NG, NBL, BL]))
            DR("dve")
            if CHCUT < 4:
                continue
            RCP(cgAP(SC_S), cgAP(SC_S))
            DR("dve")
            TT(cgAP(SC_Sdt), cgAP(SC_S), cgAP(SC_dtc))
            DR("dve")
            WS("dve", sems["gpd"], lbc)
            TT(cgAP(SC_bz), cgAP(SC_Sdt), cgAP(SC_bz))
            DR("dve")
            TT(cgAP(SC_bx), cgAP(SC_bz), cgAP(SC_bx), OP.add)
            DR("dve")
            TT(cgAP(SC_bx), cgAP(SC_dtc), cgAP(SC_bx))
            DR("dve")
            TT(cgAP(SC_bx), cgAP(SC_bx), cgAP(SC_idPw))
            TT(cgAP(SC_bz), cgAP(SC_bz), cgAP(SC_idPw))
            DR("dve")
            TT(cg3(SC_m12), cg3(SC_Sdt), Ab(Aneg[:]))
            TT(cgAP(SC_m22), cgAP(SC_Sdt), cgAP(SC_dtc))
            DR("dve")
            TT(cg3(SC_m22), cg3(SC_m22), Ab(Aneg[:]))
            DR("dve")
            dmb = TS1(cgAP(SC_m22), cgAP(SC_m22), 1.0, OP.add)
            if CHCUT < 5:
                continue
            def st_i(off, i):
                return BG(off, 2048).rearrange("p (g b i) -> p (g b) i", g=NG, i=BL)[:, :, i]
            DR("dve")
            CPY(st_i(SC_p11, 0), st_i(SC_S, 0))
            CPY(st_i(SC_p12, 0), st_i(SC_m12, 0))
            DR("dve")
            CPY(st_i(SC_p21, 0), st_i(SC_Sdt, 0))
            CPY(st_i(SC_p22, 0), st_i(SC_m22, 0))
            DR("dve")
            for i in range(1, BL):
                TT(sqt[:, 0:128], st_i(SC_m12, i), st_i(SC_p21, i - 1))
                TT(st_i(SC_p11, i), st_i(SC_S, i), st_i(SC_p11, i - 1))
                TT(sqt[:, 128:256], st_i(SC_m12, i), st_i(SC_p22, i - 1))
                TT(st_i(SC_p12, i), st_i(SC_S, i), st_i(SC_p12, i - 1))
                DR("dve")
                TT(st_i(SC_p11, i), st_i(SC_p11, i), sqt[:, 0:128], OP.add)
                TT(st_i(SC_p12, i), st_i(SC_p12, i), sqt[:, 128:256], OP.add)
                TT(sqt[:, 256:384], st_i(SC_m22, i), st_i(SC_p21, i - 1))
                TT(st_i(SC_p21, i), st_i(SC_Sdt, i), st_i(SC_p11, i - 1))
                TT(sqt[:, 384:512], st_i(SC_m22, i), st_i(SC_p22, i - 1))
                TT(st_i(SC_p22, i), st_i(SC_Sdt, i), st_i(SC_p12, i - 1))
                DR("dve")
                TT(st_i(SC_p21, i), st_i(SC_p21, i), sqt[:, 256:384], OP.add)
                TT(st_i(SC_p22, i), st_i(SC_p22, i), sqt[:, 384:512], OP.add)
                DR("dve")
            if CHCUT < 6:
                continue
            def bend(off):
                return BG(off, 2048).rearrange("p (g b i) -> p g b i", g=NG, i=BL)[:, :, :, BL - 1]
            for c2, off in (("11", SC_p11), ("12", SC_p12), ("21", SC_p21), ("22", SC_p22)):
                CPY(Edat(E1, c2)[:, :, 1:NBL], bend(off)[:, :, 0:NBL - 1])
            DR("dve")
            MS("dve", Edat(E1, "11")[:, :, 0:1], 1.0)
            MS("dve", Edat(E1, "12")[:, :, 0:1], 0.0)
            MS("dve", Edat(E1, "21")[:, :, 0:1], 0.0)
            MS("dve", Edat(E1, "22")[:, :, 0:1], 1.0)
            DR("dve")
            src_, dst_ = E1, E2
            for sh in (1, 2, 4, 8):
                TT(Edat(dst_, "11"), Edat(src_, "11"), Edat(src_, "11", sh))
                TT(sqt[:, 0:128], Edat(src_, "12"), Edat(src_, "21", sh))
                TT(Edat(dst_, "12"), Edat(src_, "11"), Edat(src_, "12", sh))
                TT(sqt[:, 128:256], Edat(src_, "12"), Edat(src_, "22", sh))
                DR("dve")
                TT(Edat(dst_, "11"), Edat(dst_, "11"), sqt[:, 0:128], OP.add)
                TT(Edat(dst_, "12"), Edat(dst_, "12"), sqt[:, 128:256], OP.add)
                TT(Edat(dst_, "21"), Edat(src_, "21"), Edat(src_, "11", sh))
                TT(sqt[:, 256:384], Edat(src_, "22"), Edat(src_, "21", sh))
                DR("dve")
                TT(Edat(dst_, "21"), Edat(dst_, "21"), sqt[:, 256:384], OP.add)
                TT(Edat(dst_, "22"), Edat(src_, "21"), Edat(src_, "12", sh))
                TT(sqt[:, 384:512], Edat(src_, "22"), Edat(src_, "22", sh))
                DR("dve")
                dE = TT(Edat(dst_, "22"), Edat(dst_, "22"), sqt[:, 384:512], OP.add)
                DR("dve")
                src_, dst_ = dst_, src_
            Ecum = src_
            def Eslot(Ex, c2, b):
                return Ex[c2][:].rearrange("p (g s) -> p g s", g=NG)[:, :, EPAD + b]
            TT(PLt["11"][:], bend(SC_p11)[:, :, NBL - 1], Eslot(Ecum, "11", NBL - 1))
            TT(sqt[:, 0:8], bend(SC_p12)[:, :, NBL - 1], Eslot(Ecum, "21", NBL - 1))
            TT(PLt["12"][:], bend(SC_p11)[:, :, NBL - 1], Eslot(Ecum, "12", NBL - 1))
            TT(sqt[:, 8:16], bend(SC_p12)[:, :, NBL - 1], Eslot(Ecum, "22", NBL - 1))
            TT(PLt["21"][:], bend(SC_p21)[:, :, NBL - 1], Eslot(Ecum, "11", NBL - 1))
            TT(sqt[:, 16:24], bend(SC_p22)[:, :, NBL - 1], Eslot(Ecum, "21", NBL - 1))
            TT(PLt["22"][:], bend(SC_p21)[:, :, NBL - 1], Eslot(Ecum, "12", NBL - 1))
            TT(sqt[:, 24:32], bend(SC_p22)[:, :, NBL - 1], Eslot(Ecum, "22", NBL - 1))
            DR("dve")
            TT(PLt["11"][:], PLt["11"][:], sqt[:, 0:8], OP.add)
            TT(PLt["12"][:], PLt["12"][:], sqt[:, 8:16], OP.add)
            TT(PLt["21"][:], PLt["21"][:], sqt[:, 16:24], OP.add)
            TT(PLt["22"][:], PLt["22"][:], sqt[:, 24:32], OP.add)
            DR("dve")
            for ci, c2 in enumerate(("11", "12", "21", "22")):
                for hs in range(4):
                    CPY(PL4[:, [[1, 1], [4, NG]]].copy() if False else
                        bass.AP(PL4.tensor if False else PL4[:].tensor, 0, [[0, 0]]) if False else
                        PL4[0:32, ci * H + hs: ci * H + hs + 29:4] if False else
                        PL4[0:32, :].rearrange("p (c h) -> p c h", c=4)[:, ci, hs::4],
                        PLt[c2][hs * 32:(hs + 1) * 32, :])
            DR("dve")
            TT(idE3 if False else BE["11"][:].rearrange("p (g b) -> p g b", g=NG),
               Edat(Ecum, "11"), idE3)
            TT(BE["12"][:].rearrange("p (g b) -> p g b", g=NG), Edat(Ecum, "12"),
               idEn[:].rearrange("p (g b) -> p g b", g=NG))
            TT(BE["21"][:].rearrange("p (g b) -> p g b", g=NG), Edat(Ecum, "21"),
               idEn[:].rearrange("p (g b) -> p g b", g=NG))
            dbe = TT(BE["22"][:].rearrange("p (g b) -> p g b", g=NG), Edat(Ecum, "22"), idE3)
            DR("dve")
            if CHCUT < 7:
                continue
            TT(cgAP(SC_qt1), cgAP(SC_p11), cgAP(SC_cz))
            TT(sqt[:, 0:0] if False else cgAP(SC_tq1), cgAP(SC_p21), cgAP(SC_cx))
            DR("dve")
            TT(cgAP(SC_qt1), cgAP(SC_qt1), cgAP(SC_tq1), OP.add)
            TT(cgAP(SC_qt2), cgAP(SC_p12), cgAP(SC_cz))
            TT(cgAP(SC_tq2), cgAP(SC_p22), cgAP(SC_cx))
            DR("dve")
            dqt = TT(cgAP(SC_qt2), cgAP(SC_qt2), cgAP(SC_tq2), OP.add)
            DR("dve")
            TT(cgAP(SC_cz), cgAP(SC_p22), cgAP(SC_bz))
            TT(cgAP(SC_tq1), cgAP(SC_p12), cgAP(SC_bx))
            TT(cgAP(SC_cx), cgAP(SC_p11), cgAP(SC_bx))
            TT(cgAP(SC_tq2), cgAP(SC_p21), cgAP(SC_bz))
            DR("dve")
            TT(cgAP(SC_cz), cgAP(SC_cz), cgAP(SC_tq1), OP.subtract)
            dkt = TT(cgAP(SC_cx), cgAP(SC_cx), cgAP(SC_tq2), OP.subtract)
            DR("dve")
            if CHCUT < 8:
                continue
            WS("gpd", sems["dve"], kb.q["dve"].count)
            for ic in range(NIC):
                WS("gpd", sems["pe"], kb.q["pe"].count)
                lvv = DMA("gpd", vring[:],
                          d_val[:, ic * T + t0: ic * T + t0 + L])
                for tb in range(2):
                    WS("pe", sems["gpd"], lvv)
                    Wt("pe", "dve", kb.q["dve"].count)
                    ptr = TRN(ps_d[:, 0:128], vring[:, tb * 128:(tb + 1) * 128],
                              identf[:])
                    Wt("dve", "pe", ptr)
                    dU = CPY(Ust[:, (ic * 2 + tb) * 128:(ic * 2 + tb) * 128 + 128],
                             ps_d[:, 0:128])
                    DR("dve")
                if ic < NIC - 1:
                    WS("gpd", sems["dve"], dU)
            if CHCUT < 9:
                continue
            for hh in range(2):
                Wt("dve", "pe", kb.q["pe"].count)
                TT(cg3(SC_tq1).rearrange("p g t -> p (g t)").rearrange("p (g b i) -> p g b i", g=NG, i=BL) if False else
                   BG(SC_tq1, 2048).rearrange("p (g b i) -> p g b i", g=NG, i=BL),
                   BG(SC_qt1, 2048).rearrange("p (g b i) -> p g b i", g=NG, i=BL),
                   Ebcast(Ecum, "11"))
                TT(BG(SC_tq2, 2048).rearrange("p (g b i) -> p g b i", g=NG, i=BL),
                   BG(SC_qt2, 2048).rearrange("p (g b i) -> p g b i", g=NG, i=BL),
                   Ebcast(Ecum, "21"))
                DR("dve")
                for hs in range(4):
                    TT(Qst[0:32, :].rearrange("p (hl t) -> p hl t", t=L)[:, hs::4, :],
                       BG(SC_tq1, 2048)[hs * 32:(hs + 1) * 32, :].rearrange(
                           "p (g t) -> p g t", t=L)[:, 4 * hh:4 * hh + 4, :],
                       BG(SC_tq2, 2048)[hs * 32:(hs + 1) * 32, :].rearrange(
                           "p (g t) -> p g t", t=L)[:, 4 * hh:4 * hh + 4, :], OP.add)
                DR("dve")
                TT(BG(SC_tq1, 2048).rearrange("p (g b i) -> p g b i", g=NG, i=BL),
                   BG(SC_qt1, 2048).rearrange("p (g b i) -> p g b i", g=NG, i=BL),
                   Ebcast(Ecum, "12"))
                TT(BG(SC_tq2, 2048).rearrange("p (g b i) -> p g b i", g=NG, i=BL),
                   BG(SC_qt2, 2048).rearrange("p (g b i) -> p g b i", g=NG, i=BL),
                   Ebcast(Ecum, "22"))
                DR("dve")
                for hs in range(4):
                    TT(Qst[32:64, :].rearrange("p (hl t) -> p hl t", t=L)[:, hs::4, :],
                       BG(SC_tq1, 2048)[hs * 32:(hs + 1) * 32, :].rearrange(
                           "p (g t) -> p g t", t=L)[:, 4 * hh:4 * hh + 4, :],
                       BG(SC_tq2, 2048)[hs * 32:(hs + 1) * 32, :].rearrange(
                           "p (g t) -> p g t", t=L)[:, 4 * hh:4 * hh + 4, :], OP.add)
                DR("dve")
                def BEb(c2):
                    return BE[c2][:].rearrange("p (g b) -> p g b", g=NG).unsqueeze(3).broadcast_to([128, NG, NBL, BL])
                TT(BG(SC_tq1, 2048).rearrange("p (g b i) -> p g b i", g=NG, i=BL),
                   BG(SC_cz, 2048).rearrange("p (g b i) -> p g b i", g=NG, i=BL), BEb("22"))
                TT(BG(SC_tq2, 2048).rearrange("p (g b i) -> p g b i", g=NG, i=BL),
                   BG(SC_cx, 2048).rearrange("p (g b i) -> p g b i", g=NG, i=BL), BEb("12"))
                DR("dve")
                for hs in range(4):
                    TT(Kst[0:32, :].rearrange("p (hl t) -> p hl t", t=L)[:, hs::4, :],
                       BG(SC_tq1, 2048)[hs * 32:(hs + 1) * 32, :].rearrange(
                           "p (g t) -> p g t", t=L)[:, 4 * hh:4 * hh + 4, :],
                       BG(SC_tq2, 2048)[hs * 32:(hs + 1) * 32, :].rearrange(
                           "p (g t) -> p g t", t=L)[:, 4 * hh:4 * hh + 4, :], OP.add)
                DR("dve")
                TT(BG(SC_tq1, 2048).rearrange("p (g b i) -> p g b i", g=NG, i=BL),
                   BG(SC_cz, 2048).rearrange("p (g b i) -> p g b i", g=NG, i=BL), BEb("21"))
                TT(BG(SC_tq2, 2048).rearrange("p (g b i) -> p g b i", g=NG, i=BL),
                   BG(SC_cx, 2048).rearrange("p (g b i) -> p g b i", g=NG, i=BL), BEb("11"))
                DR("dve")
                for hs in range(4):
                    dasm = TT(Kst[32:64, :].rearrange("p (hl t) -> p hl t", t=L)[:, hs::4, :],
                              BG(SC_tq1, 2048)[hs * 32:(hs + 1) * 32, :].rearrange(
                                  "p (g t) -> p g t", t=L)[:, 4 * hh:4 * hh + 4, :],
                              BG(SC_tq2, 2048)[hs * 32:(hs + 1) * 32, :].rearrange(
                                  "p (g t) -> p g t", t=L)[:, 4 * hh:4 * hh + 4, :], OP.add)
                DR("dve")
                HH = int(os.environ.get("HHCUT", "99"))
                for hl in range(16):
                    hgl = hh * 16 + hl
                    ic = hgl // 2
                    if HH < 2:
                        continue
                    Wt("pe", "dve", dasm)
                    Wt("pe", "act", kb.q["act"].count)
                    mmy = MM(ps_y[0:64, 0:L], Xfull[0:64, hgl * 64:(hgl + 1) * 64],
                             Qst[0:64, hl * L:(hl + 1) * L], start=True, stop=False)
                    if HH < 3:
                        continue
                    for j in (1, 0):
                        Wt("pe", "dve", kb.q["dve"].count)
                        Wt("pe", "act", kb.q["act"].count)
                        mmg = MM(ps_G[:, 0:L], Kst[0:64, hl * L + j * 128: hl * L + j * 128 + 128],
                                 Qst[0:64, hl * L:(hl + 1) * L], start=True, stop=True)
                        if HH < 4:
                            continue
                        Wt("act", "pe", mmg)
                        Wt("act", "gpd", kb.q["gpd"].count)
                        if j == 0:
                            aG = ACTV(Gsb[:, 0:256], ps_G[:, 0:256], AF.Identity)
                        else:
                            aG = ACTV(Gsb[:, 128:256], ps_G[:, 128:256], AF.Identity)
                        WS("gpd", sems["act"], aG)
                        dG = kb.on("gpd", lambda e, j=j: e.affine_select(
                            Gsb[:, j * 128:(j + 1) * 128], Gsb[:, j * 128:(j + 1) * 128],
                            [[1, 128]], OP.is_ge, 0.0, base=0, channel_multiplier=-1))
                        if HH < 5:
                            continue
                        WS("pe", sems["gpd"], dG)
                        mmy = MM(ps_y[0:64, j * 128:L],
                                 Ust[:, (ic * 2 + j) * 128 + (hgl % 2) * 64:
                                                 (ic * 2 + j) * 128 + (hgl % 2) * 64 + 64],
                                 Gsb[:, j * 128:L], start=False, stop=(j == 0))
                    if HH < 6:
                        continue
                    Wt("act", "pe", mmy)
                    Wt("act", "dve", kb.q["dve"].count)
                    ay = ACTV(ysb[(hgl % 2) * 64:(hgl % 2) * 64 + 64, 0:L], ps_y[0:64, 0:L],
                              AF.Identity)
                    if hgl % 2 == 1:
                        WS("gpd", sems["dve"], kb.q["dve"].count)
                        lvv = DMA("gpd", vring[:], d_val[:, ic * T + t0: ic * T + t0 + L])
                        lgg = DMA("gpd", gring[:], d_gate[:, ic * T + t0: ic * T + t0 + L])
                        Wt("dve", "act", ay)
                        WS("dve", sems["gpd"], lgg)
                        STT(sqt[:, 0:L], vring[:], skips[:, ic:ic + 1],
                            ysb[:, 0:L], OP.mult, OP.add)
                        DR("dve")
                        Wt("dve", "sync", kb.q["sync"].count)
                        dyg = TT(ygbf[:], sqt[:, 0:L], gring[:])
                        Wt("sync", "dve", dyg)
                        DMA("sync", d_yg[:, ic * T + t0: ic * T + t0 + L], ygbf[:])
                for hl in (range(16) if HH >= 7 else ()):
                    hgl = hh * 16 + hl
                    for j in range(2):
                        Wt("pe", "dve", kb.q["dve"].count)
                        ptr = TRN(ps_d[:, 0:64],
                                  Kst[0:64, hl * L + j * 128: hl * L + j * 128 + 128],
                                  identf[0:64, 0:64])
                        Wt("dve", "pe", ptr)
                        dKT = CPY(KTt[:], ps_d[:, 0:64])
                        Wt("pe", "dve", dKT)
                        MM(ps_k0[0:32, (hgl % 8) * 64:(hgl % 8) * 64 + 64],
                           KTt[:, 0:32],
                           Ust[:, (ic * 2 + j) * 128 + (hgl % 2) * 64:
                                           (ic * 2 + j) * 128 + (hgl % 2) * 64 + 64]
                           if False else
                           Ust[:, ((hgl // 2) * 2 + j) * 128 + (hgl % 2) * 64:
                                           ((hgl // 2) * 2 + j) * 128 + (hgl % 2) * 64 + 64],
                           start=(j == 0), stop=(j == 1))
                        mku = MM(ps_k1[0:32, (hgl % 8) * 64:(hgl % 8) * 64 + 64],
                                 KTt[:, 32:64],
                                 Ust[:, ((hgl // 2) * 2 + j) * 128 + (hgl % 2) * 64:
                                                 ((hgl // 2) * 2 + j) * 128 + (hgl % 2) * 64 + 64],
                                 start=(j == 0), stop=(j == 1))
                    if hgl % 8 == 7:
                        grp = hgl // 8
                        Wt("dve", "pe", mku)
                        TT(ktuz[:, grp * 512:(grp + 1) * 512], ps_k0[0:32, 0:512],
                           Xz[:, grp * 512:(grp + 1) * 512], OP.add)
                        dku = TT(ktus[:, grp * 512:(grp + 1) * 512], ps_k1[0:32, 0:512],
                                 Xs_[:, grp * 512:(grp + 1) * 512], OP.add)
                        DR("dve")
                        Wt("pe", "dve", dku)
            if CHCUT < 10:
                continue
            def PLb(ci):
                return PL4[0:32, :].rearrange("p (c h) -> p c h", c=4)[:, ci, :].unsqueeze(2).broadcast_to([32, H, DH])
            def hd3(ap):
                return ap.rearrange("p (h d) -> p h d", d=DH)
            TT(hd3(Xz), hd3(ktuz), PLb(0))
            TT(hd3(tmpX), hd3(ktus), PLb(1))
            DR("dve")
            TT(Xz, Xz, tmpX, OP.add)
            TT(hd3(Xs_), hd3(ktuz), PLb(2))
            DR("dve")
            TT(hd3(tmpX), hd3(ktus), PLb(3))
            DR("dve")
            TT(Xs_, Xs_, tmpX, OP.add)
            DR("dve")
            CPY(Xfull[0:32, :], Xz)
            dXf = CPY(Xfull[32:64, :], Xs_)
            DR("dve")

        if PH <= 5:
            return _finish()

        wob = BGb(0, 16384)
        woTb = BGb(8192, 8192)
        ygl = BGb(12288, 4096)
        xnl = BGb(14336, 2048)
        xt_t = BG(16384, 1024)
        uo_t = BG(17408, 1024)
        ht_t = BG(18432, 1024)
        g_t = BG(19456, 1024)
        o_t = BG(20480, 1024)
        WS("gpd", sems["pe"], kb.q["pe"].count)
        WS("gpd", sems["dve"], kb.q["dve"].count)
        for kc in range(16):
            lwo = DMA("gpd", wob[:, kc * 1024:(kc + 1) * 1024],
                      W_outT[kc * 128:(kc + 1) * 128, :])
        for kc in range(8):
            lwo = DMA("gpd", woTb[:, kc * 1024:(kc + 1) * 1024],
                      wT["wo"][kc * 128:(kc + 1) * 128, :])
        WS("dve", sems["cc"], 1)
        WS("sync", sems["cc"], 1)
        DR("dve")
        for tt in range(16):
            WS("gpd", sems["pe"], kb.q["pe"].count)
            for kc in range(16):
                lyg = DMA("gpd", ygl[:, kc * 128:(kc + 1) * 128],
                          d_yg[:, kc * T + tt * 128: kc * T + (tt + 1) * 128])
            for kc in range(8):
                lxn = DMA("gpd", xnl[:, kc * 128:(kc + 1) * 128],
                          d_xn[:, kc * T + tt * 128: kc * T + (tt + 1) * 128])
            Wt("sync", "dve", kb.q["dve"].count)
            lx = DMA("sync", xt_t, x_ssm[tt * 128:(tt + 1) * 128, :])
            lu = DMA("sync", uo_t, cc_out[tt * 128:(tt + 1) * 128, :])
            WS("pe", sems["gpd"], lyg if False else kb.q["gpd"].count)
            Wt("pe", "dve", kb.q["dve"].count)
            for oc in range(2):
                ps = (ps_a, ps_b)[oc]
                for kc in range(16):
                    mms = MM(ps[:, 0:512], ygl[:, kc * 128:(kc + 1) * 128],
                             wob[:, kc * 1024 + oc * 512: kc * 1024 + (oc + 1) * 512],
                             start=(kc == 0), stop=(kc == 15))
            for oc in range(2):
                ps = (ps_c, ps_d)[oc]
                for kc in range(8):
                    mmh = MM(ps[:, 0:512], xnl[:, kc * 128:(kc + 1) * 128],
                             woTb[:, kc * 1024 + oc * 512: kc * 1024 + (oc + 1) * 512],
                             start=(kc == 0), stop=(kc == 7))
            Wt("dve", "pe", mmh)
            WS("dve", sems["sync"], lu)
            TT(ht_t, uo_t, bo_b[:], OP.add)
            DR("dve")
            TT(ht_t[:, 0:512], ht_t[:, 0:512], ps_c[:, 0:512], OP.add)
            TT(ht_t[:, 512:1024], ht_t[:, 512:1024], ps_d[:, 0:512], OP.add)
            DR("dve")
            dht = kb.q["dve"].count
            Wt("act", "dve", dht)
            ah = ACTV(g_t, ht_t, AF.Tanh)
            DR("act")
            ah = ACTV(g_t, g_t, AF.Silu)
            Wt("dve", "act", ah)
            TT(ht_t[:, 0:512], ps_a[:, 0:512], g_t[:, 0:512])
            TT(ht_t[:, 512:1024], ps_b[:, 0:512], g_t[:, 512:1024])
            DR("dve")
            Wt("dve", "sync", kb.q["sync"].count)
            dout_ = TT(o_t, xt_t, ht_t, OP.add)
            Wt("sync", "dve", dout_)
            dfin = DMA("sync", out[tt * 128:(tt + 1) * 128, :], o_t)
        return _finish()

    return nc


_NC_CACHE = [None]


def kernel(**inputs):
    if _NC_CACHE[0] is None:
        _NC_CACHE[0] = build_program()
    nc = _NC_CACHE[0]
    in_maps = []
    for core in range(B):
        g = _host_arrays(inputs, core)
        bcH = np.zeros((32, NG * 128), dtype=np.float32)
        for gi in range(NG):
            for hs in range(4):
                bcH[4 * gi + hs, gi * 128 + hs * 32: gi * 128 + (hs + 1) * 32] = 1.0
        g["bcH"] = bcH
        in_maps.append(g)
    res = run_bass_kernel_spmd(nc, in_maps, core_ids=list(range(B)))
    return np.stack([res.results[c]["out"] for c in range(B)], axis=0)


# version 12
# speedup vs baseline: 1.1128x; 1.0926x over previous
import os
os.environ.setdefault("BASS_INTERP_NUM_WORKERS", "1")
import contextlib
import numpy as np

import concourse.bass as bass
import concourse.mybir as mybir
from concourse.bass_utils import run_bass_kernel_spmd

F32 = mybir.dt.float32
F32R = mybir.dt.float32r
BF16 = mybir.dt.bfloat16
AF = mybir.ActivationFunctionType
OP = mybir.AluOpType

B = 8; T = 2048; D = 1024; DI = 2048; H = 32; DH = 64
NG = 8; L = 256; NCH = T // L; BL = 16; NBL = L // BL
NIC = DI // 128
ROWS_F = 256
DT_MIN, DT_MAX, K_MAX, EPS = 1e-4, 0.1, 0.5, 1e-8
LN_EPS = 1e-5
EPAD = 8; EW = EPAD + NBL


def _host_arrays(inputs, core):
    W_in = inputs["L0_attn_W_in"]
    perm = np.arange(4160)
    perm[2 * DI:2 * DI + H] = 2 * DI + 2 * np.arange(H)
    perm[2 * DI + H:2 * DI + 2 * H] = 2 * DI + 2 * np.arange(H) + 1
    W_inT = np.ascontiguousarray(W_in[perm].T)

    W_bc = inputs["L0_attn_W_bc"]
    o = np.arange(4096)
    hh, qq, nn = o // 128, (o % 128) // 32, o % 32
    pos = (qq * NG + hh // 4) * 128 + (hh % 4) * 32 + nn
    new = np.zeros(4096, dtype=np.int64); new[pos] = o
    W_bcT = np.ascontiguousarray(W_bc[new].T)

    g = {
        "x_ssm": np.ascontiguousarray(inputs["L0_attn_x"][core]),
        "x_ffn": np.ascontiguousarray(
            inputs["L0_attn_x"][:, core * ROWS_F:(core + 1) * ROWS_F, :].reshape(T, D)),
        "W_inT": W_inT,
        "W_bcT": W_bcT,
        "W_outT": np.ascontiguousarray(inputs["L0_attn_W_out"].T),
        "dwW": np.ascontiguousarray(
            inputs["L0_attn_dw_weight"].reshape(NIC, 128, 4).transpose(1, 0, 2).reshape(128, NIC * 4)),
        "dwB": np.ascontiguousarray(inputs["L0_attn_dw_bias"].reshape(NIC, 128).T),
        "skipc": np.ascontiguousarray(inputs["L0_attn_skip"].reshape(NIC, 128).T),
        "A_log": np.ascontiguousarray(inputs["L0_attn_A_log"].reshape(NG, 128).T),
        "dt_bias": np.ascontiguousarray(inputs["L0_attn_dt_bias"].reshape(H, 1)),
        "bo": inputs["L0_ffn_bo"].reshape(1, D),
        "in_c": np.ascontiguousarray(inputs["L0_ffn_in_c"].reshape(8, 128).T),
    }
    for nm in ("wf", "wi", "wc", "uf", "ui", "uc", "wo", "uo"):
        g[nm + "T"] = np.ascontiguousarray(inputs["L0_ffn_" + nm].T)
    for nm in ("bf", "bi", "bc"):
        g["b" + nm[1]] = np.ascontiguousarray(inputs["L0_ffn_" + nm].reshape(8, 128).T)
    g["ident"] = np.eye(128, dtype=np.float32)
    g["trilm"] = np.triu(np.ones((128, 128), dtype=np.float32))
    sel4 = np.zeros((128, 4), dtype=np.float32)
    for hs in range(4):
        sel4[hs * 32:(hs + 1) * 32, hs] = 1.0
    g["sel4"] = sel4
    g["sel4T"] = np.ascontiguousarray(sel4.T)
    g["ones1"] = np.ones((1, 128), dtype=np.float32)
    g["onescol"] = np.ones((128, 1), dtype=np.float32)
    return g


class _Q:
    def __init__(self, step):
        self.ops = []; self.count = 0; self.step = step; self.sem = None


class KB:
    def __init__(self):
        self.q = {n: _Q(16 if n in ("sync", "gpd") else 1)
                  for n in ("sync", "gpd", "pe", "dve", "act", "gp")}

    def on(self, qn, fn):
        eq = self.q[qn]
        if eq.step == 16 and eq.count > 0:
            eq.ops.append(lambda e, s=eq, v=eq.count: e.wait_ge(s.sem, v))
        eq.ops.append(lambda e, f=fn, eq=eq: f(e).then_inc(eq.sem, eq.step))
        eq.count += eq.step
        return eq.count

    def drain(self, qn):
        self.q[qn].ops.append(lambda e: e.drain())

    def wait(self, qn, prod, val):
        if val > 0:
            self.q[qn].ops.append(lambda e, s=self.q[prod], v=val: e.wait_ge(s.sem, v))

    def wait_sem(self, qn, sem, val):
        self.q[qn].ops.append(lambda e, s=sem, v=val: e.wait_ge(s, v))


def build_program():
    PH = int(os.environ.get("KPHASE", "9"))
    nc = bass.Bass("TRN2", target_bir_lowering=False)

    def din(name, shape, dtype=F32):
        return nc.dram_tensor(name, list(shape), dtype, kind="ExternalInput").ap()

    x_ssm = din("x_ssm", (T, D)); x_ffn = din("x_ffn", (T, D))
    W_inT = din("W_inT", (D, 4160)); W_bcT = din("W_bcT", (DI, 4096))
    W_outT = din("W_outT", (DI, D))
    wT = {nm: din(nm + "T", (D, D)) for nm in ("wf", "wi", "wc", "uf", "ui", "uc", "wo", "uo")}
    dwW = din("dwW", (128, NIC * 4)); dwB = din("dwB", (128, NIC))
    skipc = din("skipc", (128, NIC))
    A_log = din("A_log", (128, NG)); dt_bias = din("dt_bias", (H, 1))
    b_in = {nm: din("b" + nm, (128, 8)) for nm in ("f", "i", "c")}
    bo = din("bo", (1, D)); in_c = din("in_c", (128, 8))
    ident_in = din("ident", (128, 128)); tril_in = din("trilm", (128, 128))
    sel4_in = din("sel4", (128, 4)); sel4T_in = din("sel4T", (4, 128))
    ones1_in = din("ones1", (1, 128))
    onescol_in = din("onescol", (128, 1))

    out = nc.dram_tensor("out", [T, D], F32, kind="ExternalOutput").ap()
    cc_in = nc.dram_tensor("cc_in", [ROWS_F, D], F32).ap()
    cc_out = nc.dram_tensor("cc_out", [T, D], F32).ap()
    d_xn = nc.dram_tensor("d_xn", [128, 8 * T], BF16).ap()
    d_xnf = nc.dram_tensor("d_xnf", [128, 8 * T], BF16).ap()
    d_gate = nc.dram_tensor("d_gate", [128, NIC * T], F32).ap()
    d_vr = nc.dram_tensor("d_vr", [128, NIC * T], F32).ap()
    d_val = nc.dram_tensor("d_val", [128, NIC * T], F32).ap()
    d_par = nc.dram_tensor("d_par", [64, T], F32).ap()
    d_bc = nc.dram_tensor("d_bc", [4, 128, NG * T], F32).ap()
    d_yg = nc.dram_tensor("d_yg", [128, NIC * T], BF16).ap()
    xw_d = nc.dram_tensor("xw_d", [3, 128, 8 * T], BF16).ap()

    es = contextlib.ExitStack()
    with es:
        def sb(name, p, fr, dtype=F32):
            return es.enter_context(nc.sbuf_tensor(name, [p, fr], dtype))

        identf = sb("identf", 128, 128)
        identr = sb("identr", 128, 128, F32R)
        identb = sb("identb", 128, 128, BF16)
        trils = sb("trils", 128, 128)
        sel4s = sb("sel4s", 128, 4); sel4Ts = sb("sel4Ts", 4, 128)
        ones1s = sb("ones1s", 1, 128)
        onescol = sb("onescol_s", 128, 1)
        zc = sb("zc", 128, 1)
        dwWs = sb("dwWs", 128, NIC * 4); dwBs = sb("dwBs", 128, NIC)
        skips = sb("skips", 128, NIC)
        A_s = sb("A_s", 128, NG); Aneg = sb("Aneg", 128, NG)
        dtb_s = sb("dtb_s", H, 1)
        bias_s = {nm: sb("bias" + nm, 128, 8) for nm in ("f", "i", "c")}
        bo_r = sb("bo_r", 1, D); bo_b = sb("bo_b", 128, D)
        inc_s = sb("inc_s", 128, 8)
        dtsl = sb("dtsl", 32, L); kdsl = sb("kdsl", 32, L)
        dt2sl = sb("dt2sl", 32, L); dk1sl = sb("dk1sl", 32, L)
        parsl = sb("parsl", 64, L)
        rbsl = sb("rbsl", 128, NG * NBL)
        PL4 = sb("PL4", 32, 4 * H)
        PLt = {c2: sb("PLt" + c2, 128, NG) for c2 in ("11", "12", "21", "22")}
        E1 = {c2: sb("E1" + c2, 128, NG * EW) for c2 in ("11", "12", "21", "22")}
        E2 = {c2: sb("E2" + c2, 128, NG * EW) for c2 in ("11", "12", "21", "22")}
        BE = {c2: sb("BE" + c2, 128, NG * NBL) for c2 in ("11", "12", "21", "22")}
        idE = sb("idE", 128, NG * NBL)
        xln = sb("xln", 128, D)
        xln2 = sb("xln2", 128, D)
        xnbf = sb("xnbf", 128, D, BF16)
        ygbf = sb("ygbf", 128, L, BF16)
        stat = sb("stat", 128, 4)
        sqt = sb("sqt", 128, 512)
        fss = sb("fss", 4, 512); fsb = sb("fsb", 128, 512)
        Gsb = sb("Gsb", 128, L)
        KTt = sb("KTt", 128, DH)
        ysb = sb("ysb", 128, L)
        winB = sb("winB", 128, 8 * 128, BF16)
        wrowB = sb("wrowB", 128, D, BF16)
        wrow2 = sb("wrow2", 128, D, BF16)
        xwr = sb("xwr", 128, 8 * ROWS_F, BF16)
        vring = sb("vring", 128, L)
        gring = sb("gring", 128, L)

        AW = 36864
        BIG_OFF = 81856
        bigh = nc.alloc_sbuf_tensor_at("big", [128, AW], F32, offset=BIG_OFF)
        big = bigh.ap()

        def BG(off, w):
            assert off + w <= AW, (off, w)
            return big[:, off:off + w]

        def BGb(off, wbf):
            assert off + wbf // 2 <= AW
            return big[:, off:off + wbf // 2].bitcast(BF16)

        def BGr(off, w):
            return big[:, off:off + w].bitcast(F32R)

        SC_bz, SC_bx, SC_cz, SC_cx = 0, 2048, 4096, 6144
        SC_dtc, SC_S, SC_Sdt, SC_m12, SC_m22 = 8192, 10240, 12288, 14336, 16384
        SC_p11, SC_p12, SC_p21, SC_p22 = 18432, 20480, 22528, 24576
        SC_cumI, SC_idPw = 26624, 28672
        SC_qt1, SC_qt2 = 26624, 28672
        SC_tq1, SC_tq2 = 14336, 16384
        SC_U = 8192
        SC_Xz, SC_Xs, SC_Xf = 30720, 32768, 34816

        Xz = big[0:32, SC_Xz:SC_Xz + 2048]
        Xs_ = big[0:32, SC_Xs:SC_Xs + 2048]
        Xfull = nc.alloc_sbuf_tensor_at(
            "Xfull", [64, 2048], F32, offset=BIG_OFF + SC_Xf * 4).ap()
        ktuz = big[0:32, SC_bz:SC_bz + 2048]
        ktus = big[0:32, SC_bx:SC_bx + 2048]
        tmpX = big[0:32, SC_tq1:SC_tq1 + 2048]
        Qst = nc.alloc_sbuf_tensor_at(
            "Qst", [64, 4096], F32, offset=BIG_OFF + SC_p11 * 4).ap()
        Kst = nc.alloc_sbuf_tensor_at(
            "Kst", [64, 4096], F32, offset=BIG_OFF + SC_p21 * 4).ap()
        Ust = nc.alloc_sbuf_tensor_at(
            "Ust", [128, 4096], F32, offset=BIG_OFF + SC_U * 4).ap()
        xwr3 = BGb(20480, 6144)
        cst = BG(23552, 2048)
        cbf = BGb(25600, 4096)
        cbf2 = BGb(27648, 4096)
        gateT = BG(29696, 768)

        ps_a = es.enter_context(nc.psum_tensor("ps_a", [128, 512], F32))
        ps_b = es.enter_context(nc.psum_tensor("ps_b", [128, 512], F32))
        ps_c = es.enter_context(nc.psum_tensor("ps_c", [128, 512], F32))
        ps_d = es.enter_context(nc.psum_tensor("ps_d", [128, 512], F32))
        ps_G = es.enter_context(nc.psum_tensor("ps_G", [128, 512], F32))
        ps_y = es.enter_context(nc.psum_tensor("ps_y", [64, 512], F32))
        ps_k0 = es.enter_context(nc.psum_tensor("ps_k0", [32, 512], F32))
        ps_k1 = es.enter_context(nc.psum_tensor("ps_k1", [32, 512], F32))

        kb = KB()
        sems = {}
        for nm in ("sync", "gpd", "pe", "dve", "act", "gp", "cc"):
            sems[nm] = es.enter_context(nc.semaphore(nm))
        for nm in ("sync", "gpd", "pe", "dve", "act", "gp"):
            kb.q[nm].sem = sems[nm]

        blkctx = es.enter_context(nc.Block())

        def DMA(qn, dst, src):
            return kb.on(qn, lambda e: e.dma_start(out=dst, in_=src))

        def MM(dst, lhsT, rhs, start=True, stop=True):
            return kb.on("pe", lambda e: e.matmul(dst, lhsT, rhs, start=start, stop=stop))

        def TRN(dst, src, ident):
            return kb.on("pe", lambda e: e.transpose(dst, src, ident))

        def TT(dst, a, b, op=OP.mult, q="dve"):
            return kb.on(q, lambda e: e.tensor_tensor(dst, a, b, op))

        def TS1(dst, a, s1, op=OP.mult, q="dve"):
            return kb.on(q, lambda e: e.tensor_single_scalar(dst, a, s1, op))

        def TS2(dst, a, s1, s2, op0, op1, q="dve"):
            return kb.on(q, lambda e: e.tensor_scalar(dst, a, s1, s2, op0, op1))

        def STT(dst, in0, scalar, in1, op0=OP.mult, op1=OP.add, q="dve"):
            return kb.on(q, lambda e: e.scalar_tensor_tensor(dst, in0, scalar, in1, op0, op1))

        def CPY(dst, src, q="dve"):
            return kb.on(q, lambda e: e.tensor_copy(dst, src))

        def RCP(dst, src):
            return kb.on("dve", lambda e: e.reciprocal(dst, src))

        def ACTV(dst, src, func, bias=0.0, scale=1.0):
            return kb.on("act", lambda e: e.activation(dst, src, func, bias=bias, scale=scale))

        def SCANOP(dst, d0, d1, init):
            return kb.on("dve", lambda e: e.tensor_tensor_scan(dst, d0, d1, init, OP.mult, OP.add))

        def MS(qn, ap, v):
            return kb.on(qn, lambda e: e.memset(ap, v))

        def FILL(ap, v):
            src_col = onescol if v == 1.0 else zc
            shp = list(ap.shape)
            bc_ap = src_col[0:shp[0], 0:1]
            for d in shp[2:]:
                bc_ap = bc_ap.unsqueeze(len(bc_ap.shape))
            return kb.on("dve", lambda e: e.tensor_copy(ap, bc_ap.broadcast_to(shp)))

        def RS(dst, src):
            return kb.on("dve", lambda e: e.reduce_sum(dst, src, axis=mybir.AxisListType.X))

        Wt = kb.wait; WS = kb.wait_sem; DR = kb.drain

        def bar(qn):
            for n in kb.q:
                if n != qn and kb.q[n].count > 0:
                    Wt(qn, n, kb.q[n].count)

        for dst, src in ((trils[:], tril_in), (sel4s[:], sel4_in), (sel4Ts[:], sel4T_in),
                         (dwWs[:], dwW), (dwBs[:], dwB), (skips[:], skipc),
                         (A_s[:], A_log), (dtb_s[:], dt_bias), (bo_r[:], bo),
                         (inc_s[:], in_c), (identf[:], ident_in), (ones1s[:], ones1_in),
                         (onescol[:], onescol_in)):
            DMA("sync", dst, src)
        for nm in ("f", "i", "c"):
            ld_c = DMA("sync", bias_s[nm][:], b_in[nm])

        Wt("act", "sync", ld_c)
        aA = ACTV(A_s[:], A_s[:], AF.Exp)
        Wt("dve", "act", aA)
        CPY(identb[:], identf[:])
        CPY(identr[:], identf[:])
        DR("dve")
        TS1(Aneg[:], A_s[:], -1.0)
        if os.environ.get("SKIPBO", "") != "1":
            Wt("pe", "sync", ld_c)
            MM(ps_a[:, 0:512], ones1s[:], bo_r[:, 0:512])
            m2 = MM(ps_b[:, 0:512], ones1s[:], bo_r[:, 512:1024])
            Wt("dve", "pe", m2)
            CPY(bo_b[:, 0:512], ps_a[:, 0:512])
            CPY(bo_b[:, 512:1024], ps_b[:, 0:512])
        SKIPX = os.environ.get("SKIPX", "") == "1"
        for Ex in (() if SKIPX else (E1, E2)):
            MS("dve", Ex["11"][:], 0.0); MS("dve", Ex["12"][:], 0.0)
            MS("dve", Ex["21"][:], 0.0); MS("dve", Ex["22"][:], 0.0)
        DR("dve")
        for Ex in (() if SKIPX else (E1, E2)):
            for gi in range(NG):
                MS("dve", Ex["11"][:, gi * EW: gi * EW + EPAD], 1.0)
                MS("dve", Ex["22"][:, gi * EW: gi * EW + EPAD], 1.0)
        if not SKIPX:
            MS("dve", Xz, 0.0)
            MS("dve", Xs_, 0.0)
            MS("dve", zc[:], 0.0)
            DR("dve")
            CPY(Xfull[0:32, :], Xz)
            CPY(Xfull[32:64, :], Xs_)
            DR("dve")

        def _finish():
            kb.q["sync"].ops.append(
                lambda e, s=kb.q["sync"]: e.wait_ge(s.sem, s.count))

            @blkctx.sync
            def _(e):
                for op in kb.q["sync"].ops:
                    op(e)

            @blkctx.tensor
            def _(e):
                for op in kb.q["pe"].ops:
                    op(e)

            @blkctx.vector
            def _(e):
                for op in kb.q["dve"].ops:
                    op(e)

            @blkctx.scalar
            def _(e):
                for op in kb.q["act"].ops:
                    op(e)

            @blkctx.gpsimd
            def _(e):
                for op in kb.q["gpd"].ops:
                    op(e)
            return nc


        LNCUT = int(os.environ.get("LNCUT", "9"))

        def ln_tile(src_hbm, tt, dst_dram):
            lx = DMA("sync", xln[:], src_hbm[tt * 128:(tt + 1) * 128, :])
            Wt("dve", "sync", lx)
            bar("dve") if False else None
            DR("dve")
            RS(stat[:, 0:1].unsqueeze(2), xln[:].unsqueeze(1))
            if LNCUT <= 1:
                return lx
            Wt("act", "sync", lx)
            Wt("act", "dve", kb.q["dve"].count)
            a1 = ACTV(sqt[:, 0:512], xln[:, 0:512], AF.Square)
            Wt("dve", "act", a1)
            DR("dve")
            RS(stat[:, 1:2].unsqueeze(2), sqt[:, 0:512].unsqueeze(1))
            d0 = kb.q["dve"].count
            Wt("act", "dve", d0)
            a2 = ACTV(sqt[:, 0:512], xln[:, 512:1024], AF.Square)
            Wt("dve", "act", a2)
            DR("dve")
            RS(stat[:, 2:3].unsqueeze(2), sqt[:, 0:512].unsqueeze(1))
            DR("dve")
            TT(stat[:, 1:2], stat[:, 1:2], stat[:, 2:3], OP.add)
            DR("dve")
            TS1(stat[:, 0:1], stat[:, 0:1], 1.0 / D)
            TS1(stat[:, 1:2], stat[:, 1:2], 1.0 / D)
            DR("dve")
            STT(stat[:, 2:3], stat[:, 0:1], stat[:, 0:1], stat[:, 1:2],
                OP.mult, OP.subtract)
            DR("dve")
            d1 = TS2(stat[:, 2:3], stat[:, 2:3], -1.0, LN_EPS, OP.mult, OP.add)
            Wt("act", "dve", d1)
            ar = ACTV(stat[:, 3:4], stat[:, 2:3], AF.Sqrt)
            Wt("dve", "act", ar)
            RCP(stat[:, 3:4], stat[:, 3:4])
            DR("dve")
            TT(stat[:, 2:3], stat[:, 0:1], stat[:, 3:4], OP.mult)
            DR("dve")
            d2 = TS1(stat[:, 2:3], stat[:, 2:3], -1.0)
            if LNCUT <= 2:
                return lx
            Wt("act", "dve", d2)
            a3 = ACTV(xln2[:], xln[:], AF.Identity, bias=stat[:, 2:3], scale=stat[:, 3:4])
            if LNCUT <= 3:
                return lx
            for dc in range(8):
                Wt("pe", "act", kb.q["act"].count)
                ptr = TRN(ps_d[:, 0:128], xln2[:, dc * 128:(dc + 1) * 128], identf[:])
                if LNCUT <= 4:
                    continue
                Wt("act", "pe", ptr)
                Wt("act", "sync", kb.q["sync"].count)
                dcc2 = ACTV(ygbf[:, 0:128], ps_d[:, 0:128], AF.Identity)
                if LNCUT <= 5:
                    continue
                Wt("sync", "act", dcc2)
                dl = DMA("sync", dst_dram[:, dc * T + tt * 128: dc * T + tt * 128 + 128],
                         ygbf[:, 0:128])
            if LNCUT <= 5:
                return lx
            return dl

        if PH <= -2:
            return _finish()
        for tt in range(16):
            ln_tile(x_ffn, tt, d_xnf)
            if PH <= -1:
                return _finish()
        for tt in range(16):
            ld_xn = ln_tile(x_ssm, tt, d_xn)

        if PH <= 0:
            return _finish()

        bcH_in = din("bcH", (32, NG * 128))
        bcHs = sb("bcHs", 32, NG * 128)
        idEn = sb("idEn", 128, NG * NBL)
        ldh = DMA("sync", bcHs[:], bcH_in)

        xnfb = BGb(0, 16384)
        WS("gpd", sems["sync"], kb.q["sync"].count)
        DMA("gpd", xnfb[:, 0:8192], d_xnf[:, 0:8192])
        ldf = DMA("gpd", xnfb[:, 8192:16384], d_xnf[:, 8192:16384])
        FFu = {"uf": 8192, "ui": 12288, "uc": 16384}
        for nm in ("uf", "ui", "uc"):
            for kc in range(8):
                ldf = DMA("gpd", BGb(FFu[nm] + kc * 512, 2048)[:, 0:1024],
                          wT[nm][kc * 128:(kc + 1) * 128, :])

        PSL = [ps_a, ps_b, ps_c, ps_d]
        for gidx, wnm in enumerate(("wf", "wi", "wc")):
            for rs in range(4):
                for mh in range(2):
                    for kc in range(8):
                        WS("gpd", sems["pe"], kb.q["pe"].count)
                        lw = DMA("gpd", wrowB[:, 0:512],
                                 wT[wnm][kc * 128:(kc + 1) * 128, mh * 512:(mh + 1) * 512])
                        WS("pe", sems["gpd"], lw)
                        if kc == 0:
                            Wt("pe", "act", kb.q["act"].count)
                        for mq in range(4):
                            mmx = MM(PSL[mq][:, 0:512],
                                     wrowB[:, mq * 128:(mq + 1) * 128],
                                     xnfb[:, kc * 2048 + rs * 512: kc * 2048 + (rs + 1) * 512],
                                     start=(kc == 0), stop=(kc == 7))
                    Wt("act", "pe", mmx)
                    for mq in range(4):
                        mc = mh * 4 + mq
                        Wt("act", "sync", kb.q["sync"].count)
                        av = ACTV(xnbf[:, 0:512], PSL[mq][:, 0:512], AF.Identity)
                        Wt("sync", "act", av)
                        DMA("sync", xw_d[gidx][:, mc * T + rs * 512: mc * T + (rs + 1) * 512],
                            xnbf[:, 0:512])

        Wt("dve", "sync", ld_c)
        for mc in range(8):
            CPY(cst[:, mc * 256:(mc + 1) * 256],
                inc_s[:, mc:mc + 1].broadcast_to([128, 256]))
        DR("dve")
        for mc in range(8):
            d_cbf = CPY(cbf[:, mc * 256:(mc + 1) * 256], cst[:, mc * 256:(mc + 1) * 256])
        WS("pe", sems["gpd"], ldf)

        for step in range(8):
            csrc = cbf if step % 2 == 0 else cbf2
            cdst = cbf2 if step % 2 == 0 else cbf
            Wt("sync", "dve", kb.q["dve"].count)
            for gidx in range(3):
                lxw = DMA("sync", xwr3[:, gidx * 2048:(gidx + 1) * 2048].rearrange(
                              "p (mc t) -> p mc t", mc=8),
                          xw_d[gidx].rearrange("p (mc t) -> p mc t", mc=8)[:, :, step * 256:(step + 1) * 256])
            WS("pe", sems["sync"], lxw)
            Wt("pe", "dve", d_cbf)
            for mc in range(8):
                Wt("pe", "dve", kb.q["dve"].count)
                for gidx in range(3):
                    ps = (ps_a, ps_b, ps_c)[gidx]
                    unm = ("uf", "ui", "uc")[gidx]
                    for kc in range(8):
                        mmu = MM(ps[:, 0:256],
                                 BGb(FFu[unm] + kc * 512, 2048)[:, mc * 128:(mc + 1) * 128],
                                 csrc[:, kc * 256:(kc + 1) * 256],
                                 start=(kc == 0), stop=(kc == 7))
                Wt("dve", "pe", mmu)
                for gidx in range(3):
                    ps = (ps_a, ps_b, ps_c)[gidx]
                    TT(sqt[:, 0:256], ps[:, 0:256],
                       xwr3[:, gidx * 2048 + mc * 256: gidx * 2048 + (mc + 1) * 256], OP.add)
                    DR("dve")
                    dtx = kb.q["dve"].count
                    Wt("act", "dve", dtx)
                    fn = (AF.Sigmoid, AF.Sigmoid, AF.Tanh)[gidx]
                    ag = ACTV(gateT[:, gidx * 256:(gidx + 1) * 256], sqt[:, 0:256],
                              fn, bias=bias_s[("f", "i", "c")[gidx]][:, mc:mc + 1])
                    Wt("dve", "act", ag)
                TT(cst[:, mc * 256:(mc + 1) * 256], gateT[:, 0:256],
                   cst[:, mc * 256:(mc + 1) * 256])
                TT(gateT[:, 256:512], gateT[:, 256:512], gateT[:, 512:768])
                DR("dve")
                TT(cst[:, mc * 256:(mc + 1) * 256], cst[:, mc * 256:(mc + 1) * 256],
                   gateT[:, 256:512], OP.add)
                DR("dve")
                d_cbf = CPY(cdst[:, mc * 256:(mc + 1) * 256], cst[:, mc * 256:(mc + 1) * 256])

        cfin = cbf2 if 8 % 2 == 1 else cbf
        for rc in range(2):
            for oc in range(2):
                ps = (ps_a, ps_b)[oc]
                for kc in range(8):
                    WS("gpd", sems["pe"], kb.q["pe"].count)
                    lw = DMA("gpd", wrowB[:], wT["uo"][kc * 128:(kc + 1) * 128, :])
                    WS("pe", sems["gpd"], lw)
                    if kc == 0:
                        Wt("pe", "dve", kb.q["dve"].count)
                    mmu = MM(ps[:, 0:512],
                             cfin[:, kc * 256 + rc * 128: kc * 256 + rc * 128 + 128],
                             wrowB[:, oc * 512:(oc + 1) * 512],
                             start=(kc == 0), stop=(kc == 7))
                Wt("dve", "pe", mmu)
                Wt("dve", "sync", kb.q["sync"].count)
                duo = CPY(sqt[:, 0:512], ps[:, 0:512])
                Wt("sync", "dve", duo)
                dcc = DMA("sync", cc_in[rc * 128:(rc + 1) * 128, oc * 512:(oc + 1) * 512],
                          sqt[:, 0:512])
        WS("gpd", sems["sync"], dcc)
        kb.q["gpd"].ops.append(
            lambda e: e.collective_compute(
                "AllGather", OP.bypass, replica_groups=[list(range(8))],
                ins=[cc_in], outs=[cc_out]).then_inc(sems["cc"], 1))

        if PH <= 1:
            return _finish()

        xnb = BGb(0, 16384)
        WS("gpd", sems["pe"], kb.q["pe"].count)
        DMA("gpd", xnb[:, 0:8192], d_xn[:, 0:8192])
        ldx = DMA("gpd", xnb[:, 8192:16384], d_xn[:, 8192:16384])
        WS("pe", sems["gpd"], ldx)
        winB2 = BGb(8192, 1024)
        pe_done = [0, 0]
        lw_par = [0, 0]
        for oc in range(33):
            owid = 128 if oc < 32 else 64
            wbuf = (winB, winB2)[oc % 2]
            WS("gpd", sems["pe"], pe_done[oc % 2])
            for kc in range(8):
                lw = DMA("gpd", wbuf[:, kc * 128: kc * 128 + owid],
                         W_inT[kc * 128:(kc + 1) * 128, oc * 128: oc * 128 + owid])
            lw_par[oc % 2] = lw
            for tch in range(4):
                WS("pe", sems["gpd"], lw_par[oc % 2])
                Wt("pe", "act", kb.q["act"].count)
                Wt("pe", "dve", kb.q["dve"].count)
                for kc in range(8):
                    mmp = MM(ps_d[0:owid, 0:512], wbuf[:, kc * 128: kc * 128 + owid],
                             xnb[:, kc * 2048 + tch * 512: kc * 2048 + (tch + 1) * 512],
                             start=(kc == 0), stop=(kc == 7))
                if oc < 16:
                    Wt("act", "pe", mmp)
                    Wt("act", "sync", kb.q["sync"].count)
                    av = ACTV(sqt[:, 0:512], ps_d[:, 0:512], AF.Silu)
                    Wt("sync", "act", av)
                    DMA("sync", d_gate[:, oc * T + tch * 512: oc * T + (tch + 1) * 512],
                        sqt[:, 0:512])
                elif oc < 32:
                    Wt("act", "pe", mmp)
                    Wt("act", "sync", kb.q["sync"].count)
                    av = ACTV(sqt[:, 0:512], ps_d[:, 0:512], AF.Identity)
                    Wt("sync", "act", av)
                    DMA("sync", d_vr[:, (oc - 16) * T + tch * 512: (oc - 16) * T + (tch + 1) * 512],
                        sqt[:, 0:512])
                else:
                    Wt("act", "pe", mmp)
                    Wt("act", "sync", kb.q["sync"].count)
                    av = ACTV(sqt[0:64, 0:512], ps_d[0:64, 0:512], AF.Identity)
                    Wt("sync", "act", av)
                    DMA("sync", d_par[:, tch * 512:(tch + 1) * 512], sqt[0:64, 0:512])
            pe_done[oc % 2] = kb.q["pe"].count

        if PH <= 2:
            return _finish()

        vrb = BG(0, 2048)
        acc = BG(2048, 2048)
        vout = BGr(4096, 2048)
        WS("gpd", sems["pe"], kb.q["pe"].count)
        for ic in range(NIC):
            WS("gpd", sems["sync"], kb.q["sync"].count)
            WS("gpd", sems["dve"], kb.q["dve"].count)
            WS("gpd", sems["act"], kb.q["act"].count)
            lv = DMA("gpd", vrb, d_vr[:, ic * T:(ic + 1) * T])
            WS("dve", sems["gpd"], lv)
            TS2(acc, vrb, dwWs[:, ic * 4 + 3:ic * 4 + 4], dwBs[:, ic:ic + 1],
                OP.mult, OP.add)
            DR("dve")
            for kk in (2, 1, 0):
                sh = 3 - kk
                STT(acc[:, sh:T], vrb[:, 0:T - sh],
                    dwWs[:, ic * 4 + kk:ic * 4 + kk + 1], acc[:, sh:T],
                    OP.mult, OP.add)
                DR("dve")
            dcv = kb.q["dve"].count
            Wt("act", "dve", dcv)
            av = ACTV(vout, acc, AF.Silu)
            Wt("sync", "act", av)
            dv = DMA("sync", d_val[:, ic * T:(ic + 1) * T], vout.bitcast(F32))

        if PH <= 3:
            return _finish()

        valb = BGb(0, 32768)
        bstg = BG(16384, 512)
        WS("gpd", sems["sync"], dv)
        WS("gpd", sems["dve"], kb.q["dve"].count)
        ldv = DMA("gpd", valb[:, 0:8192], d_val[:, 0:8192])
        ldv = DMA("gpd", valb[:, 8192:16384], d_val[:, 8192:16384])
        ldv = DMA("gpd", valb[:, 16384:24576], d_val[:, 16384:24576])
        ldv = DMA("gpd", valb[:, 24576:32768], d_val[:, 24576:32768])
        WS("pe", sems["gpd"], ldv)
        wrow3 = BGb(16896, 1024)
        wrow4 = BGb(17408, 1024)
        pe_done4 = [0, 0]
        lw_par4 = [0, 0]
        for q4 in range(4):
            for gi in range(NG):
                ocol = q4 * NG + gi
                par = ocol % 2
                pair = ((wrowB, wrow2), (wrow3, wrow4))[par]
                WS("gpd", sems["pe"], pe_done4[par])
                for kc in range(16):
                    buf, slot = (pair[0], kc) if kc < 8 else (pair[1], kc - 8)
                    lw = DMA("gpd", buf[:, slot * 128:(slot + 1) * 128],
                             W_bcT[kc * 128:(kc + 1) * 128, ocol * 128:(ocol + 1) * 128])
                lw_par4[par] = lw
                for tch in range(4):
                    WS("pe", sems["gpd"], lw_par4[par])
                    Wt("pe", "dve", kb.q["dve"].count)
                    Wt("pe", "act", kb.q["act"].count)
                    for kc in range(16):
                        buf, slot = (pair[0], kc) if kc < 8 else (pair[1], kc - 8)
                        mmb = MM(ps_d[:, 0:512], buf[:, slot * 128:(slot + 1) * 128],
                                 valb[:, kc * 2048 + tch * 512: kc * 2048 + (tch + 1) * 512],
                                 start=(kc == 0), stop=(kc == 15))
                    Wt("act", "pe", mmb)
                    asq = ACTV(sqt[:, 0:512], ps_d[:, 0:512], AF.Square)
                    Wt("pe", "act", asq)
                    mss = MM(ps_G[0:4, 0:512], sel4s[:], sqt[:, 0:512])
                    Wt("act", "pe", mss)
                    asr = ACTV(fss[:, 0:512], ps_G[0:4, 0:512], AF.Sqrt)
                    Wt("dve", "act", asr)
                    TS1(fss[:, 0:512], fss[:, 0:512], EPS, OP.add)
                    DR("dve")
                    dfr = RCP(fss[:, 0:512], fss[:, 0:512])
                    Wt("pe", "dve", dfr)
                    mbb = MM(ps_G[:, 0:512], sel4Ts[:], fss[:, 0:512])
                    Wt("dve", "pe", mbb)
                    dcp = CPY(fsb[:, 0:512], ps_G[:, 0:512])
                    DR("dve")
                    Wt("dve", "sync", kb.q["sync"].count)
                    dtt = TT(bstg, ps_d[:, 0:512], fsb[:, 0:512])
                    Wt("sync", "dve", dtt)
                    dbc = DMA("sync", d_bc[q4][:, gi * T + tch * 512: gi * T + (tch + 1) * 512],
                              bstg)
                pe_done4[par] = kb.q["pe"].count

        if PH <= 4:
            return _finish()

        def cgAP(off):
            return BG(off, 2048)

        def cgrAP(off):
            return BGr(off, 2048)

        def Ab(col):
            return col.unsqueeze(2).broadcast_to([128, NG, 256])

        def cg3(off):
            return BG(off, 2048).rearrange("p (g t) -> p g t", g=NG)

        def Edat(Ex, c2, sh=0):
            ap = Ex[c2][:].rearrange("p (g s) -> p g s", g=NG)
            return ap[:, :, EPAD - sh: EPAD - sh + NBL]

        def Ebcast(Ex, c2):
            ap = Ex[c2][:].rearrange("p (g s) -> p g s", g=NG)
            return ap[:, :, EPAD:EPAD + NBL].unsqueeze(3).broadcast_to([128, NG, NBL, BL])

        idE3 = idE[:].rearrange("p (g b) -> p g b", g=NG)

        CHCUT = int(os.environ.get("CHCUT", "99"))
        for ch in range(NCH):
            t0 = ch * L
            WS("gpd", sems["dve"], kb.q["dve"].count)
            WS("gpd", sems["pe"], kb.q["pe"].count)
            for q4, off in enumerate((SC_bz, SC_bx, SC_cz, SC_cx)):
                lbc = DMA("gpd", cg3(off),
                          d_bc[q4].rearrange("p (g t) -> p g t", g=NG)[:, :, t0:t0 + L])
            if CHCUT < 1:
                continue
            Wt("sync", "act", kb.q["act"].count)
            Wt("sync", "dve", kb.q["dve"].count)
            lpar = DMA("sync", parsl[:], d_par[:, t0:t0 + L])
            WS("act", sems["sync"], lpar)
            Wt("act", "dve", kb.q["dve"].count)
            Wt("act", "pe", kb.q["pe"].count)
            ax = ACTV(kdsl[:], parsl[0:32, :], AF.Identity, bias=dtb_s[:])
            aab = ACTV(dtsl[:], parsl[0:32, :], AF.Abs, bias=dtb_s[:])
            DR("act")
            ae = ACTV(dtsl[:], dtsl[:], AF.Exp, scale=-1.0)
            DR("act")
            al = ACTV(dtsl[:], dtsl[:], AF.Ln, bias=1.0)
            Wt("dve", "act", al)
            STT(dtsl[:], kdsl[:], 0.0, dtsl[:], OP.max, OP.add)
            DR("dve")
            ddt = TS2(dtsl[:], dtsl[:], DT_MAX, DT_MIN, OP.min, OP.max)
            Wt("act", "dve", ddt)
            akd = ACTV(kdsl[:], parsl[32:64, :], AF.Sigmoid)
            Wt("dve", "act", akd)
            TS1(kdsl[:], kdsl[:], K_MAX)
            DR("dve")
            TT(dt2sl[:], dtsl[:], dtsl[:])
            TT(dk1sl[:], dtsl[:], kdsl[:])
            DR("dve")
            dgr = TS1(dk1sl[:], dk1sl[:], 1.0, OP.add)
            if CHCUT < 2:
                continue
            Wt("pe", "dve", dgr)
            for gi in range(NG):
                mm1 = MM(ps_a[:, 0:256], bcHs[:, gi * 128:(gi + 1) * 128], dtsl[:])
                mm2 = MM(ps_b[:, 0:256], bcHs[:, gi * 128:(gi + 1) * 128], dk1sl[:])
                mm3 = MM(ps_c[:, 0:256], bcHs[:, gi * 128:(gi + 1) * 128], dt2sl[:])
                Wt("dve", "pe", mm3)
                CPY(cgAP(SC_dtc)[:, gi * 256:(gi + 1) * 256], ps_a[:, 0:256])
                CPY(cgAP(SC_S)[:, gi * 256:(gi + 1) * 256], ps_b[:, 0:256])
                DR("dve")
                dbb = STT(cgAP(SC_S)[:, gi * 256:(gi + 1) * 256], ps_c[:, 0:256],
                          A_s[:, gi:gi + 1], cgAP(SC_S)[:, gi * 256:(gi + 1) * 256],
                          OP.mult, OP.add)
                Wt("pe", "dve", dbb)
            DR("dve")
            if CHCUT < 3:
                continue
            for gi in range(NG):
                SCANOP(cgAP(SC_cumI)[:, gi * 256:(gi + 1) * 256],
                       cgAP(SC_S)[:, gi * 256:(gi + 1) * 256],
                       zc[:].broadcast_to([128, 256]), 1.0)
            DR("dve")
            MS("dve", rbsl[:], 1.0)
            MS("dve", idE[:], 1.0)
            DR("dve")
            cum3 = cg3(SC_cumI)
            RCP(rbsl[:].rearrange("p (g b) -> p g b", g=NG)[:, :, 1:NBL],
                cum3[:, :, BL - 1::BL][:, :, 0:NBL - 1])
            CPY(idE3[:, :, 1:NBL], cum3[:, :, BL - 1::BL][:, :, 0:NBL - 1])
            DR("dve")
            TS1(idEn[:], idE[:], -1.0)
            TT(BG(SC_idPw, 2048).rearrange("p (g b i) -> p g b i", g=NG, i=BL),
               BG(SC_cumI, 2048).rearrange("p (g b i) -> p g b i", g=NG, i=BL),
               rbsl[:].rearrange("p (g b) -> p g b", g=NG).unsqueeze(3).broadcast_to([128, NG, NBL, BL]))
            DR("dve")
            if CHCUT < 4:
                continue
            RCP(cgAP(SC_S), cgAP(SC_S))
            DR("dve")
            TT(cgAP(SC_Sdt), cgAP(SC_S), cgAP(SC_dtc))
            DR("dve")
            WS("dve", sems["gpd"], lbc)
            TT(cgAP(SC_bz), cgAP(SC_Sdt), cgAP(SC_bz))
            DR("dve")
            TT(cgAP(SC_bx), cgAP(SC_bz), cgAP(SC_bx), OP.add)
            DR("dve")
            TT(cgAP(SC_bx), cgAP(SC_dtc), cgAP(SC_bx))
            DR("dve")
            TT(cgAP(SC_bx), cgAP(SC_bx), cgAP(SC_idPw))
            TT(cgAP(SC_bz), cgAP(SC_bz), cgAP(SC_idPw))
            DR("dve")
            TT(cg3(SC_m12), cg3(SC_Sdt), Ab(Aneg[:]))
            TT(cgAP(SC_m22), cgAP(SC_Sdt), cgAP(SC_dtc))
            DR("dve")
            TT(cg3(SC_m22), cg3(SC_m22), Ab(Aneg[:]))
            DR("dve")
            dmb = TS1(cgAP(SC_m22), cgAP(SC_m22), 1.0, OP.add)
            if CHCUT < 5:
                continue
            def st_i(off, i):
                return BG(off, 2048).rearrange("p (g b i) -> p (g b) i", g=NG, i=BL)[:, :, i]
            DR("dve")
            CPY(st_i(SC_p11, 0), st_i(SC_S, 0))
            CPY(st_i(SC_p12, 0), st_i(SC_m12, 0))
            DR("dve")
            CPY(st_i(SC_p21, 0), st_i(SC_Sdt, 0))
            CPY(st_i(SC_p22, 0), st_i(SC_m22, 0))
            DR("dve")
            for i in range(1, BL):
                TT(sqt[:, 0:128], st_i(SC_m12, i), st_i(SC_p21, i - 1))
                TT(st_i(SC_p11, i), st_i(SC_S, i), st_i(SC_p11, i - 1))
                TT(sqt[:, 128:256], st_i(SC_m12, i), st_i(SC_p22, i - 1))
                TT(st_i(SC_p12, i), st_i(SC_S, i), st_i(SC_p12, i - 1))
                DR("dve")
                TT(st_i(SC_p11, i), st_i(SC_p11, i), sqt[:, 0:128], OP.add)
                TT(st_i(SC_p12, i), st_i(SC_p12, i), sqt[:, 128:256], OP.add)
                TT(sqt[:, 256:384], st_i(SC_m22, i), st_i(SC_p21, i - 1))
                TT(st_i(SC_p21, i), st_i(SC_Sdt, i), st_i(SC_p11, i - 1))
                TT(sqt[:, 384:512], st_i(SC_m22, i), st_i(SC_p22, i - 1))
                TT(st_i(SC_p22, i), st_i(SC_Sdt, i), st_i(SC_p12, i - 1))
                DR("dve")
                TT(st_i(SC_p21, i), st_i(SC_p21, i), sqt[:, 256:384], OP.add)
                TT(st_i(SC_p22, i), st_i(SC_p22, i), sqt[:, 384:512], OP.add)
                DR("dve")
            if CHCUT < 6:
                continue
            def bend(off):
                return BG(off, 2048).rearrange("p (g b i) -> p g b i", g=NG, i=BL)[:, :, :, BL - 1]
            for c2, off in (("11", SC_p11), ("12", SC_p12), ("21", SC_p21), ("22", SC_p22)):
                CPY(Edat(E1, c2)[:, :, 1:NBL], bend(off)[:, :, 0:NBL - 1])
            DR("dve")
            MS("dve", Edat(E1, "11")[:, :, 0:1], 1.0)
            MS("dve", Edat(E1, "12")[:, :, 0:1], 0.0)
            MS("dve", Edat(E1, "21")[:, :, 0:1], 0.0)
            MS("dve", Edat(E1, "22")[:, :, 0:1], 1.0)
            DR("dve")
            src_, dst_ = E1, E2
            for sh in (1, 2, 4, 8):
                TT(Edat(dst_, "11"), Edat(src_, "11"), Edat(src_, "11", sh))
                TT(sqt[:, 0:128], Edat(src_, "12"), Edat(src_, "21", sh))
                TT(Edat(dst_, "12"), Edat(src_, "11"), Edat(src_, "12", sh))
                TT(sqt[:, 128:256], Edat(src_, "12"), Edat(src_, "22", sh))
                DR("dve")
                TT(Edat(dst_, "11"), Edat(dst_, "11"), sqt[:, 0:128], OP.add)
                TT(Edat(dst_, "12"), Edat(dst_, "12"), sqt[:, 128:256], OP.add)
                TT(Edat(dst_, "21"), Edat(src_, "21"), Edat(src_, "11", sh))
                TT(sqt[:, 256:384], Edat(src_, "22"), Edat(src_, "21", sh))
                DR("dve")
                TT(Edat(dst_, "21"), Edat(dst_, "21"), sqt[:, 256:384], OP.add)
                TT(Edat(dst_, "22"), Edat(src_, "21"), Edat(src_, "12", sh))
                TT(sqt[:, 384:512], Edat(src_, "22"), Edat(src_, "22", sh))
                DR("dve")
                dE = TT(Edat(dst_, "22"), Edat(dst_, "22"), sqt[:, 384:512], OP.add)
                DR("dve")
                src_, dst_ = dst_, src_
            Ecum = src_
            def Eslot(Ex, c2, b):
                return Ex[c2][:].rearrange("p (g s) -> p g s", g=NG)[:, :, EPAD + b]
            TT(PLt["11"][:], bend(SC_p11)[:, :, NBL - 1], Eslot(Ecum, "11", NBL - 1))
            TT(sqt[:, 0:8], bend(SC_p12)[:, :, NBL - 1], Eslot(Ecum, "21", NBL - 1))
            TT(PLt["12"][:], bend(SC_p11)[:, :, NBL - 1], Eslot(Ecum, "12", NBL - 1))
            TT(sqt[:, 8:16], bend(SC_p12)[:, :, NBL - 1], Eslot(Ecum, "22", NBL - 1))
            TT(PLt["21"][:], bend(SC_p21)[:, :, NBL - 1], Eslot(Ecum, "11", NBL - 1))
            TT(sqt[:, 16:24], bend(SC_p22)[:, :, NBL - 1], Eslot(Ecum, "21", NBL - 1))
            TT(PLt["22"][:], bend(SC_p21)[:, :, NBL - 1], Eslot(Ecum, "12", NBL - 1))
            TT(sqt[:, 24:32], bend(SC_p22)[:, :, NBL - 1], Eslot(Ecum, "22", NBL - 1))
            DR("dve")
            TT(PLt["11"][:], PLt["11"][:], sqt[:, 0:8], OP.add)
            TT(PLt["12"][:], PLt["12"][:], sqt[:, 8:16], OP.add)
            TT(PLt["21"][:], PLt["21"][:], sqt[:, 16:24], OP.add)
            TT(PLt["22"][:], PLt["22"][:], sqt[:, 24:32], OP.add)
            DR("dve")
            for ci, c2 in enumerate(("11", "12", "21", "22")):
                for hs in range(4):
                    CPY(PL4[:, [[1, 1], [4, NG]]].copy() if False else
                        bass.AP(PL4.tensor if False else PL4[:].tensor, 0, [[0, 0]]) if False else
                        PL4[0:32, ci * H + hs: ci * H + hs + 29:4] if False else
                        PL4[0:32, :].rearrange("p (c h) -> p c h", c=4)[:, ci, hs::4],
                        PLt[c2][hs * 32:(hs + 1) * 32, :])
            DR("dve")
            TT(idE3 if False else BE["11"][:].rearrange("p (g b) -> p g b", g=NG),
               Edat(Ecum, "11"), idE3)
            TT(BE["12"][:].rearrange("p (g b) -> p g b", g=NG), Edat(Ecum, "12"),
               idEn[:].rearrange("p (g b) -> p g b", g=NG))
            TT(BE["21"][:].rearrange("p (g b) -> p g b", g=NG), Edat(Ecum, "21"),
               idEn[:].rearrange("p (g b) -> p g b", g=NG))
            dbe = TT(BE["22"][:].rearrange("p (g b) -> p g b", g=NG), Edat(Ecum, "22"), idE3)
            DR("dve")
            if CHCUT < 7:
                continue
            TT(cgAP(SC_qt1), cgAP(SC_p11), cgAP(SC_cz))
            TT(sqt[:, 0:0] if False else cgAP(SC_tq1), cgAP(SC_p21), cgAP(SC_cx))
            DR("dve")
            TT(cgAP(SC_qt1), cgAP(SC_qt1), cgAP(SC_tq1), OP.add)
            TT(cgAP(SC_qt2), cgAP(SC_p12), cgAP(SC_cz))
            TT(cgAP(SC_tq2), cgAP(SC_p22), cgAP(SC_cx))
            DR("dve")
            dqt = TT(cgAP(SC_qt2), cgAP(SC_qt2), cgAP(SC_tq2), OP.add)
            DR("dve")
            TT(cgAP(SC_cz), cgAP(SC_p22), cgAP(SC_bz))
            TT(cgAP(SC_tq1), cgAP(SC_p12), cgAP(SC_bx))
            TT(cgAP(SC_cx), cgAP(SC_p11), cgAP(SC_bx))
            TT(cgAP(SC_tq2), cgAP(SC_p21), cgAP(SC_bz))
            DR("dve")
            TT(cgAP(SC_cz), cgAP(SC_cz), cgAP(SC_tq1), OP.subtract)
            dkt = TT(cgAP(SC_cx), cgAP(SC_cx), cgAP(SC_tq2), OP.subtract)
            DR("dve")
            if CHCUT < 8:
                continue
            WS("gpd", sems["dve"], kb.q["dve"].count)
            for ic in range(NIC):
                WS("gpd", sems["pe"], kb.q["pe"].count)
                lvv = DMA("gpd", vring[:],
                          d_val[:, ic * T + t0: ic * T + t0 + L])
                for tb in range(2):
                    WS("pe", sems["gpd"], lvv)
                    Wt("pe", "dve", kb.q["dve"].count)
                    ptr = TRN(ps_d[:, 0:128], vring[:, tb * 128:(tb + 1) * 128],
                              identf[:])
                    Wt("dve", "pe", ptr)
                    dU = CPY(Ust[:, (ic * 2 + tb) * 128:(ic * 2 + tb) * 128 + 128],
                             ps_d[:, 0:128])
                    DR("dve")
                if ic < NIC - 1:
                    WS("gpd", sems["dve"], dU)
            if CHCUT < 9:
                continue
            for hh in range(2):
                Wt("dve", "pe", kb.q["pe"].count)
                TT(cg3(SC_tq1).rearrange("p g t -> p (g t)").rearrange("p (g b i) -> p g b i", g=NG, i=BL) if False else
                   BG(SC_tq1, 2048).rearrange("p (g b i) -> p g b i", g=NG, i=BL),
                   BG(SC_qt1, 2048).rearrange("p (g b i) -> p g b i", g=NG, i=BL),
                   Ebcast(Ecum, "11"))
                TT(BG(SC_tq2, 2048).rearrange("p (g b i) -> p g b i", g=NG, i=BL),
                   BG(SC_qt2, 2048).rearrange("p (g b i) -> p g b i", g=NG, i=BL),
                   Ebcast(Ecum, "21"))
                DR("dve")
                for hs in range(4):
                    TT(Qst[0:32, :].rearrange("p (hl t) -> p hl t", t=L)[:, hs::4, :],
                       BG(SC_tq1, 2048)[hs * 32:(hs + 1) * 32, :].rearrange(
                           "p (g t) -> p g t", t=L)[:, 4 * hh:4 * hh + 4, :],
                       BG(SC_tq2, 2048)[hs * 32:(hs + 1) * 32, :].rearrange(
                           "p (g t) -> p g t", t=L)[:, 4 * hh:4 * hh + 4, :], OP.add)
                DR("dve")
                TT(BG(SC_tq1, 2048).rearrange("p (g b i) -> p g b i", g=NG, i=BL),
                   BG(SC_qt1, 2048).rearrange("p (g b i) -> p g b i", g=NG, i=BL),
                   Ebcast(Ecum, "12"))
                TT(BG(SC_tq2, 2048).rearrange("p (g b i) -> p g b i", g=NG, i=BL),
                   BG(SC_qt2, 2048).rearrange("p (g b i) -> p g b i", g=NG, i=BL),
                   Ebcast(Ecum, "22"))
                DR("dve")
                for hs in range(4):
                    TT(Qst[32:64, :].rearrange("p (hl t) -> p hl t", t=L)[:, hs::4, :],
                       BG(SC_tq1, 2048)[hs * 32:(hs + 1) * 32, :].rearrange(
                           "p (g t) -> p g t", t=L)[:, 4 * hh:4 * hh + 4, :],
                       BG(SC_tq2, 2048)[hs * 32:(hs + 1) * 32, :].rearrange(
                           "p (g t) -> p g t", t=L)[:, 4 * hh:4 * hh + 4, :], OP.add)
                DR("dve")
                def BEb(c2):
                    return BE[c2][:].rearrange("p (g b) -> p g b", g=NG).unsqueeze(3).broadcast_to([128, NG, NBL, BL])
                TT(BG(SC_tq1, 2048).rearrange("p (g b i) -> p g b i", g=NG, i=BL),
                   BG(SC_cz, 2048).rearrange("p (g b i) -> p g b i", g=NG, i=BL), BEb("22"))
                TT(BG(SC_tq2, 2048).rearrange("p (g b i) -> p g b i", g=NG, i=BL),
                   BG(SC_cx, 2048).rearrange("p (g b i) -> p g b i", g=NG, i=BL), BEb("12"))
                DR("dve")
                for hs in range(4):
                    TT(Kst[0:32, :].rearrange("p (hl t) -> p hl t", t=L)[:, hs::4, :],
                       BG(SC_tq1, 2048)[hs * 32:(hs + 1) * 32, :].rearrange(
                           "p (g t) -> p g t", t=L)[:, 4 * hh:4 * hh + 4, :],
                       BG(SC_tq2, 2048)[hs * 32:(hs + 1) * 32, :].rearrange(
                           "p (g t) -> p g t", t=L)[:, 4 * hh:4 * hh + 4, :], OP.add)
                DR("dve")
                TT(BG(SC_tq1, 2048).rearrange("p (g b i) -> p g b i", g=NG, i=BL),
                   BG(SC_cz, 2048).rearrange("p (g b i) -> p g b i", g=NG, i=BL), BEb("21"))
                TT(BG(SC_tq2, 2048).rearrange("p (g b i) -> p g b i", g=NG, i=BL),
                   BG(SC_cx, 2048).rearrange("p (g b i) -> p g b i", g=NG, i=BL), BEb("11"))
                DR("dve")
                for hs in range(4):
                    dasm = TT(Kst[32:64, :].rearrange("p (hl t) -> p hl t", t=L)[:, hs::4, :],
                              BG(SC_tq1, 2048)[hs * 32:(hs + 1) * 32, :].rearrange(
                                  "p (g t) -> p g t", t=L)[:, 4 * hh:4 * hh + 4, :],
                              BG(SC_tq2, 2048)[hs * 32:(hs + 1) * 32, :].rearrange(
                                  "p (g t) -> p g t", t=L)[:, 4 * hh:4 * hh + 4, :], OP.add)
                DR("dve")
                HH = int(os.environ.get("HHCUT", "99"))
                for hl in range(16):
                    hgl = hh * 16 + hl
                    ic = hgl // 2
                    if HH < 2:
                        continue
                    Wt("pe", "dve", dasm)
                    Wt("pe", "act", kb.q["act"].count)
                    mmy = MM(ps_y[0:64, 0:L], Xfull[0:64, hgl * 64:(hgl + 1) * 64],
                             Qst[0:64, hl * L:(hl + 1) * L], start=True, stop=False)
                    if HH < 3:
                        continue
                    for j in (1, 0):
                        Wt("pe", "dve", kb.q["dve"].count)
                        Wt("pe", "act", kb.q["act"].count)
                        mmg = MM(ps_G[:, 0:L], Kst[0:64, hl * L + j * 128: hl * L + j * 128 + 128],
                                 Qst[0:64, hl * L:(hl + 1) * L], start=True, stop=True)
                        if HH < 4:
                            continue
                        Wt("act", "pe", mmg)
                        Wt("act", "gpd", kb.q["gpd"].count)
                        if j == 0:
                            aG = ACTV(Gsb[:, 0:256], ps_G[:, 0:256], AF.Identity)
                        else:
                            aG = ACTV(Gsb[:, 128:256], ps_G[:, 128:256], AF.Identity)
                        WS("gpd", sems["act"], aG)
                        dG = kb.on("gpd", lambda e, j=j: e.affine_select(
                            Gsb[:, j * 128:(j + 1) * 128], Gsb[:, j * 128:(j + 1) * 128],
                            [[1, 128]], OP.is_ge, 0.0, base=0, channel_multiplier=-1))
                        if HH < 5:
                            continue
                        WS("pe", sems["gpd"], dG)
                        mmy = MM(ps_y[0:64, j * 128:L],
                                 Ust[:, (ic * 2 + j) * 128 + (hgl % 2) * 64:
                                                 (ic * 2 + j) * 128 + (hgl % 2) * 64 + 64],
                                 Gsb[:, j * 128:L], start=False, stop=(j == 0))
                    if HH < 6:
                        continue
                    Wt("act", "pe", mmy)
                    Wt("act", "dve", kb.q["dve"].count)
                    ay = ACTV(ysb[(hgl % 2) * 64:(hgl % 2) * 64 + 64, 0:L], ps_y[0:64, 0:L],
                              AF.Identity)
                    if hgl % 2 == 1:
                        WS("gpd", sems["dve"], kb.q["dve"].count)
                        lvv = DMA("gpd", vring[:], d_val[:, ic * T + t0: ic * T + t0 + L])
                        lgg = DMA("gpd", gring[:], d_gate[:, ic * T + t0: ic * T + t0 + L])
                        Wt("dve", "act", ay)
                        WS("dve", sems["gpd"], lgg)
                        STT(sqt[:, 0:L], vring[:], skips[:, ic:ic + 1],
                            ysb[:, 0:L], OP.mult, OP.add)
                        DR("dve")
                        Wt("dve", "sync", kb.q["sync"].count)
                        dyg = TT(ygbf[:], sqt[:, 0:L], gring[:])
                        Wt("sync", "dve", dyg)
                        DMA("sync", d_yg[:, ic * T + t0: ic * T + t0 + L], ygbf[:])
                for hl in (range(16) if HH >= 7 else ()):
                    hgl = hh * 16 + hl
                    for j in range(2):
                        Wt("pe", "dve", kb.q["dve"].count)
                        ptr = TRN(ps_d[:, 0:64],
                                  Kst[0:64, hl * L + j * 128: hl * L + j * 128 + 128],
                                  identf[0:64, 0:64])
                        Wt("dve", "pe", ptr)
                        dKT = CPY(KTt[:], ps_d[:, 0:64])
                        Wt("pe", "dve", dKT)
                        MM(ps_k0[0:32, (hgl % 8) * 64:(hgl % 8) * 64 + 64],
                           KTt[:, 0:32],
                           Ust[:, (ic * 2 + j) * 128 + (hgl % 2) * 64:
                                           (ic * 2 + j) * 128 + (hgl % 2) * 64 + 64]
                           if False else
                           Ust[:, ((hgl // 2) * 2 + j) * 128 + (hgl % 2) * 64:
                                           ((hgl // 2) * 2 + j) * 128 + (hgl % 2) * 64 + 64],
                           start=(j == 0), stop=(j == 1))
                        mku = MM(ps_k1[0:32, (hgl % 8) * 64:(hgl % 8) * 64 + 64],
                                 KTt[:, 32:64],
                                 Ust[:, ((hgl // 2) * 2 + j) * 128 + (hgl % 2) * 64:
                                                 ((hgl // 2) * 2 + j) * 128 + (hgl % 2) * 64 + 64],
                                 start=(j == 0), stop=(j == 1))
                    if hgl % 8 == 7:
                        grp = hgl // 8
                        Wt("dve", "pe", mku)
                        TT(ktuz[:, grp * 512:(grp + 1) * 512], ps_k0[0:32, 0:512],
                           Xz[:, grp * 512:(grp + 1) * 512], OP.add)
                        dku = TT(ktus[:, grp * 512:(grp + 1) * 512], ps_k1[0:32, 0:512],
                                 Xs_[:, grp * 512:(grp + 1) * 512], OP.add)
                        DR("dve")
                        Wt("pe", "dve", dku)
            if CHCUT < 10:
                continue
            def PLb(ci):
                return PL4[0:32, :].rearrange("p (c h) -> p c h", c=4)[:, ci, :].unsqueeze(2).broadcast_to([32, H, DH])
            def hd3(ap):
                return ap.rearrange("p (h d) -> p h d", d=DH)
            TT(hd3(Xz), hd3(ktuz), PLb(0))
            TT(hd3(tmpX), hd3(ktus), PLb(1))
            DR("dve")
            TT(Xz, Xz, tmpX, OP.add)
            TT(hd3(Xs_), hd3(ktuz), PLb(2))
            DR("dve")
            TT(hd3(tmpX), hd3(ktus), PLb(3))
            DR("dve")
            TT(Xs_, Xs_, tmpX, OP.add)
            DR("dve")
            CPY(Xfull[0:32, :], Xz)
            dXf = CPY(Xfull[32:64, :], Xs_)
            DR("dve")

        if PH <= 5:
            return _finish()

        wob = BGb(0, 16384)
        woTb = BGb(8192, 8192)
        ygl = BGb(12288, 4096)
        xnl = BGb(14336, 2048)
        xt_t = BG(16384, 1024)
        uo_t = BG(17408, 1024)
        ht_t = BG(18432, 1024)
        g_t = BG(19456, 1024)
        o_t = BG(20480, 1024)
        WS("gpd", sems["pe"], kb.q["pe"].count)
        WS("gpd", sems["dve"], kb.q["dve"].count)
        for kc in range(16):
            lwo = DMA("gpd", wob[:, kc * 1024:(kc + 1) * 1024],
                      W_outT[kc * 128:(kc + 1) * 128, :])
        for kc in range(8):
            lwo = DMA("gpd", woTb[:, kc * 1024:(kc + 1) * 1024],
                      wT["wo"][kc * 128:(kc + 1) * 128, :])
        WS("dve", sems["cc"], 1)
        WS("sync", sems["cc"], 1)
        DR("dve")
        for tt in range(16):
            WS("gpd", sems["pe"], kb.q["pe"].count)
            for kc in range(16):
                lyg = DMA("gpd", ygl[:, kc * 128:(kc + 1) * 128],
                          d_yg[:, kc * T + tt * 128: kc * T + (tt + 1) * 128])
            for kc in range(8):
                lxn = DMA("gpd", xnl[:, kc * 128:(kc + 1) * 128],
                          d_xn[:, kc * T + tt * 128: kc * T + (tt + 1) * 128])
            Wt("sync", "dve", kb.q["dve"].count)
            lx = DMA("sync", xt_t, x_ssm[tt * 128:(tt + 1) * 128, :])
            lu = DMA("sync", uo_t, cc_out[tt * 128:(tt + 1) * 128, :])
            WS("pe", sems["gpd"], lyg if False else kb.q["gpd"].count)
            Wt("pe", "dve", kb.q["dve"].count)
            for oc in range(2):
                ps = (ps_a, ps_b)[oc]
                for kc in range(16):
                    mms = MM(ps[:, 0:512], ygl[:, kc * 128:(kc + 1) * 128],
                             wob[:, kc * 1024 + oc * 512: kc * 1024 + (oc + 1) * 512],
                             start=(kc == 0), stop=(kc == 15))
            for oc in range(2):
                ps = (ps_c, ps_d)[oc]
                for kc in range(8):
                    mmh = MM(ps[:, 0:512], xnl[:, kc * 128:(kc + 1) * 128],
                             woTb[:, kc * 1024 + oc * 512: kc * 1024 + (oc + 1) * 512],
                             start=(kc == 0), stop=(kc == 7))
            Wt("dve", "pe", mmh)
            WS("dve", sems["sync"], lu)
            TT(ht_t, uo_t, bo_b[:], OP.add)
            DR("dve")
            TT(ht_t[:, 0:512], ht_t[:, 0:512], ps_c[:, 0:512], OP.add)
            TT(ht_t[:, 512:1024], ht_t[:, 512:1024], ps_d[:, 0:512], OP.add)
            DR("dve")
            dht = kb.q["dve"].count
            Wt("act", "dve", dht)
            ah = ACTV(g_t, ht_t, AF.Tanh)
            DR("act")
            ah = ACTV(g_t, g_t, AF.Silu)
            Wt("dve", "act", ah)
            TT(ht_t[:, 0:512], ps_a[:, 0:512], g_t[:, 0:512])
            TT(ht_t[:, 512:1024], ps_b[:, 0:512], g_t[:, 512:1024])
            DR("dve")
            Wt("dve", "sync", kb.q["sync"].count)
            dout_ = TT(o_t, xt_t, ht_t, OP.add)
            Wt("sync", "dve", dout_)
            dfin = DMA("sync", out[tt * 128:(tt + 1) * 128, :], o_t)
        return _finish()

    return nc


_NC_CACHE = [None]


def kernel(**inputs):
    if _NC_CACHE[0] is None:
        _NC_CACHE[0] = build_program()
    nc = _NC_CACHE[0]
    in_maps = []
    for core in range(B):
        g = _host_arrays(inputs, core)
        bcH = np.zeros((32, NG * 128), dtype=np.float32)
        for gi in range(NG):
            for hs in range(4):
                bcH[4 * gi + hs, gi * 128 + hs * 32: gi * 128 + (hs + 1) * 32] = 1.0
        g["bcH"] = bcH
        in_maps.append(g)
    res = run_bass_kernel_spmd(nc, in_maps, core_ids=list(range(B)))
    return np.stack([res.results[c]["out"] for c in range(B)], axis=0)


# version 13 (final)
# speedup vs baseline: 1.1511x; 1.0344x over previous
import os
os.environ.setdefault("BASS_INTERP_NUM_WORKERS", "1")
import contextlib
import numpy as np

import concourse.bass as bass
import concourse.mybir as mybir
from concourse.bass_utils import run_bass_kernel_spmd

F32 = mybir.dt.float32
F32R = mybir.dt.float32r
BF16 = mybir.dt.bfloat16
AF = mybir.ActivationFunctionType
OP = mybir.AluOpType

B = 8; T = 2048; D = 1024; DI = 2048; H = 32; DH = 64
NG = 8; L = 256; NCH = T // L; BL = 16; NBL = L // BL
NIC = DI // 128
ROWS_F = 256
DT_MIN, DT_MAX, K_MAX, EPS = 1e-4, 0.1, 0.5, 1e-8
LN_EPS = 1e-5
EPAD = 8; EW = EPAD + NBL


def _host_arrays(inputs, core):
    W_in = inputs["L0_attn_W_in"]
    perm = np.arange(4160)
    perm[2 * DI:2 * DI + H] = 2 * DI + 2 * np.arange(H)
    perm[2 * DI + H:2 * DI + 2 * H] = 2 * DI + 2 * np.arange(H) + 1
    W_inT = np.ascontiguousarray(W_in[perm].T)

    W_bc = inputs["L0_attn_W_bc"]
    o = np.arange(4096)
    hh, qq, nn = o // 128, (o % 128) // 32, o % 32
    pos = (qq * NG + hh // 4) * 128 + (hh % 4) * 32 + nn
    new = np.zeros(4096, dtype=np.int64); new[pos] = o
    W_bcT = np.ascontiguousarray(W_bc[new].T)

    g = {
        "x_ssm": np.ascontiguousarray(inputs["L0_attn_x"][core]),
        "x_ffn": np.ascontiguousarray(
            inputs["L0_attn_x"][:, core * ROWS_F:(core + 1) * ROWS_F, :].reshape(T, D)),
        "W_inT": W_inT,
        "W_bcT": W_bcT,
        "W_outT": np.ascontiguousarray(inputs["L0_attn_W_out"].T),
        "dwW": np.ascontiguousarray(
            inputs["L0_attn_dw_weight"].reshape(NIC, 128, 4).transpose(1, 0, 2).reshape(128, NIC * 4)),
        "dwB": np.ascontiguousarray(inputs["L0_attn_dw_bias"].reshape(NIC, 128).T),
        "skipc": np.ascontiguousarray(inputs["L0_attn_skip"].reshape(NIC, 128).T),
        "A_log": np.ascontiguousarray(inputs["L0_attn_A_log"].reshape(NG, 128).T),
        "dt_bias": np.ascontiguousarray(inputs["L0_attn_dt_bias"].reshape(H, 1)),
        "bo": inputs["L0_ffn_bo"].reshape(1, D),
        "in_c": np.ascontiguousarray(inputs["L0_ffn_in_c"].reshape(8, 128).T),
    }
    for nm in ("wf", "wi", "wc", "uf", "ui", "uc", "wo", "uo"):
        g[nm + "T"] = np.ascontiguousarray(inputs["L0_ffn_" + nm].T)
    for nm in ("bf", "bi", "bc"):
        g["b" + nm[1]] = np.ascontiguousarray(inputs["L0_ffn_" + nm].reshape(8, 128).T)
    g["ident"] = np.eye(128, dtype=np.float32)
    g["trilm"] = np.triu(np.ones((128, 128), dtype=np.float32))
    sel4 = np.zeros((128, 4), dtype=np.float32)
    for hs in range(4):
        sel4[hs * 32:(hs + 1) * 32, hs] = 1.0
    g["sel4"] = sel4
    g["sel4T"] = np.ascontiguousarray(sel4.T)
    g["ones1"] = np.ones((1, 128), dtype=np.float32)
    g["onescol"] = np.ones((128, 1), dtype=np.float32)
    return g


class _Q:
    def __init__(self, step):
        self.ops = []; self.count = 0; self.step = step; self.sem = None


class KB:
    def __init__(self):
        self.q = {n: _Q(16 if n in ("sync", "gpd") else 1)
                  for n in ("sync", "gpd", "pe", "dve", "act", "gp")}

    def on(self, qn, fn):
        eq = self.q[qn]
        if eq.step == 16 and eq.count > 0:
            eq.ops.append(lambda e, s=eq, v=eq.count: e.wait_ge(s.sem, v))
        eq.ops.append(lambda e, f=fn, eq=eq: f(e).then_inc(eq.sem, eq.step))
        eq.count += eq.step
        return eq.count

    def drain(self, qn):
        self.q[qn].ops.append(lambda e: e.drain())

    def wait(self, qn, prod, val):
        if val > 0:
            self.q[qn].ops.append(lambda e, s=self.q[prod], v=val: e.wait_ge(s.sem, v))

    def wait_sem(self, qn, sem, val):
        self.q[qn].ops.append(lambda e, s=sem, v=val: e.wait_ge(s, v))


def build_program():
    PH = int(os.environ.get("KPHASE", "9"))
    nc = bass.Bass("TRN2", target_bir_lowering=False)

    def din(name, shape, dtype=F32):
        return nc.dram_tensor(name, list(shape), dtype, kind="ExternalInput").ap()

    x_ssm = din("x_ssm", (T, D)); x_ffn = din("x_ffn", (T, D))
    W_inT = din("W_inT", (D, 4160)); W_bcT = din("W_bcT", (DI, 4096))
    W_outT = din("W_outT", (DI, D))
    wT = {nm: din(nm + "T", (D, D)) for nm in ("wf", "wi", "wc", "uf", "ui", "uc", "wo", "uo")}
    dwW = din("dwW", (128, NIC * 4)); dwB = din("dwB", (128, NIC))
    skipc = din("skipc", (128, NIC))
    A_log = din("A_log", (128, NG)); dt_bias = din("dt_bias", (H, 1))
    b_in = {nm: din("b" + nm, (128, 8)) for nm in ("f", "i", "c")}
    bo = din("bo", (1, D)); in_c = din("in_c", (128, 8))
    ident_in = din("ident", (128, 128)); tril_in = din("trilm", (128, 128))
    sel4_in = din("sel4", (128, 4)); sel4T_in = din("sel4T", (4, 128))
    ones1_in = din("ones1", (1, 128))
    onescol_in = din("onescol", (128, 1))

    out = nc.dram_tensor("out", [T, D], F32, kind="ExternalOutput").ap()
    cc_in = nc.dram_tensor("cc_in", [ROWS_F, D], F32).ap()
    cc_out = nc.dram_tensor("cc_out", [T, D], F32).ap()
    d_xn = nc.dram_tensor("d_xn", [128, 8 * T], BF16).ap()
    d_xnf = nc.dram_tensor("d_xnf", [128, 8 * T], BF16).ap()
    d_gate = nc.dram_tensor("d_gate", [128, NIC * T], F32).ap()
    d_vr = nc.dram_tensor("d_vr", [128, NIC * T], F32).ap()
    d_val = nc.dram_tensor("d_val", [128, NIC * T], F32).ap()
    d_par = nc.dram_tensor("d_par", [64, T], F32).ap()
    d_bc = nc.dram_tensor("d_bc", [4, 128, NG * T], F32).ap()
    d_yg = nc.dram_tensor("d_yg", [128, NIC * T], BF16).ap()
    xw_d = nc.dram_tensor("xw_d", [3, 128, 8 * T], BF16).ap()

    es = contextlib.ExitStack()
    with es:
        def sb(name, p, fr, dtype=F32):
            return es.enter_context(nc.sbuf_tensor(name, [p, fr], dtype))

        identf = sb("identf", 128, 128)
        identr = sb("identr", 128, 128, F32R)
        identb = sb("identb", 128, 128, BF16)
        trils = sb("trils", 128, 128)
        sel4s = sb("sel4s", 128, 4); sel4Ts = sb("sel4Ts", 4, 128)
        ones1s = sb("ones1s", 1, 128)
        onescol = sb("onescol_s", 128, 1)
        zc = sb("zc", 128, 1)
        dwWs = sb("dwWs", 128, NIC * 4); dwBs = sb("dwBs", 128, NIC)
        skips = sb("skips", 128, NIC)
        A_s = sb("A_s", 128, NG); Aneg = sb("Aneg", 128, NG)
        dtb_s = sb("dtb_s", H, 1)
        bias_s = {nm: sb("bias" + nm, 128, 8) for nm in ("f", "i", "c")}
        bo_r = sb("bo_r", 1, D); bo_b = sb("bo_b", 128, D)
        inc_s = sb("inc_s", 128, 8)
        dtsl = sb("dtsl", 32, L); kdsl = sb("kdsl", 32, L)
        dt2sl = sb("dt2sl", 32, L); dk1sl = sb("dk1sl", 32, L)
        parsl = sb("parsl", 64, L)
        rbsl = sb("rbsl", 128, NG * NBL)
        PL4 = sb("PL4", 32, 4 * H)
        PLt = {c2: sb("PLt" + c2, 128, NG) for c2 in ("11", "12", "21", "22")}
        E1 = {c2: sb("E1" + c2, 128, NG * EW) for c2 in ("11", "12", "21", "22")}
        E2 = {c2: sb("E2" + c2, 128, NG * EW) for c2 in ("11", "12", "21", "22")}
        BE = {c2: sb("BE" + c2, 128, NG * NBL) for c2 in ("11", "12", "21", "22")}
        idE = sb("idE", 128, NG * NBL)
        xln = sb("xln", 128, D)
        xln2 = sb("xln2", 128, D)
        xnbf = sb("xnbf", 128, D, BF16)
        ygbf = sb("ygbf", 128, L, BF16)
        stat = sb("stat", 128, 4)
        sqt = sb("sqt", 128, 512)
        fss = sb("fss", 4, 512); fsb = sb("fsb", 128, 512)
        Gsb = sb("Gsb", 128, L)
        KTt = sb("KTt", 128, DH)
        ysb = sb("ysb", 128, L)
        winB = sb("winB", 128, 8 * 128, BF16)
        wrowB = sb("wrowB", 128, D, BF16)
        wrow2 = sb("wrow2", 128, D, BF16)
        xwr = sb("xwr", 128, 8 * ROWS_F, BF16)
        vring = sb("vring", 128, L)
        gring = sb("gring", 128, L)

        AW = 36864
        BIG_OFF = 81856
        bigh = nc.alloc_sbuf_tensor_at("big", [128, AW], F32, offset=BIG_OFF)
        big = bigh.ap()

        def BG(off, w):
            assert off + w <= AW, (off, w)
            return big[:, off:off + w]

        def BGb(off, wbf):
            assert off + wbf // 2 <= AW
            return big[:, off:off + wbf // 2].bitcast(BF16)

        def BGr(off, w):
            return big[:, off:off + w].bitcast(F32R)

        SC_bz, SC_bx, SC_cz, SC_cx = 0, 2048, 4096, 6144
        SC_dtc, SC_S, SC_Sdt, SC_m12, SC_m22 = 8192, 10240, 12288, 14336, 16384
        SC_p11, SC_p12, SC_p21, SC_p22 = 18432, 20480, 22528, 24576
        SC_cumI, SC_idPw = 26624, 28672
        SC_qt1, SC_qt2 = 26624, 28672
        SC_tq1, SC_tq2 = 14336, 16384
        SC_U = 8192
        SC_Xz, SC_Xs, SC_Xf = 30720, 32768, 34816

        Xz = big[0:32, SC_Xz:SC_Xz + 2048]
        Xs_ = big[0:32, SC_Xs:SC_Xs + 2048]
        Xfull = nc.alloc_sbuf_tensor_at(
            "Xfull", [64, 2048], F32, offset=BIG_OFF + SC_Xf * 4).ap()
        ktuz = big[0:32, SC_bz:SC_bz + 2048]
        ktus = big[0:32, SC_bx:SC_bx + 2048]
        tmpX = big[0:32, SC_tq1:SC_tq1 + 2048]
        Qst = nc.alloc_sbuf_tensor_at(
            "Qst", [64, 4096], F32, offset=BIG_OFF + SC_p11 * 4).ap()
        Kst = nc.alloc_sbuf_tensor_at(
            "Kst", [64, 4096], F32, offset=BIG_OFF + SC_p21 * 4).ap()
        Ust = nc.alloc_sbuf_tensor_at(
            "Ust", [128, 4096], F32, offset=BIG_OFF + SC_U * 4).ap()
        xwr3 = BGb(20480, 6144)
        cst = BG(23552, 2048)
        cbf = BGb(25600, 4096)
        cbf2 = BGb(27648, 4096)
        gateT = BG(29696, 768)

        ps_a = es.enter_context(nc.psum_tensor("ps_a", [128, 512], F32))
        ps_b = es.enter_context(nc.psum_tensor("ps_b", [128, 512], F32))
        ps_c = es.enter_context(nc.psum_tensor("ps_c", [128, 512], F32))
        ps_d = es.enter_context(nc.psum_tensor("ps_d", [128, 512], F32))
        ps_G = es.enter_context(nc.psum_tensor("ps_G", [128, 512], F32))
        ps_y = es.enter_context(nc.psum_tensor("ps_y", [64, 512], F32))
        ps_k0 = es.enter_context(nc.psum_tensor("ps_k0", [32, 512], F32))
        ps_k1 = es.enter_context(nc.psum_tensor("ps_k1", [32, 512], F32))

        kb = KB()
        sems = {}
        for nm in ("sync", "gpd", "pe", "dve", "act", "gp", "cc"):
            sems[nm] = es.enter_context(nc.semaphore(nm))
        for nm in ("sync", "gpd", "pe", "dve", "act", "gp"):
            kb.q[nm].sem = sems[nm]

        blkctx = es.enter_context(nc.Block())

        def DMA(qn, dst, src):
            return kb.on(qn, lambda e: e.dma_start(out=dst, in_=src))

        def MM(dst, lhsT, rhs, start=True, stop=True):
            return kb.on("pe", lambda e: e.matmul(dst, lhsT, rhs, start=start, stop=stop))

        def TRN(dst, src, ident):
            return kb.on("pe", lambda e: e.transpose(dst, src, ident))

        def TT(dst, a, b, op=OP.mult, q="dve"):
            return kb.on(q, lambda e: e.tensor_tensor(dst, a, b, op))

        def TS1(dst, a, s1, op=OP.mult, q="dve"):
            return kb.on(q, lambda e: e.tensor_single_scalar(dst, a, s1, op))

        def TS2(dst, a, s1, s2, op0, op1, q="dve"):
            return kb.on(q, lambda e: e.tensor_scalar(dst, a, s1, s2, op0, op1))

        def STT(dst, in0, scalar, in1, op0=OP.mult, op1=OP.add, q="dve"):
            return kb.on(q, lambda e: e.scalar_tensor_tensor(dst, in0, scalar, in1, op0, op1))

        def CPY(dst, src, q="dve"):
            return kb.on(q, lambda e: e.tensor_copy(dst, src))

        def RCP(dst, src):
            return kb.on("dve", lambda e: e.reciprocal(dst, src))

        def ACTV(dst, src, func, bias=0.0, scale=1.0):
            return kb.on("act", lambda e: e.activation(dst, src, func, bias=bias, scale=scale))

        def SCANOP(dst, d0, d1, init):
            return kb.on("dve", lambda e: e.tensor_tensor_scan(dst, d0, d1, init, OP.mult, OP.add))

        def MS(qn, ap, v):
            return kb.on(qn, lambda e: e.memset(ap, v))

        def FILL(ap, v):
            src_col = onescol if v == 1.0 else zc
            shp = list(ap.shape)
            bc_ap = src_col[0:shp[0], 0:1]
            for d in shp[2:]:
                bc_ap = bc_ap.unsqueeze(len(bc_ap.shape))
            return kb.on("dve", lambda e: e.tensor_copy(ap, bc_ap.broadcast_to(shp)))

        def RS(dst, src):
            return kb.on("dve", lambda e: e.reduce_sum(dst, src, axis=mybir.AxisListType.X))

        Wt = kb.wait; WS = kb.wait_sem; DR = kb.drain

        def bar(qn):
            for n in kb.q:
                if n != qn and kb.q[n].count > 0:
                    Wt(qn, n, kb.q[n].count)

        for dst, src in ((trils[:], tril_in), (sel4s[:], sel4_in), (sel4Ts[:], sel4T_in),
                         (dwWs[:], dwW), (dwBs[:], dwB), (skips[:], skipc),
                         (A_s[:], A_log), (dtb_s[:], dt_bias), (bo_r[:], bo),
                         (inc_s[:], in_c), (identf[:], ident_in), (ones1s[:], ones1_in),
                         (onescol[:], onescol_in)):
            DMA("sync", dst, src)
        for nm in ("f", "i", "c"):
            ld_c = DMA("sync", bias_s[nm][:], b_in[nm])

        Wt("act", "sync", ld_c)
        aA = ACTV(A_s[:], A_s[:], AF.Exp)
        Wt("dve", "act", aA)
        CPY(identb[:], identf[:])
        CPY(identr[:], identf[:])
        DR("dve")
        TS1(Aneg[:], A_s[:], -1.0)
        if os.environ.get("SKIPBO", "") != "1":
            Wt("pe", "sync", ld_c)
            MM(ps_a[:, 0:512], ones1s[:], bo_r[:, 0:512])
            m2 = MM(ps_b[:, 0:512], ones1s[:], bo_r[:, 512:1024])
            Wt("dve", "pe", m2)
            CPY(bo_b[:, 0:512], ps_a[:, 0:512])
            CPY(bo_b[:, 512:1024], ps_b[:, 0:512])
        SKIPX = os.environ.get("SKIPX", "") == "1"
        for Ex in (() if SKIPX else (E1, E2)):
            MS("dve", Ex["11"][:], 0.0); MS("dve", Ex["12"][:], 0.0)
            MS("dve", Ex["21"][:], 0.0); MS("dve", Ex["22"][:], 0.0)
        DR("dve")
        for Ex in (() if SKIPX else (E1, E2)):
            for gi in range(NG):
                MS("dve", Ex["11"][:, gi * EW: gi * EW + EPAD], 1.0)
                MS("dve", Ex["22"][:, gi * EW: gi * EW + EPAD], 1.0)
        if not SKIPX:
            MS("dve", Xz, 0.0)
            MS("dve", Xs_, 0.0)
            MS("dve", zc[:], 0.0)
            DR("dve")
            CPY(Xfull[0:32, :], Xz)
            CPY(Xfull[32:64, :], Xs_)
            DR("dve")

        def _finish():
            kb.q["sync"].ops.append(
                lambda e, s=kb.q["sync"]: e.wait_ge(s.sem, s.count))

            @blkctx.sync
            def _(e):
                for op in kb.q["sync"].ops:
                    op(e)

            @blkctx.tensor
            def _(e):
                for op in kb.q["pe"].ops:
                    op(e)

            @blkctx.vector
            def _(e):
                for op in kb.q["dve"].ops:
                    op(e)

            @blkctx.scalar
            def _(e):
                for op in kb.q["act"].ops:
                    op(e)

            @blkctx.gpsimd
            def _(e):
                for op in kb.q["gpd"].ops:
                    op(e)
            return nc


        LNCUT = int(os.environ.get("LNCUT", "9"))

        def ln_tile(src_hbm, tt, dst_dram):
            lx = DMA("sync", xln[:], src_hbm[tt * 128:(tt + 1) * 128, :])
            Wt("dve", "sync", lx)
            bar("dve") if False else None
            DR("dve")
            RS(stat[:, 0:1].unsqueeze(2), xln[:].unsqueeze(1))
            if LNCUT <= 1:
                return lx
            Wt("act", "sync", lx)
            Wt("act", "dve", kb.q["dve"].count)
            a1 = ACTV(sqt[:, 0:512], xln[:, 0:512], AF.Square)
            Wt("dve", "act", a1)
            DR("dve")
            RS(stat[:, 1:2].unsqueeze(2), sqt[:, 0:512].unsqueeze(1))
            d0 = kb.q["dve"].count
            Wt("act", "dve", d0)
            a2 = ACTV(sqt[:, 0:512], xln[:, 512:1024], AF.Square)
            Wt("dve", "act", a2)
            DR("dve")
            RS(stat[:, 2:3].unsqueeze(2), sqt[:, 0:512].unsqueeze(1))
            DR("dve")
            TT(stat[:, 1:2], stat[:, 1:2], stat[:, 2:3], OP.add)
            DR("dve")
            TS1(stat[:, 0:1], stat[:, 0:1], 1.0 / D)
            TS1(stat[:, 1:2], stat[:, 1:2], 1.0 / D)
            DR("dve")
            STT(stat[:, 2:3], stat[:, 0:1], stat[:, 0:1], stat[:, 1:2],
                OP.mult, OP.subtract)
            DR("dve")
            d1 = TS2(stat[:, 2:3], stat[:, 2:3], -1.0, LN_EPS, OP.mult, OP.add)
            Wt("act", "dve", d1)
            ar = ACTV(stat[:, 3:4], stat[:, 2:3], AF.Sqrt)
            Wt("dve", "act", ar)
            RCP(stat[:, 3:4], stat[:, 3:4])
            DR("dve")
            TT(stat[:, 2:3], stat[:, 0:1], stat[:, 3:4], OP.mult)
            DR("dve")
            d2 = TS1(stat[:, 2:3], stat[:, 2:3], -1.0)
            if LNCUT <= 2:
                return lx
            Wt("act", "dve", d2)
            a3 = ACTV(xln2[:], xln[:], AF.Identity, bias=stat[:, 2:3], scale=stat[:, 3:4])
            if LNCUT <= 3:
                return lx
            for dc in range(8):
                Wt("pe", "act", kb.q["act"].count)
                ptr = TRN(ps_d[:, 0:128], xln2[:, dc * 128:(dc + 1) * 128], identf[:])
                if LNCUT <= 4:
                    continue
                Wt("act", "pe", ptr)
                Wt("act", "sync", kb.q["sync"].count)
                dcc2 = ACTV(ygbf[:, 0:128], ps_d[:, 0:128], AF.Identity)
                if LNCUT <= 5:
                    continue
                Wt("sync", "act", dcc2)
                dl = DMA("sync", dst_dram[:, dc * T + tt * 128: dc * T + tt * 128 + 128],
                         ygbf[:, 0:128])
            if LNCUT <= 5:
                return lx
            return dl

        if PH <= -2:
            return _finish()
        for tt in range(16):
            ln_tile(x_ffn, tt, d_xnf)
            if PH <= -1:
                return _finish()
        for tt in range(16):
            ld_xn = ln_tile(x_ssm, tt, d_xn)

        if PH <= 0:
            return _finish()

        bcH_in = din("bcH", (32, NG * 128))
        bcHs = sb("bcHs", 32, NG * 128)
        idEn = sb("idEn", 128, NG * NBL)
        ldh = DMA("sync", bcHs[:], bcH_in)

        xnfb = BGb(0, 16384)
        WS("gpd", sems["sync"], kb.q["sync"].count)
        DMA("gpd", xnfb[:, 0:8192], d_xnf[:, 0:8192])
        ldf = DMA("gpd", xnfb[:, 8192:16384], d_xnf[:, 8192:16384])
        FFu = {"uf": 8192, "ui": 12288, "uc": 16384}
        for nm in ("uf", "ui", "uc"):
            for kc in range(8):
                ldf = DMA("gpd", BGb(FFu[nm] + kc * 512, 2048)[:, 0:1024],
                          wT[nm][kc * 128:(kc + 1) * 128, :])

        PSL = [ps_a, ps_b, ps_c, ps_d]
        xwj = 0
        pe_doneX = [0, 0]
        lw_parX = [0, 0]
        for gidx, wnm in enumerate(("wf", "wi", "wc")):
            for rs in range(4):
                for mh in range(2):
                    for kc in range(8):
                        parX = xwj % 2
                        woff = parX * 512
                        WS("gpd", sems["pe"], pe_doneX[parX])
                        lw = DMA("gpd", wrowB[:, woff:woff + 512],
                                 wT[wnm][kc * 128:(kc + 1) * 128, mh * 512:(mh + 1) * 512])
                        lw_parX[parX] = lw
                        WS("pe", sems["gpd"], lw_parX[parX])
                        if kc == 0:
                            Wt("pe", "act", kb.q["act"].count)
                        for mq in range(4):
                            mmx = MM(PSL[mq][:, 0:512],
                                     wrowB[:, woff + mq * 128: woff + (mq + 1) * 128],
                                     xnfb[:, kc * 2048 + rs * 512: kc * 2048 + (rs + 1) * 512],
                                     start=(kc == 0), stop=(kc == 7))
                        pe_doneX[parX] = kb.q["pe"].count
                        xwj += 1
                    Wt("act", "pe", mmx)
                    for mq in range(4):
                        mc = mh * 4 + mq
                        Wt("act", "sync", kb.q["sync"].count)
                        av = ACTV(xnbf[:, 0:512], PSL[mq][:, 0:512], AF.Identity)
                        Wt("sync", "act", av)
                        DMA("sync", xw_d[gidx][:, mc * T + rs * 512: mc * T + (rs + 1) * 512],
                            xnbf[:, 0:512])

        Wt("dve", "sync", ld_c)
        for mc in range(8):
            CPY(cst[:, mc * 256:(mc + 1) * 256],
                inc_s[:, mc:mc + 1].broadcast_to([128, 256]))
        DR("dve")
        for mc in range(8):
            d_cbf = CPY(cbf[:, mc * 256:(mc + 1) * 256], cst[:, mc * 256:(mc + 1) * 256])
        WS("pe", sems["gpd"], ldf)

        for step in range(8):
            csrc = cbf if step % 2 == 0 else cbf2
            cdst = cbf2 if step % 2 == 0 else cbf
            Wt("sync", "dve", kb.q["dve"].count)
            for gidx in range(3):
                lxw = DMA("sync", xwr3[:, gidx * 2048:(gidx + 1) * 2048].rearrange(
                              "p (mc t) -> p mc t", mc=8),
                          xw_d[gidx].rearrange("p (mc t) -> p mc t", mc=8)[:, :, step * 256:(step + 1) * 256])
            WS("pe", sems["sync"], lxw)
            Wt("pe", "dve", d_cbf)
            for mc in range(8):
                Wt("pe", "dve", kb.q["dve"].count)
                for gidx in range(3):
                    ps = (ps_a, ps_b, ps_c)[gidx]
                    unm = ("uf", "ui", "uc")[gidx]
                    for kc in range(8):
                        mmu = MM(ps[:, 0:256],
                                 BGb(FFu[unm] + kc * 512, 2048)[:, mc * 128:(mc + 1) * 128],
                                 csrc[:, kc * 256:(kc + 1) * 256],
                                 start=(kc == 0), stop=(kc == 7))
                Wt("dve", "pe", mmu)
                for gidx in range(3):
                    ps = (ps_a, ps_b, ps_c)[gidx]
                    TT(sqt[:, 0:256], ps[:, 0:256],
                       xwr3[:, gidx * 2048 + mc * 256: gidx * 2048 + (mc + 1) * 256], OP.add)
                    DR("dve")
                    dtx = kb.q["dve"].count
                    Wt("act", "dve", dtx)
                    fn = (AF.Sigmoid, AF.Sigmoid, AF.Tanh)[gidx]
                    ag = ACTV(gateT[:, gidx * 256:(gidx + 1) * 256], sqt[:, 0:256],
                              fn, bias=bias_s[("f", "i", "c")[gidx]][:, mc:mc + 1])
                    Wt("dve", "act", ag)
                TT(cst[:, mc * 256:(mc + 1) * 256], gateT[:, 0:256],
                   cst[:, mc * 256:(mc + 1) * 256])
                TT(gateT[:, 256:512], gateT[:, 256:512], gateT[:, 512:768])
                DR("dve")
                TT(cst[:, mc * 256:(mc + 1) * 256], cst[:, mc * 256:(mc + 1) * 256],
                   gateT[:, 256:512], OP.add)
                DR("dve")
                d_cbf = CPY(cdst[:, mc * 256:(mc + 1) * 256], cst[:, mc * 256:(mc + 1) * 256])

        cfin = cbf2 if 8 % 2 == 1 else cbf
        for rc in range(2):
            for oc in range(2):
                ps = (ps_a, ps_b)[oc]
                for kc in range(8):
                    WS("gpd", sems["pe"], kb.q["pe"].count)
                    lw = DMA("gpd", wrowB[:], wT["uo"][kc * 128:(kc + 1) * 128, :])
                    WS("pe", sems["gpd"], lw)
                    if kc == 0:
                        Wt("pe", "dve", kb.q["dve"].count)
                    mmu = MM(ps[:, 0:512],
                             cfin[:, kc * 256 + rc * 128: kc * 256 + rc * 128 + 128],
                             wrowB[:, oc * 512:(oc + 1) * 512],
                             start=(kc == 0), stop=(kc == 7))
                Wt("dve", "pe", mmu)
                Wt("dve", "sync", kb.q["sync"].count)
                duo = CPY(sqt[:, 0:512], ps[:, 0:512])
                Wt("sync", "dve", duo)
                dcc = DMA("sync", cc_in[rc * 128:(rc + 1) * 128, oc * 512:(oc + 1) * 512],
                          sqt[:, 0:512])
        WS("gpd", sems["sync"], dcc)
        kb.q["gpd"].ops.append(
            lambda e: e.collective_compute(
                "AllGather", OP.bypass, replica_groups=[list(range(8))],
                ins=[cc_in], outs=[cc_out]).then_inc(sems["cc"], 1))

        if PH <= 1:
            return _finish()

        xnb = BGb(0, 16384)
        WS("gpd", sems["pe"], kb.q["pe"].count)
        DMA("gpd", xnb[:, 0:8192], d_xn[:, 0:8192])
        ldx = DMA("gpd", xnb[:, 8192:16384], d_xn[:, 8192:16384])
        WS("pe", sems["gpd"], ldx)
        winB2 = BGb(8192, 1024)
        pe_done = [0, 0]
        lw_par = [0, 0]
        for oc in range(33):
            owid = 128 if oc < 32 else 64
            wbuf = (winB, winB2)[oc % 2]
            WS("gpd", sems["pe"], pe_done[oc % 2])
            for kc in range(8):
                lw = DMA("gpd", wbuf[:, kc * 128: kc * 128 + owid],
                         W_inT[kc * 128:(kc + 1) * 128, oc * 128: oc * 128 + owid])
            lw_par[oc % 2] = lw
            for tch in range(4):
                WS("pe", sems["gpd"], lw_par[oc % 2])
                Wt("pe", "act", kb.q["act"].count)
                Wt("pe", "dve", kb.q["dve"].count)
                for kc in range(8):
                    mmp = MM(ps_d[0:owid, 0:512], wbuf[:, kc * 128: kc * 128 + owid],
                             xnb[:, kc * 2048 + tch * 512: kc * 2048 + (tch + 1) * 512],
                             start=(kc == 0), stop=(kc == 7))
                if oc < 16:
                    Wt("act", "pe", mmp)
                    Wt("act", "sync", kb.q["sync"].count)
                    av = ACTV(sqt[:, 0:512], ps_d[:, 0:512], AF.Silu)
                    Wt("sync", "act", av)
                    DMA("sync", d_gate[:, oc * T + tch * 512: oc * T + (tch + 1) * 512],
                        sqt[:, 0:512])
                elif oc < 32:
                    Wt("act", "pe", mmp)
                    Wt("act", "sync", kb.q["sync"].count)
                    av = ACTV(sqt[:, 0:512], ps_d[:, 0:512], AF.Identity)
                    Wt("sync", "act", av)
                    DMA("sync", d_vr[:, (oc - 16) * T + tch * 512: (oc - 16) * T + (tch + 1) * 512],
                        sqt[:, 0:512])
                else:
                    Wt("act", "pe", mmp)
                    Wt("act", "sync", kb.q["sync"].count)
                    av = ACTV(sqt[0:64, 0:512], ps_d[0:64, 0:512], AF.Identity)
                    Wt("sync", "act", av)
                    DMA("sync", d_par[:, tch * 512:(tch + 1) * 512], sqt[0:64, 0:512])
            pe_done[oc % 2] = kb.q["pe"].count

        if PH <= 2:
            return _finish()

        vrb = BG(0, 2048)
        acc = BG(2048, 2048)
        vout = BGr(4096, 2048)
        WS("gpd", sems["pe"], kb.q["pe"].count)
        for ic in range(NIC):
            WS("gpd", sems["sync"], kb.q["sync"].count)
            WS("gpd", sems["dve"], kb.q["dve"].count)
            WS("gpd", sems["act"], kb.q["act"].count)
            lv = DMA("gpd", vrb, d_vr[:, ic * T:(ic + 1) * T])
            WS("dve", sems["gpd"], lv)
            TS2(acc, vrb, dwWs[:, ic * 4 + 3:ic * 4 + 4], dwBs[:, ic:ic + 1],
                OP.mult, OP.add)
            DR("dve")
            for kk in (2, 1, 0):
                sh = 3 - kk
                STT(acc[:, sh:T], vrb[:, 0:T - sh],
                    dwWs[:, ic * 4 + kk:ic * 4 + kk + 1], acc[:, sh:T],
                    OP.mult, OP.add)
                DR("dve")
            dcv = kb.q["dve"].count
            Wt("act", "dve", dcv)
            av = ACTV(vout, acc, AF.Silu)
            Wt("sync", "act", av)
            dv = DMA("sync", d_val[:, ic * T:(ic + 1) * T], vout.bitcast(F32))

        if PH <= 3:
            return _finish()

        valb = BGb(0, 32768)
        bstg = BG(16384, 512)
        WS("gpd", sems["sync"], dv)
        WS("gpd", sems["dve"], kb.q["dve"].count)
        ldv = DMA("gpd", valb[:, 0:8192], d_val[:, 0:8192])
        ldv = DMA("gpd", valb[:, 8192:16384], d_val[:, 8192:16384])
        ldv = DMA("gpd", valb[:, 16384:24576], d_val[:, 16384:24576])
        ldv = DMA("gpd", valb[:, 24576:32768], d_val[:, 24576:32768])
        WS("pe", sems["gpd"], ldv)
        wrow3 = BGb(16896, 1024)
        wrow4 = BGb(17408, 1024)
        pe_done4 = [0, 0]
        lw_par4 = [0, 0]
        for q4 in range(4):
            for gi in range(NG):
                ocol = q4 * NG + gi
                par = ocol % 2
                pair = ((wrowB, wrow2), (wrow3, wrow4))[par]
                WS("gpd", sems["pe"], pe_done4[par])
                for kc in range(16):
                    buf, slot = (pair[0], kc) if kc < 8 else (pair[1], kc - 8)
                    lw = DMA("gpd", buf[:, slot * 128:(slot + 1) * 128],
                             W_bcT[kc * 128:(kc + 1) * 128, ocol * 128:(ocol + 1) * 128])
                lw_par4[par] = lw
                for tch in range(4):
                    WS("pe", sems["gpd"], lw_par4[par])
                    Wt("pe", "dve", kb.q["dve"].count)
                    Wt("pe", "act", kb.q["act"].count)
                    for kc in range(16):
                        buf, slot = (pair[0], kc) if kc < 8 else (pair[1], kc - 8)
                        mmb = MM(ps_d[:, 0:512], buf[:, slot * 128:(slot + 1) * 128],
                                 valb[:, kc * 2048 + tch * 512: kc * 2048 + (tch + 1) * 512],
                                 start=(kc == 0), stop=(kc == 15))
                    Wt("act", "pe", mmb)
                    asq = ACTV(sqt[:, 0:512], ps_d[:, 0:512], AF.Square)
                    Wt("pe", "act", asq)
                    mss = MM(ps_G[0:4, 0:512], sel4s[:], sqt[:, 0:512])
                    Wt("act", "pe", mss)
                    asr = ACTV(fss[:, 0:512], ps_G[0:4, 0:512], AF.Sqrt)
                    Wt("dve", "act", asr)
                    TS1(fss[:, 0:512], fss[:, 0:512], EPS, OP.add)
                    DR("dve")
                    dfr = RCP(fss[:, 0:512], fss[:, 0:512])
                    Wt("pe", "dve", dfr)
                    mbb = MM(ps_G[:, 0:512], sel4Ts[:], fss[:, 0:512])
                    Wt("dve", "pe", mbb)
                    dcp = CPY(fsb[:, 0:512], ps_G[:, 0:512])
                    DR("dve")
                    Wt("dve", "sync", kb.q["sync"].count)
                    dtt = TT(bstg, ps_d[:, 0:512], fsb[:, 0:512])
                    Wt("sync", "dve", dtt)
                    dbc = DMA("sync", d_bc[q4][:, gi * T + tch * 512: gi * T + (tch + 1) * 512],
                              bstg)
                pe_done4[par] = kb.q["pe"].count

        if PH <= 4:
            return _finish()

        def cgAP(off):
            return BG(off, 2048)

        def cgrAP(off):
            return BGr(off, 2048)

        def Ab(col):
            return col.unsqueeze(2).broadcast_to([128, NG, 256])

        def cg3(off):
            return BG(off, 2048).rearrange("p (g t) -> p g t", g=NG)

        def Edat(Ex, c2, sh=0):
            ap = Ex[c2][:].rearrange("p (g s) -> p g s", g=NG)
            return ap[:, :, EPAD - sh: EPAD - sh + NBL]

        def Ebcast(Ex, c2):
            ap = Ex[c2][:].rearrange("p (g s) -> p g s", g=NG)
            return ap[:, :, EPAD:EPAD + NBL].unsqueeze(3).broadcast_to([128, NG, NBL, BL])

        idE3 = idE[:].rearrange("p (g b) -> p g b", g=NG)

        CHCUT = int(os.environ.get("CHCUT", "99"))
        for ch in range(NCH):
            t0 = ch * L
            WS("gpd", sems["dve"], kb.q["dve"].count)
            WS("gpd", sems["pe"], kb.q["pe"].count)
            for q4, off in enumerate((SC_bz, SC_bx, SC_cz, SC_cx)):
                lbc = DMA("gpd", cg3(off),
                          d_bc[q4].rearrange("p (g t) -> p g t", g=NG)[:, :, t0:t0 + L])
            if CHCUT < 1:
                continue
            Wt("sync", "act", kb.q["act"].count)
            Wt("sync", "dve", kb.q["dve"].count)
            lpar = DMA("sync", parsl[:], d_par[:, t0:t0 + L])
            WS("act", sems["sync"], lpar)
            Wt("act", "dve", kb.q["dve"].count)
            Wt("act", "pe", kb.q["pe"].count)
            ax = ACTV(kdsl[:], parsl[0:32, :], AF.Identity, bias=dtb_s[:])
            aab = ACTV(dtsl[:], parsl[0:32, :], AF.Abs, bias=dtb_s[:])
            DR("act")
            ae = ACTV(dtsl[:], dtsl[:], AF.Exp, scale=-1.0)
            DR("act")
            al = ACTV(dtsl[:], dtsl[:], AF.Ln, bias=1.0)
            Wt("dve", "act", al)
            STT(dtsl[:], kdsl[:], 0.0, dtsl[:], OP.max, OP.add)
            DR("dve")
            ddt = TS2(dtsl[:], dtsl[:], DT_MAX, DT_MIN, OP.min, OP.max)
            Wt("act", "dve", ddt)
            akd = ACTV(kdsl[:], parsl[32:64, :], AF.Sigmoid)
            Wt("dve", "act", akd)
            TS1(kdsl[:], kdsl[:], K_MAX)
            DR("dve")
            TT(dt2sl[:], dtsl[:], dtsl[:])
            TT(dk1sl[:], dtsl[:], kdsl[:])
            DR("dve")
            dgr = TS1(dk1sl[:], dk1sl[:], 1.0, OP.add)
            if CHCUT < 2:
                continue
            Wt("pe", "dve", dgr)
            for gi in range(NG):
                mm1 = MM(ps_a[:, 0:256], bcHs[:, gi * 128:(gi + 1) * 128], dtsl[:])
                mm2 = MM(ps_b[:, 0:256], bcHs[:, gi * 128:(gi + 1) * 128], dk1sl[:])
                mm3 = MM(ps_c[:, 0:256], bcHs[:, gi * 128:(gi + 1) * 128], dt2sl[:])
                Wt("dve", "pe", mm3)
                CPY(cgAP(SC_dtc)[:, gi * 256:(gi + 1) * 256], ps_a[:, 0:256])
                CPY(cgAP(SC_S)[:, gi * 256:(gi + 1) * 256], ps_b[:, 0:256])
                DR("dve")
                dbb = STT(cgAP(SC_S)[:, gi * 256:(gi + 1) * 256], ps_c[:, 0:256],
                          A_s[:, gi:gi + 1], cgAP(SC_S)[:, gi * 256:(gi + 1) * 256],
                          OP.mult, OP.add)
                Wt("pe", "dve", dbb)
            DR("dve")
            if CHCUT < 3:
                continue
            for gi in range(NG):
                SCANOP(cgAP(SC_cumI)[:, gi * 256:(gi + 1) * 256],
                       cgAP(SC_S)[:, gi * 256:(gi + 1) * 256],
                       zc[:].broadcast_to([128, 256]), 1.0)
            DR("dve")
            MS("dve", rbsl[:], 1.0)
            MS("dve", idE[:], 1.0)
            DR("dve")
            cum3 = cg3(SC_cumI)
            RCP(rbsl[:].rearrange("p (g b) -> p g b", g=NG)[:, :, 1:NBL],
                cum3[:, :, BL - 1::BL][:, :, 0:NBL - 1])
            CPY(idE3[:, :, 1:NBL], cum3[:, :, BL - 1::BL][:, :, 0:NBL - 1])
            DR("dve")
            TS1(idEn[:], idE[:], -1.0)
            TT(BG(SC_idPw, 2048).rearrange("p (g b i) -> p g b i", g=NG, i=BL),
               BG(SC_cumI, 2048).rearrange("p (g b i) -> p g b i", g=NG, i=BL),
               rbsl[:].rearrange("p (g b) -> p g b", g=NG).unsqueeze(3).broadcast_to([128, NG, NBL, BL]))
            DR("dve")
            if CHCUT < 4:
                continue
            RCP(cgAP(SC_S), cgAP(SC_S))
            DR("dve")
            TT(cgAP(SC_Sdt), cgAP(SC_S), cgAP(SC_dtc))
            DR("dve")
            WS("dve", sems["gpd"], lbc)
            TT(cgAP(SC_bz), cgAP(SC_Sdt), cgAP(SC_bz))
            DR("dve")
            TT(cgAP(SC_bx), cgAP(SC_bz), cgAP(SC_bx), OP.add)
            DR("dve")
            TT(cgAP(SC_bx), cgAP(SC_dtc), cgAP(SC_bx))
            DR("dve")
            TT(cgAP(SC_bx), cgAP(SC_bx), cgAP(SC_idPw))
            TT(cgAP(SC_bz), cgAP(SC_bz), cgAP(SC_idPw))
            DR("dve")
            TT(cg3(SC_m12), cg3(SC_Sdt), Ab(Aneg[:]))
            TT(cgAP(SC_m22), cgAP(SC_Sdt), cgAP(SC_dtc))
            DR("dve")
            TT(cg3(SC_m22), cg3(SC_m22), Ab(Aneg[:]))
            DR("dve")
            dmb = TS1(cgAP(SC_m22), cgAP(SC_m22), 1.0, OP.add)
            if CHCUT < 5:
                continue
            def st_i(off, i):
                return BG(off, 2048).rearrange("p (g b i) -> p (g b) i", g=NG, i=BL)[:, :, i]
            DR("dve")
            CPY(st_i(SC_p11, 0), st_i(SC_S, 0))
            CPY(st_i(SC_p12, 0), st_i(SC_m12, 0))
            DR("dve")
            CPY(st_i(SC_p21, 0), st_i(SC_Sdt, 0))
            CPY(st_i(SC_p22, 0), st_i(SC_m22, 0))
            DR("dve")
            for i in range(1, BL):
                TT(sqt[:, 0:128], st_i(SC_m12, i), st_i(SC_p21, i - 1))
                TT(st_i(SC_p11, i), st_i(SC_S, i), st_i(SC_p11, i - 1))
                TT(sqt[:, 128:256], st_i(SC_m12, i), st_i(SC_p22, i - 1))
                TT(st_i(SC_p12, i), st_i(SC_S, i), st_i(SC_p12, i - 1))
                DR("dve")
                TT(st_i(SC_p11, i), st_i(SC_p11, i), sqt[:, 0:128], OP.add)
                TT(st_i(SC_p12, i), st_i(SC_p12, i), sqt[:, 128:256], OP.add)
                TT(sqt[:, 256:384], st_i(SC_m22, i), st_i(SC_p21, i - 1))
                TT(st_i(SC_p21, i), st_i(SC_Sdt, i), st_i(SC_p11, i - 1))
                TT(sqt[:, 384:512], st_i(SC_m22, i), st_i(SC_p22, i - 1))
                TT(st_i(SC_p22, i), st_i(SC_Sdt, i), st_i(SC_p12, i - 1))
                DR("dve")
                TT(st_i(SC_p21, i), st_i(SC_p21, i), sqt[:, 256:384], OP.add)
                TT(st_i(SC_p22, i), st_i(SC_p22, i), sqt[:, 384:512], OP.add)
                DR("dve")
            if CHCUT < 6:
                continue
            def bend(off):
                return BG(off, 2048).rearrange("p (g b i) -> p g b i", g=NG, i=BL)[:, :, :, BL - 1]
            for c2, off in (("11", SC_p11), ("12", SC_p12), ("21", SC_p21), ("22", SC_p22)):
                CPY(Edat(E1, c2)[:, :, 1:NBL], bend(off)[:, :, 0:NBL - 1])
            DR("dve")
            MS("dve", Edat(E1, "11")[:, :, 0:1], 1.0)
            MS("dve", Edat(E1, "12")[:, :, 0:1], 0.0)
            MS("dve", Edat(E1, "21")[:, :, 0:1], 0.0)
            MS("dve", Edat(E1, "22")[:, :, 0:1], 1.0)
            DR("dve")
            src_, dst_ = E1, E2
            for sh in (1, 2, 4, 8):
                TT(Edat(dst_, "11"), Edat(src_, "11"), Edat(src_, "11", sh))
                TT(sqt[:, 0:128], Edat(src_, "12"), Edat(src_, "21", sh))
                TT(Edat(dst_, "12"), Edat(src_, "11"), Edat(src_, "12", sh))
                TT(sqt[:, 128:256], Edat(src_, "12"), Edat(src_, "22", sh))
                DR("dve")
                TT(Edat(dst_, "11"), Edat(dst_, "11"), sqt[:, 0:128], OP.add)
                TT(Edat(dst_, "12"), Edat(dst_, "12"), sqt[:, 128:256], OP.add)
                TT(Edat(dst_, "21"), Edat(src_, "21"), Edat(src_, "11", sh))
                TT(sqt[:, 256:384], Edat(src_, "22"), Edat(src_, "21", sh))
                DR("dve")
                TT(Edat(dst_, "21"), Edat(dst_, "21"), sqt[:, 256:384], OP.add)
                TT(Edat(dst_, "22"), Edat(src_, "21"), Edat(src_, "12", sh))
                TT(sqt[:, 384:512], Edat(src_, "22"), Edat(src_, "22", sh))
                DR("dve")
                dE = TT(Edat(dst_, "22"), Edat(dst_, "22"), sqt[:, 384:512], OP.add)
                DR("dve")
                src_, dst_ = dst_, src_
            Ecum = src_
            def Eslot(Ex, c2, b):
                return Ex[c2][:].rearrange("p (g s) -> p g s", g=NG)[:, :, EPAD + b]
            TT(PLt["11"][:], bend(SC_p11)[:, :, NBL - 1], Eslot(Ecum, "11", NBL - 1))
            TT(sqt[:, 0:8], bend(SC_p12)[:, :, NBL - 1], Eslot(Ecum, "21", NBL - 1))
            TT(PLt["12"][:], bend(SC_p11)[:, :, NBL - 1], Eslot(Ecum, "12", NBL - 1))
            TT(sqt[:, 8:16], bend(SC_p12)[:, :, NBL - 1], Eslot(Ecum, "22", NBL - 1))
            TT(PLt["21"][:], bend(SC_p21)[:, :, NBL - 1], Eslot(Ecum, "11", NBL - 1))
            TT(sqt[:, 16:24], bend(SC_p22)[:, :, NBL - 1], Eslot(Ecum, "21", NBL - 1))
            TT(PLt["22"][:], bend(SC_p21)[:, :, NBL - 1], Eslot(Ecum, "12", NBL - 1))
            TT(sqt[:, 24:32], bend(SC_p22)[:, :, NBL - 1], Eslot(Ecum, "22", NBL - 1))
            DR("dve")
            TT(PLt["11"][:], PLt["11"][:], sqt[:, 0:8], OP.add)
            TT(PLt["12"][:], PLt["12"][:], sqt[:, 8:16], OP.add)
            TT(PLt["21"][:], PLt["21"][:], sqt[:, 16:24], OP.add)
            TT(PLt["22"][:], PLt["22"][:], sqt[:, 24:32], OP.add)
            DR("dve")
            for ci, c2 in enumerate(("11", "12", "21", "22")):
                for hs in range(4):
                    CPY(PL4[:, [[1, 1], [4, NG]]].copy() if False else
                        bass.AP(PL4.tensor if False else PL4[:].tensor, 0, [[0, 0]]) if False else
                        PL4[0:32, ci * H + hs: ci * H + hs + 29:4] if False else
                        PL4[0:32, :].rearrange("p (c h) -> p c h", c=4)[:, ci, hs::4],
                        PLt[c2][hs * 32:(hs + 1) * 32, :])
            DR("dve")
            TT(idE3 if False else BE["11"][:].rearrange("p (g b) -> p g b", g=NG),
               Edat(Ecum, "11"), idE3)
            TT(BE["12"][:].rearrange("p (g b) -> p g b", g=NG), Edat(Ecum, "12"),
               idEn[:].rearrange("p (g b) -> p g b", g=NG))
            TT(BE["21"][:].rearrange("p (g b) -> p g b", g=NG), Edat(Ecum, "21"),
               idEn[:].rearrange("p (g b) -> p g b", g=NG))
            dbe = TT(BE["22"][:].rearrange("p (g b) -> p g b", g=NG), Edat(Ecum, "22"), idE3)
            DR("dve")
            if CHCUT < 7:
                continue
            TT(cgAP(SC_qt1), cgAP(SC_p11), cgAP(SC_cz))
            TT(sqt[:, 0:0] if False else cgAP(SC_tq1), cgAP(SC_p21), cgAP(SC_cx))
            DR("dve")
            TT(cgAP(SC_qt1), cgAP(SC_qt1), cgAP(SC_tq1), OP.add)
            TT(cgAP(SC_qt2), cgAP(SC_p12), cgAP(SC_cz))
            TT(cgAP(SC_tq2), cgAP(SC_p22), cgAP(SC_cx))
            DR("dve")
            dqt = TT(cgAP(SC_qt2), cgAP(SC_qt2), cgAP(SC_tq2), OP.add)
            DR("dve")
            TT(cgAP(SC_cz), cgAP(SC_p22), cgAP(SC_bz))
            TT(cgAP(SC_tq1), cgAP(SC_p12), cgAP(SC_bx))
            TT(cgAP(SC_cx), cgAP(SC_p11), cgAP(SC_bx))
            TT(cgAP(SC_tq2), cgAP(SC_p21), cgAP(SC_bz))
            DR("dve")
            TT(cgAP(SC_cz), cgAP(SC_cz), cgAP(SC_tq1), OP.subtract)
            dkt = TT(cgAP(SC_cx), cgAP(SC_cx), cgAP(SC_tq2), OP.subtract)
            DR("dve")
            if CHCUT < 8:
                continue
            WS("gpd", sems["dve"], kb.q["dve"].count)
            for ic in range(NIC):
                WS("gpd", sems["pe"], kb.q["pe"].count)
                lvv = DMA("gpd", vring[:],
                          d_val[:, ic * T + t0: ic * T + t0 + L])
                for tb in range(2):
                    WS("pe", sems["gpd"], lvv)
                    Wt("pe", "dve", kb.q["dve"].count)
                    ptr = TRN(ps_d[:, 0:128], vring[:, tb * 128:(tb + 1) * 128],
                              identf[:])
                    Wt("dve", "pe", ptr)
                    dU = CPY(Ust[:, (ic * 2 + tb) * 128:(ic * 2 + tb) * 128 + 128],
                             ps_d[:, 0:128])
                    DR("dve")
                if ic < NIC - 1:
                    WS("gpd", sems["dve"], dU)
            if CHCUT < 9:
                continue
            for hh in range(2):
                Wt("dve", "pe", kb.q["pe"].count)
                TT(cg3(SC_tq1).rearrange("p g t -> p (g t)").rearrange("p (g b i) -> p g b i", g=NG, i=BL) if False else
                   BG(SC_tq1, 2048).rearrange("p (g b i) -> p g b i", g=NG, i=BL),
                   BG(SC_qt1, 2048).rearrange("p (g b i) -> p g b i", g=NG, i=BL),
                   Ebcast(Ecum, "11"))
                TT(BG(SC_tq2, 2048).rearrange("p (g b i) -> p g b i", g=NG, i=BL),
                   BG(SC_qt2, 2048).rearrange("p (g b i) -> p g b i", g=NG, i=BL),
                   Ebcast(Ecum, "21"))
                DR("dve")
                for hs in range(4):
                    TT(Qst[0:32, :].rearrange("p (hl t) -> p hl t", t=L)[:, hs::4, :],
                       BG(SC_tq1, 2048)[hs * 32:(hs + 1) * 32, :].rearrange(
                           "p (g t) -> p g t", t=L)[:, 4 * hh:4 * hh + 4, :],
                       BG(SC_tq2, 2048)[hs * 32:(hs + 1) * 32, :].rearrange(
                           "p (g t) -> p g t", t=L)[:, 4 * hh:4 * hh + 4, :], OP.add)
                DR("dve")
                TT(BG(SC_tq1, 2048).rearrange("p (g b i) -> p g b i", g=NG, i=BL),
                   BG(SC_qt1, 2048).rearrange("p (g b i) -> p g b i", g=NG, i=BL),
                   Ebcast(Ecum, "12"))
                TT(BG(SC_tq2, 2048).rearrange("p (g b i) -> p g b i", g=NG, i=BL),
                   BG(SC_qt2, 2048).rearrange("p (g b i) -> p g b i", g=NG, i=BL),
                   Ebcast(Ecum, "22"))
                DR("dve")
                for hs in range(4):
                    TT(Qst[32:64, :].rearrange("p (hl t) -> p hl t", t=L)[:, hs::4, :],
                       BG(SC_tq1, 2048)[hs * 32:(hs + 1) * 32, :].rearrange(
                           "p (g t) -> p g t", t=L)[:, 4 * hh:4 * hh + 4, :],
                       BG(SC_tq2, 2048)[hs * 32:(hs + 1) * 32, :].rearrange(
                           "p (g t) -> p g t", t=L)[:, 4 * hh:4 * hh + 4, :], OP.add)
                DR("dve")
                def BEb(c2):
                    return BE[c2][:].rearrange("p (g b) -> p g b", g=NG).unsqueeze(3).broadcast_to([128, NG, NBL, BL])
                TT(BG(SC_tq1, 2048).rearrange("p (g b i) -> p g b i", g=NG, i=BL),
                   BG(SC_cz, 2048).rearrange("p (g b i) -> p g b i", g=NG, i=BL), BEb("22"))
                TT(BG(SC_tq2, 2048).rearrange("p (g b i) -> p g b i", g=NG, i=BL),
                   BG(SC_cx, 2048).rearrange("p (g b i) -> p g b i", g=NG, i=BL), BEb("12"))
                DR("dve")
                for hs in range(4):
                    TT(Kst[0:32, :].rearrange("p (hl t) -> p hl t", t=L)[:, hs::4, :],
                       BG(SC_tq1, 2048)[hs * 32:(hs + 1) * 32, :].rearrange(
                           "p (g t) -> p g t", t=L)[:, 4 * hh:4 * hh + 4, :],
                       BG(SC_tq2, 2048)[hs * 32:(hs + 1) * 32, :].rearrange(
                           "p (g t) -> p g t", t=L)[:, 4 * hh:4 * hh + 4, :], OP.add)
                DR("dve")
                TT(BG(SC_tq1, 2048).rearrange("p (g b i) -> p g b i", g=NG, i=BL),
                   BG(SC_cz, 2048).rearrange("p (g b i) -> p g b i", g=NG, i=BL), BEb("21"))
                TT(BG(SC_tq2, 2048).rearrange("p (g b i) -> p g b i", g=NG, i=BL),
                   BG(SC_cx, 2048).rearrange("p (g b i) -> p g b i", g=NG, i=BL), BEb("11"))
                DR("dve")
                for hs in range(4):
                    dasm = TT(Kst[32:64, :].rearrange("p (hl t) -> p hl t", t=L)[:, hs::4, :],
                              BG(SC_tq1, 2048)[hs * 32:(hs + 1) * 32, :].rearrange(
                                  "p (g t) -> p g t", t=L)[:, 4 * hh:4 * hh + 4, :],
                              BG(SC_tq2, 2048)[hs * 32:(hs + 1) * 32, :].rearrange(
                                  "p (g t) -> p g t", t=L)[:, 4 * hh:4 * hh + 4, :], OP.add)
                DR("dve")
                HH = int(os.environ.get("HHCUT", "99"))
                for hl in range(16):
                    hgl = hh * 16 + hl
                    ic = hgl // 2
                    if HH < 2:
                        continue
                    Wt("pe", "dve", dasm)
                    Wt("pe", "act", kb.q["act"].count)
                    mmy = MM(ps_y[0:64, 0:L], Xfull[0:64, hgl * 64:(hgl + 1) * 64],
                             Qst[0:64, hl * L:(hl + 1) * L], start=True, stop=False)
                    if HH < 3:
                        continue
                    for j in (1, 0):
                        Wt("pe", "dve", kb.q["dve"].count)
                        Wt("pe", "act", kb.q["act"].count)
                        mmg = MM(ps_G[:, 0:L], Kst[0:64, hl * L + j * 128: hl * L + j * 128 + 128],
                                 Qst[0:64, hl * L:(hl + 1) * L], start=True, stop=True)
                        if HH < 4:
                            continue
                        Wt("act", "pe", mmg)
                        Wt("act", "gpd", kb.q["gpd"].count)
                        if j == 0:
                            aG = ACTV(Gsb[:, 0:256], ps_G[:, 0:256], AF.Identity)
                        else:
                            aG = ACTV(Gsb[:, 128:256], ps_G[:, 128:256], AF.Identity)
                        WS("gpd", sems["act"], aG)
                        dG = kb.on("gpd", lambda e, j=j: e.affine_select(
                            Gsb[:, j * 128:(j + 1) * 128], Gsb[:, j * 128:(j + 1) * 128],
                            [[1, 128]], OP.is_ge, 0.0, base=0, channel_multiplier=-1))
                        if HH < 5:
                            continue
                        WS("pe", sems["gpd"], dG)
                        mmy = MM(ps_y[0:64, j * 128:L],
                                 Ust[:, (ic * 2 + j) * 128 + (hgl % 2) * 64:
                                                 (ic * 2 + j) * 128 + (hgl % 2) * 64 + 64],
                                 Gsb[:, j * 128:L], start=False, stop=(j == 0))
                    if HH < 6:
                        continue
                    Wt("act", "pe", mmy)
                    Wt("act", "dve", kb.q["dve"].count)
                    ay = ACTV(ysb[(hgl % 2) * 64:(hgl % 2) * 64 + 64, 0:L], ps_y[0:64, 0:L],
                              AF.Identity)
                    if hgl % 2 == 1:
                        WS("gpd", sems["dve"], kb.q["dve"].count)
                        lvv = DMA("gpd", vring[:], d_val[:, ic * T + t0: ic * T + t0 + L])
                        lgg = DMA("gpd", gring[:], d_gate[:, ic * T + t0: ic * T + t0 + L])
                        Wt("dve", "act", ay)
                        WS("dve", sems["gpd"], lgg)
                        STT(sqt[:, 0:L], vring[:], skips[:, ic:ic + 1],
                            ysb[:, 0:L], OP.mult, OP.add)
                        DR("dve")
                        Wt("dve", "sync", kb.q["sync"].count)
                        dyg = TT(ygbf[:], sqt[:, 0:L], gring[:])
                        Wt("sync", "dve", dyg)
                        DMA("sync", d_yg[:, ic * T + t0: ic * T + t0 + L], ygbf[:])
                for hl in (range(16) if HH >= 7 else ()):
                    hgl = hh * 16 + hl
                    for j in range(2):
                        Wt("pe", "dve", kb.q["dve"].count)
                        ptr = TRN(ps_d[:, 0:64],
                                  Kst[0:64, hl * L + j * 128: hl * L + j * 128 + 128],
                                  identf[0:64, 0:64])
                        Wt("dve", "pe", ptr)
                        dKT = CPY(KTt[:], ps_d[:, 0:64])
                        Wt("pe", "dve", dKT)
                        MM(ps_k0[0:32, (hgl % 8) * 64:(hgl % 8) * 64 + 64],
                           KTt[:, 0:32],
                           Ust[:, (ic * 2 + j) * 128 + (hgl % 2) * 64:
                                           (ic * 2 + j) * 128 + (hgl % 2) * 64 + 64]
                           if False else
                           Ust[:, ((hgl // 2) * 2 + j) * 128 + (hgl % 2) * 64:
                                           ((hgl // 2) * 2 + j) * 128 + (hgl % 2) * 64 + 64],
                           start=(j == 0), stop=(j == 1))
                        mku = MM(ps_k1[0:32, (hgl % 8) * 64:(hgl % 8) * 64 + 64],
                                 KTt[:, 32:64],
                                 Ust[:, ((hgl // 2) * 2 + j) * 128 + (hgl % 2) * 64:
                                                 ((hgl // 2) * 2 + j) * 128 + (hgl % 2) * 64 + 64],
                                 start=(j == 0), stop=(j == 1))
                    if hgl % 8 == 7:
                        grp = hgl // 8
                        Wt("dve", "pe", mku)
                        TT(ktuz[:, grp * 512:(grp + 1) * 512], ps_k0[0:32, 0:512],
                           Xz[:, grp * 512:(grp + 1) * 512], OP.add)
                        dku = TT(ktus[:, grp * 512:(grp + 1) * 512], ps_k1[0:32, 0:512],
                                 Xs_[:, grp * 512:(grp + 1) * 512], OP.add)
                        DR("dve")
                        Wt("pe", "dve", dku)
            if CHCUT < 10:
                continue
            def PLb(ci):
                return PL4[0:32, :].rearrange("p (c h) -> p c h", c=4)[:, ci, :].unsqueeze(2).broadcast_to([32, H, DH])
            def hd3(ap):
                return ap.rearrange("p (h d) -> p h d", d=DH)
            TT(hd3(Xz), hd3(ktuz), PLb(0))
            TT(hd3(tmpX), hd3(ktus), PLb(1))
            DR("dve")
            TT(Xz, Xz, tmpX, OP.add)
            TT(hd3(Xs_), hd3(ktuz), PLb(2))
            DR("dve")
            TT(hd3(tmpX), hd3(ktus), PLb(3))
            DR("dve")
            TT(Xs_, Xs_, tmpX, OP.add)
            DR("dve")
            CPY(Xfull[0:32, :], Xz)
            dXf = CPY(Xfull[32:64, :], Xs_)
            DR("dve")

        if PH <= 5:
            return _finish()

        wob = BGb(0, 16384)
        woTb = BGb(8192, 8192)
        ygl = BGb(12288, 4096)
        xnl = BGb(14336, 2048)
        xt_t = BG(16384, 1024)
        uo_t = BG(17408, 1024)
        ht_t = BG(18432, 1024)
        g_t = BG(19456, 1024)
        o_t = BG(20480, 1024)
        WS("gpd", sems["pe"], kb.q["pe"].count)
        WS("gpd", sems["dve"], kb.q["dve"].count)
        for kc in range(16):
            lwo = DMA("gpd", wob[:, kc * 1024:(kc + 1) * 1024],
                      W_outT[kc * 128:(kc + 1) * 128, :])
        for kc in range(8):
            lwo = DMA("gpd", woTb[:, kc * 1024:(kc + 1) * 1024],
                      wT["wo"][kc * 128:(kc + 1) * 128, :])
        WS("dve", sems["cc"], 1)
        WS("sync", sems["cc"], 1)
        DR("dve")
        for tt in range(16):
            WS("gpd", sems["pe"], kb.q["pe"].count)
            for kc in range(16):
                lyg = DMA("gpd", ygl[:, kc * 128:(kc + 1) * 128],
                          d_yg[:, kc * T + tt * 128: kc * T + (tt + 1) * 128])
            for kc in range(8):
                lxn = DMA("gpd", xnl[:, kc * 128:(kc + 1) * 128],
                          d_xn[:, kc * T + tt * 128: kc * T + (tt + 1) * 128])
            Wt("sync", "dve", kb.q["dve"].count)
            lx = DMA("sync", xt_t, x_ssm[tt * 128:(tt + 1) * 128, :])
            lu = DMA("sync", uo_t, cc_out[tt * 128:(tt + 1) * 128, :])
            WS("pe", sems["gpd"], lyg if False else kb.q["gpd"].count)
            Wt("pe", "dve", kb.q["dve"].count)
            for oc in range(2):
                ps = (ps_a, ps_b)[oc]
                for kc in range(16):
                    mms = MM(ps[:, 0:512], ygl[:, kc * 128:(kc + 1) * 128],
                             wob[:, kc * 1024 + oc * 512: kc * 1024 + (oc + 1) * 512],
                             start=(kc == 0), stop=(kc == 15))
            for oc in range(2):
                ps = (ps_c, ps_d)[oc]
                for kc in range(8):
                    mmh = MM(ps[:, 0:512], xnl[:, kc * 128:(kc + 1) * 128],
                             woTb[:, kc * 1024 + oc * 512: kc * 1024 + (oc + 1) * 512],
                             start=(kc == 0), stop=(kc == 7))
            Wt("dve", "pe", mmh)
            WS("dve", sems["sync"], lu)
            TT(ht_t, uo_t, bo_b[:], OP.add)
            DR("dve")
            TT(ht_t[:, 0:512], ht_t[:, 0:512], ps_c[:, 0:512], OP.add)
            TT(ht_t[:, 512:1024], ht_t[:, 512:1024], ps_d[:, 0:512], OP.add)
            DR("dve")
            dht = kb.q["dve"].count
            Wt("act", "dve", dht)
            ah = ACTV(g_t, ht_t, AF.Tanh)
            DR("act")
            ah = ACTV(g_t, g_t, AF.Silu)
            Wt("dve", "act", ah)
            TT(ht_t[:, 0:512], ps_a[:, 0:512], g_t[:, 0:512])
            TT(ht_t[:, 512:1024], ps_b[:, 0:512], g_t[:, 512:1024])
            DR("dve")
            Wt("dve", "sync", kb.q["sync"].count)
            dout_ = TT(o_t, xt_t, ht_t, OP.add)
            Wt("sync", "dve", dout_)
            dfin = DMA("sync", out[tt * 128:(tt + 1) * 128, :], o_t)
        return _finish()

    return nc


_NC_CACHE = [None]


def kernel(**inputs):
    if _NC_CACHE[0] is None:
        _NC_CACHE[0] = build_program()
    nc = _NC_CACHE[0]
    in_maps = []
    for core in range(B):
        g = _host_arrays(inputs, core)
        bcH = np.zeros((32, NG * 128), dtype=np.float32)
        for gi in range(NG):
            for hs in range(4):
                bcH[4 * gi + hs, gi * 128 + hs * 32: gi * 128 + (hs + 1) * 32] = 1.0
        g["bcH"] = bcH
        in_maps.append(g)
    res = run_bass_kernel_spmd(nc, in_maps, core_ids=list(range(B)))
    return np.stack([res.results[c]["out"] for c in range(B)], axis=0)
